# version 10
# baseline (speedup 1.0000x reference)
from contextlib import ExitStack

import numpy as np
import ml_dtypes
import concourse.bass as bass
import concourse.mybir as mybir
from concourse.bass_utils import run_bass_kernel_spmd
from concourse.alu_op_type import AluOpType

F32 = mybir.dt.float32
BF16 = mybir.dt.bfloat16
AF = mybir.ActivationFunctionType
NP_BF16 = ml_dtypes.bfloat16

B, T, D = 4, 4096, 1024
NCORE = 8
NH, HD = 16, 64
TC = 2048
NEG = -30000.0
MAGIC = 12582912.0
EPS = 1e-5


def host_rope_R():
    theta = 1.0 / 10000.0 ** (np.arange(0, 32, 2, dtype=np.float64) / 32.0)
    out = np.zeros((8, 128, 128), np.float32)
    for h in range(NH):
        c, off = divmod(h * 64, 128)
        cs = np.cos(h * theta).astype(np.float32)
        sn = np.sin(h * theta).astype(np.float32)
        blk = np.zeros((64, 64), np.float32)
        for half in (0, 32):
            for j in range(16):
                e, o = half + 2 * j, half + 2 * j + 1
                blk[e, half + j] = cs[j]
                blk[o, half + j] = -sn[j]
                blk[e, half + 16 + j] = sn[j]
                blk[o, half + 16 + j] = cs[j]
        out[c, off:off + 64, off:off + 64] = blk
    return out.astype(NP_BF16)


def host_consts():
    tri = np.zeros((128, 512), np.float32)
    tri[:, :128] = np.where(np.arange(128)[:, None] > np.arange(128)[None, :],
                            NEG, 0.0)
    hm = np.zeros((5, 128, 8), np.float32)
    for t in range(4):
        for sub in range(2):
            hm[t, sub * 64:(sub + 1) * 64, 2 * t + sub] = 1.0
    hm[4, :, :] = 0.5
    return {
        "ropeR": host_rope_R(),
        "tri": tri.astype(NP_BF16),
        "i128": np.eye(128, dtype=NP_BF16),
        "hm": hm.astype(NP_BF16),
        "i8": np.eye(8, dtype=np.float32),
        "ones8": np.ones((8, 128), np.float32),
        "onesf": np.ones((1, 64), dtype=NP_BF16),
        "oc1f": np.ones((1, 128), np.float32),
        "c127": np.full((1, 128), 127.0, np.float32),
        "ocf": np.ones((128, 1), np.float32),
        "ocb": np.ones((128, 1), dtype=NP_BF16),
        "or512": np.ones((1, 512), dtype=NP_BF16),
        "oc64": np.ones((1, 64), dtype=NP_BF16),
    }



class Cnt:
    def __init__(self):
        self.v = {}

    def inc(self, name, amt=1):
        self.v[name] = self.v.get(name, 0) + amt
        return self.v[name]

    def get(self, name, default=0):
        return self.v.get(name, default)


def _emit(engine, prog, sems):
    for item in prog:
        if item[0] == "wait":
            engine.wait_ge(sems[item[1]], item[2])
        else:
            inst = item[1](engine)
            if item[2] is not None:
                inst.then_inc(sems[item[2][0]], item[2][1])


class Prog:

    def __init__(self, nc):
        self.nc = nc
        self.c = Cnt()
        self.progs = {"t": [], "s": [], "v": []}

    def wait(self, eng, sem, val):
        if val > 0:
            self.progs[eng].append(("wait", sem, val))

    def run(self, eng, fn):
        if eng in ("s", "v") and self.c.get(eng) > 0:
            self.progs[eng].append(("wait", eng, self.c.get(eng)))
        self.c.inc(eng, 1)
        self.progs[eng].append(("op", fn, (eng, 1)))
        return self.c.get(eng)

    def barrier(self):
        t, s, v = self.c.get("t"), self.c.get("s"), self.c.get("v")
        self.wait("t", "s", s); self.wait("t", "v", v)
        self.wait("s", "t", t); self.wait("s", "v", v)
        self.wait("v", "t", t); self.wait("v", "s", s)
        return t, s, v

    def emit(self, block, sems):
        block.tensor(lambda e: _emit(e, self.progs["t"], sems))
        block.scalar(lambda e: _emit(e, self.progs["s"], sems))
        block.vector(lambda e: _emit(e, self.progs["v"], sems))


class Ctx:
    pass



def emit_ln_tail(pr, rbuf, scr, rows, bcm, bcs, xout_ap, psSum, psSq, psBc,
                 onescolb, ones1f, epsb, gcol=None, bcol=None, nchunk=8, ntc=4,
                 pre_wait=None, chunk_prewait=None, post_add_ap=None):
    c = pr.c
    CW = 512
    if pre_wait is not None:
        pr.wait("v", pre_wait[0], pre_wait[1])
    mm_done = {}
    for tc_i in range(ntc):
        for ch in range(nchunk):
            slot = ch % 2
            if (tc_i, ch) >= (0, 2) and mm_done:
                key = (tc_i, ch - 2) if ch >= 2 else (tc_i - 1, nchunk - 2 + ch)
                if key in mm_done:
                    pr.wait("v", "t", mm_done[key])
            pr.run("v", (lambda ch=ch, tc_i=tc_i, slot=slot: lambda v: v.tensor_copy(
                out=scr[:, (2 * slot) * CW:(2 * slot + 1) * CW],
                in_=rbuf[ch][:, tc_i * CW:(tc_i + 1) * CW]))())
            vd = pr.run("v", (lambda ch=ch, tc_i=tc_i, slot=slot: lambda v: v.tensor_tensor(
                out=scr[:, (2 * slot + 1) * CW:(2 * slot + 2) * CW],
                in0=rbuf[ch][:, tc_i * CW:(tc_i + 1) * CW],
                in1=rbuf[ch][:, tc_i * CW:(tc_i + 1) * CW], op=AluOpType.mult))())
            pr.wait("t", "v", vd)
            def mk_mm(ch=ch, slot=slot, first=(ch == 0), last=(ch == nchunk - 1)):
                def f(tensor):
                    tensor.matmul(psSum[0:1, :], onescolb[:, :],
                                  scr[:, (2 * slot) * CW:(2 * slot + 1) * CW],
                                  start=first, stop=last)
                    return tensor.matmul(psSq[0:1, :], onescolb[:, :],
                                         scr[:, (2 * slot + 1) * CW:(2 * slot + 2) * CW],
                                         start=first, stop=last)
                return f
            mm_done[(tc_i, ch)] = pr.run("t", mk_mm())
        pr.wait("v", "t", mm_done[(tc_i, nchunk - 1)])
        pr.run("v", (lambda tc_i=tc_i: lambda v: v.tensor_copy(
            out=rows[0][:, tc_i * CW:(tc_i + 1) * CW], in_=psSum[0:1, :]))())
        V_EV = pr.run("v", (lambda tc_i=tc_i: lambda v: v.tensor_copy(
            out=rows[1][:, tc_i * CW:(tc_i + 1) * CW], in_=psSq[0:1, :]))())
        if tc_i < ntc - 1:
            pr.wait("t", "v", V_EV)
    n = float(nchunk * 128)
    pr.run("v", lambda v: v.tensor_scalar(out=rows[0][:, :], in0=rows[0][:, :],
                                          scalar1=1.0 / n, scalar2=None,
                                          op0=AluOpType.mult))
    pr.run("v", lambda v: v.tensor_scalar(out=rows[1][:, :], in0=rows[1][:, :],
                                          scalar1=1.0 / n, scalar2=None,
                                          op0=AluOpType.mult))
    pr.run("v", lambda v: v.tensor_tensor(out=rows[2][:, :], in0=rows[0][:, :],
                                          in1=rows[0][:, :], op=AluOpType.mult))
    pr.run("v", lambda v: v.tensor_tensor(out=rows[1][:, :], in0=rows[1][:, :],
                                          in1=rows[2][:, :], op=AluOpType.subtract))
    pr.run("v", lambda v: v.memset(epsb[:, :], EPS))
    V_VAR = c.get("v")
    pr.wait("s", "v", V_VAR)
    S_SQ = pr.run("s", lambda s: s.activation(rows[2][:, :], rows[1][:, :],
                                              AF.Sqrt, bias=epsb[:, :], scale=1.0))
    pr.wait("t", "s", S_SQ)
    for tc_i in range(ntc):
        td = pr.run("t", (lambda tc_i=tc_i: lambda t: t.matmul(
            psBc[0:128, :], ones1f[:, :],
            rows[2][:, tc_i * CW:(tc_i + 1) * CW],
            start=True, stop=True))())
        pr.wait("v", "t", td)
        pr.run("v", (lambda tc_i=tc_i: lambda v: v.reciprocal(
            out=bcs[:, tc_i * CW:(tc_i + 1) * CW], in_=psBc[0:128, :]))())
        pr.wait("t", "v", c.get("v"))
        td = pr.run("t", (lambda tc_i=tc_i: lambda t: t.matmul(
            psBc[0:128, :], ones1f[:, :],
            rows[0][:, tc_i * CW:(tc_i + 1) * CW],
            start=True, stop=True))())
        pr.wait("v", "t", td)
        pr.run("v", (lambda tc_i=tc_i: lambda v: v.tensor_copy(
            out=bcm[:, tc_i * CW:(tc_i + 1) * CW], in_=psBc[0:128, :]))())
        pr.run("v", (lambda tc_i=tc_i: lambda v: v.tensor_tensor(
            out=bcm[:, tc_i * CW:(tc_i + 1) * CW],
            in0=bcm[:, tc_i * CW:(tc_i + 1) * CW],
            in1=bcs[:, tc_i * CW:(tc_i + 1) * CW], op=AluOpType.mult))())
        if tc_i < ntc - 1:
            pr.wait("t", "v", c.get("v"))
    ready = []
    for ch in range(nchunk):
        if chunk_prewait is not None:
            pw = chunk_prewait(ch)
            if pw is not None:
                pr.wait("v", pw[0], pw[1])
        xo = xout_ap(ch)
        pr.run("v", (lambda ch=ch, xo=xo: lambda v: v.tensor_tensor(
            out=xo, in0=rbuf[ch][:, :], in1=bcs[:, :], op=AluOpType.mult))())
        pr.run("v", (lambda ch=ch, xo=xo: lambda v: v.tensor_tensor(
            out=xo, in0=xo, in1=bcm[:, :], op=AluOpType.subtract))())
        if gcol is not None:
            pr.run("v", (lambda ch=ch, xo=xo: lambda v: v.tensor_scalar(
                out=xo, in0=xo, scalar1=gcol[:, ch:ch + 1], scalar2=bcol[:, ch:ch + 1],
                op0=AluOpType.mult, op1=AluOpType.add))())
        if post_add_ap is not None:
            pa = post_add_ap(ch)
            pr.run("v", (lambda xo=xo, pa=pa: lambda v: v.tensor_tensor(
                out=pa, in0=xo, in1=pa, op=AluOpType.add))())
        ready.append(c.get("v"))
    return ready



def emit_absmax_row(pr, xch, nch, sc32, scb, psT, psTR, mrow, i128, sqtmp=None):
    c = pr.c
    if nch == 1:
        pr.run("v", lambda v: v.tensor_copy(out=scb[:, 0:TC], in_=xch(0)))
    else:
        pr.run("v", lambda v: v.tensor_tensor(out=sc32[:, :], in0=xch(0), in1=xch(0),
                                              op=AluOpType.mult))
        for ch in range(1, nch):
            pr.run("v", (lambda ch=ch: lambda v: v.tensor_tensor(
                out=sqtmp[:, :], in0=xch(ch), in1=xch(ch), op=AluOpType.mult))())
            pr.run("v", lambda v: v.tensor_tensor(out=sc32[:, :], in0=sc32[:, :],
                                                  in1=sqtmp[:, :], op=AluOpType.max))
        pr.run("v", lambda v: v.tensor_copy(out=scb[:, 0:TC], in_=sc32[:, :]))
    V_AM = c.get("v")
    pr.wait("t", "v", V_AM)
    for tt in range(16):
        td = pr.run("t", (lambda tt=tt: lambda t: t.transpose(
            psT[0:128, 0:128], scb[:, tt * 128:(tt + 1) * 128], i128[:, :]))())
        pr.wait("v", "t", td)
        pr.run("v", (lambda tt=tt: lambda v: v.tensor_reduce(
            out=sc32[:, tt:tt + 1], in_=psT[0:128, 0:128],
            axis=mybir.AxisListType.X, op=AluOpType.max))())
        vd = pr.run("v", (lambda tt=tt: lambda v: v.tensor_copy(
            out=scb[:, TC + tt:TC + tt + 1], in_=sc32[:, tt:tt + 1]))())
        pr.wait("t", "v", vd)
        td2 = pr.run("t", (lambda tt=tt: lambda t: t.transpose(
            psTR[0:1, 0:128], scb[:, TC + tt:TC + tt + 1], i128[:, :]))())
        pr.wait("v", "t", td2)
        pr.run("v", (lambda tt=tt: lambda v: v.tensor_copy(
            out=mrow[:, tt * 128:(tt + 1) * 128], in_=psTR[0:1, 0:128]))())
    pr.run("v", lambda v: v.tensor_scalar(out=mrow[:, :], in0=mrow[:, :],
                                          scalar1=1e-30, scalar2=None,
                                          op0=AluOpType.max))
    V_PRE = c.get("v")
    pr.wait("s", "v", V_PRE)
    S_RT = pr.run("s", lambda s: s.activation(mrow[:, :], mrow[:, :], AF.Sqrt,
                                              scale=1.0))
    pr.wait("v", "s", S_RT)
    return c.get("v")


def emit_row_bcast(pr, row_sl, dst, lhs, psX, ntc=4):
    for tc_i in range(ntc):
        td = pr.run("t", (lambda tc_i=tc_i: lambda t: t.matmul(
            psX[0:128, :], lhs[:, :], row_sl(tc_i), start=True, stop=True))())
        pr.wait("v", "t", td)
        pr.run("v", (lambda tc_i=tc_i: lambda v: v.tensor_copy(
            out=dst[:, tc_i * 512:(tc_i + 1) * 512], in_=psX[0:128, :]))())
        pr.wait("t", "v", pr.c.get("v"))


def emit_wquant_pass1(pr, c0, wf, wsum, wcol, msc, wsc128, ocf, oc1f, psX,
                      group_sz, ng, n_el, wbase):
    pr.run("v", lambda v: v.memset(wsum[:, :], 0.0))
    absum_done = []
    for gq in range(ng):
        half = gq % 2
        pr.wait("v", "w%d" % half, wbase[half] + (gq // 2 + 1) * group_sz * 16)
        for cc in range(group_sz):
            pr.run("v", (lambda half=half, cc=cc: lambda v: v.tensor_reduce(
                out=wcol[:, :], in_=wf[:, (half * group_sz + cc) * 1024:
                                       (half * group_sz + cc + 1) * 1024],
                axis=mybir.AxisListType.X, op=AluOpType.add,
                apply_absolute_value=True))())
            pr.run("v", lambda v: v.tensor_tensor(out=wsum[:, :], in0=wsum[:, :],
                                                  in1=wcol[:, :], op=AluOpType.add))
        absum_done.append(pr.c.get("v"))
    pr.wait("t", "v", pr.c.get("v"))
    td = pr.run("t", lambda t: t.matmul(psX[0:1, 0:1], wsum[:, :], ocf[:, :],
                                        start=True, stop=True))
    pr.wait("v", "t", td)
    pr.run("v", lambda v: v.tensor_copy(out=msc[:, 0:1], in_=psX[0:1, 0:1]))
    pr.run("v", (lambda n_el=n_el: lambda v: v.tensor_scalar(
        out=msc[:, 1:2], in0=msc[:, 0:1], scalar1=1.0 / n_el, scalar2=1e-5,
        op0=AluOpType.mult, op1=AluOpType.max))())
    pr.run("v", lambda v: v.reciprocal(out=msc[:, 2:3], in_=msc[:, 1:2]))
    V_SC = pr.c.get("v")
    pr.wait("t", "v", V_SC)
    td = pr.run("t", lambda t: t.matmul(psX[0:128, 0:1], oc1f[:, :], msc[:, 2:3],
                                        start=True, stop=True))
    pr.wait("v", "t", td)
    pr.run("v", lambda v: v.tensor_copy(out=wsc128[:, :], in_=psX[0:128, 0:1]))
    return absum_done


def emit_ternarize_group(pr, wf, wq_ap, wsc128, half, group_sz, wtmp):
    for cc in range(group_sz):
        src_ap = wf[:, (half * group_sz + cc) * 1024:(half * group_sz + cc + 1) * 1024]
        dst = wq_ap(cc)
        pr.run("v", (lambda src_ap=src_ap: lambda v: v.tensor_scalar(
            out=wtmp[:, :], in0=src_ap, scalar1=wsc128[:, :], scalar2=1.0,
            op0=AluOpType.mult, op1=AluOpType.min))())
        pr.run("v", lambda v: v.tensor_scalar(
            out=wtmp[:, :], in0=wtmp[:, :], scalar1=-1.0, scalar2=MAGIC,
            op0=AluOpType.max, op1=AluOpType.add))
        pr.run("v", (lambda dst=dst: lambda v: v.tensor_scalar(
            out=dst, in0=wtmp[:, :], scalar1=MAGIC, scalar2=None,
            op0=AluOpType.subtract))())
    return pr.c.get("v")



def build_fused():
    nc = bass.Bass("TRN2", target_bir_lowering=False)

    def din(name, shape, dt=F32):
        return nc.dram_tensor(name, shape, dt, kind="ExternalInput").ap()

    I = Ctx()
    I.xT = din("xT", [D, T])
    I.wq = din("wq", [D, D]); I.bq = din("bq", [D])
    I.wk = din("wk", [D, HD]); I.bk = din("bk", [HD])
    I.wv = din("wv", [D, HD]); I.bv = din("bv", [HD])
    I.wo = din("wo", [D, D]); I.bo = din("bo", [D])
    I.w1T = din("w1T", [D, 4096])
    I.cw = din("cw", [4096, 3]); I.cb = din("cb", [4096])
    I.w2T = din("w2T", [4096, D])
    I.wqkv = din("wqkv", [D, 3 * D])
    I.wproj = din("wproj", [D, D])
    I.wenc = din("wenc", [D, 2048]); I.benc = din("benc", [2048])
    I.ws2c = din("ws2c", [2048, 64]); I.bs2c = din("bs2c", [64])
    I.wr = din("wr", [64, 8]); I.br = din("br", [8])
    I.w1e = din("w1e", [8, 64, 1024]); I.b1e = din("b1e", [8, 1024])
    I.w2e = din("w2e", [8, 1024, 64]); I.b2e = din("b2e", [8, 64])
    I.wc2s = din("wc2s", [64, 2048]); I.bc2s = din("bc2s", [2048])
    I.wdec = din("wdec", [2048, D]); I.bdec = din("bdec", [D])
    I.lng = din("lng", [D]); I.lnb = din("lnb", [D])
    I.ropeR = din("ropeR", [8, 128, 128], BF16)
    I.tri_t = din("tri", [128, 512], BF16)
    I.i128_t = din("i128", [128, 128], BF16)
    I.hm_t = din("hm", [5, 128, 8], BF16)
    I.i8_t = din("i8", [8, 8], F32)
    I.ones8_t = din("ones8", [8, 128], F32)
    I.onesf_t = din("onesf", [1, 64], BF16)
    I.oc1f_t = din("oc1f", [1, 128], F32)
    I.c127_t = din("c127", [1, 128], F32)
    I.ocf_t = din("ocf", [128, 1], F32)
    I.ocb_t = din("ocb", [128, 1], BF16)
    I.or512_t = din("or512", [1, 512], BF16)
    I.oc64_t = din("oc64", [1, 64], BF16)
    I.xo = nc.dram_tensor("xo", [D, T], F32, kind="ExternalOutput").ap()

    S = Ctx()
    S.qd0 = nc.dram_tensor("qd0", [D, T], BF16).ap()
    S.kd0 = nc.dram_tensor("kd0", [HD, T], BF16).ap()
    S.vd0 = nc.dram_tensor("vd0", [T, HD], BF16).ap()
    S.attd = nc.dram_tensor("attd", [D, T], BF16).ap()
    S.x1park = nc.dram_tensor("x1park", [D, T], F32).ap()
    S.hD = nc.dram_tensor("hD", [4096, T], BF16).ap()
    S.csx = nc.dram_tensor("csx", [4096, T], BF16).ap()
    S.qx = nc.dram_tensor("qx", [4096, T], BF16).ap()
    S.x2park = nc.dram_tensor("x2park", [D, T], F32).ap()
    S.qd1 = nc.dram_tensor("qd1", [D, T], BF16).ap()
    S.kd1 = nc.dram_tensor("kd1", [D, T], BF16).ap()
    S.vd1 = nc.dram_tensor("vd1", [NH * T, HD], BF16).ap()
    S.x3park = nc.dram_tensor("x3park", [D, T], F32).ap()
    S.gdram = nc.dram_tensor("gdram", [8, TC], BF16).ap()

    pr = Prog(nc)
    c0 = pr.c

    with ExitStack() as _st:
        E = _st.enter_context
        K = Ctx()
        K.i128 = E(nc.sbuf_tensor("i128_s", [128, 128], BF16))
        K.tri = E(nc.sbuf_tensor("tri_s", [128, 512], BF16))
        K.hmb = E(nc.sbuf_tensor("hm_s", [128, 5 * 8], BF16))
        K.i8 = E(nc.sbuf_tensor("i8_s", [8, 8], F32))
        K.ones8 = E(nc.sbuf_tensor("ones8_s", [8, 128], F32))
        K.onesf = E(nc.sbuf_tensor("onesf_s", [1, 64], BF16))
        K.oc1f = E(nc.sbuf_tensor("oc1f_s", [1, 128], F32))
        K.c127 = E(nc.sbuf_tensor("c127_s", [1, 128], F32))
        K.ocf = E(nc.sbuf_tensor("ocf_s", [128, 1], F32))
        K.ocb = E(nc.sbuf_tensor("ocb_s", [128, 1], BF16))
        K.or512 = E(nc.sbuf_tensor("or512_s", [1, 512], BF16))
        K.oc64 = E(nc.sbuf_tensor("oc64_s", [1, 64], BF16))
        K.epsb = E(nc.sbuf_tensor("epsb", [1, 1], F32))
        psA = E(nc.psum_tensor("psA", [128, 512], F32))
        psB = E(nc.psum_tensor("psB", [128, 512], F32))
        psC = E(nc.psum_tensor("psC", [128, 512], F32))
        psD = E(nc.psum_tensor("psD", [128, 512], F32))
        psE = E(nc.psum_tensor("psE", [128, 512], F32))
        psF = E(nc.psum_tensor("psF", [128, 512], F32))
        PS = [psA, psB, psC, psD, psE, psF]
        sem_names = ["dh", "ds", "t", "s", "v", "st", "sg",
                     "w0", "w1", "h0", "h1", "c0", "c1", "sp"]
        sems = {n: E(nc.semaphore("s_" + n)) for n in sem_names}
        block = E(nc.Block())

        P = dict(nc=nc, pr=pr, c0=c0, sems=sems, block=block, K=K, I=I, S=S,
                 PS=PS)

        def sync_consts(sync):
            for dst, srct in ((K.i128, I.i128_t), (K.tri, I.tri_t),
                              (K.i8, I.i8_t), (K.ones8, I.ones8_t),
                              (K.onesf, I.onesf_t), (K.oc1f, I.oc1f_t),
                              (K.c127, I.c127_t), (K.ocf, I.ocf_t),
                              (K.ocb, I.ocb_t), (K.or512, I.or512_t),
                              (K.oc64, I.oc64_t)):
                sync.dma_start(out=dst[:, :], in_=srct[:, :]).then_inc(sems["dh"], 16)
                c0.inc("dh", 16)
            for t5 in range(5):
                sync.dma_start(out=K.hmb[:, t5 * 8:(t5 + 1) * 8],
                               in_=I.hm_t[t5, :, :]).then_inc(sems["dh"], 16)
                c0.inc("dh", 16)
        block.sync(sync_consts)
        pr.wait("t", "dh", c0.get("dh"))
        pr.wait("v", "dh", c0.get("dh"))
        pr.wait("s", "dh", c0.get("dh"))

        for half in range(2):
            emit_p1_half(P, half)
        emit_attn(P, mqa=True, q_src=S.qd0, k_src=S.kd0, v_src=S.vd0,
                  att_dst=S.attd)
        for half in range(2):
            emit_proj_ln_half(P, half, att_src=S.attd, w_src=I.wo,
                              bias_src=I.bo, x_src=I.xT, xpark=S.x1park,
                              tag="p3")
        for half in range(2):
            emit_ffn_a_half(P, half)
        emit_w2quant(P)
        for half in range(2):
            emit_conv_half(P, half)
            emit_bitlin2_ln_half(P, half)
        P["msc2cm"].__exit__(None, None, None)
        P["w2qcm"].__exit__(None, None, None)
        for half in range(2):
            emit_l1_qkv_half(P, half)
        emit_attn(P, mqa=False, q_src=S.qd1, k_src=S.kd1, v_src=S.vd1,
                  att_dst=S.attd)
        for half in range(2):
            emit_proj_ln_half(P, half, att_src=S.attd, w_src=I.wproj,
                              bias_src=None, x_src=S.x2park, xpark=S.x3park,
                              tag="p9")
        for half in range(2):
            emit_l1_ffn_half(P, half)

        pr.emit(block, sems)
        FINAL_ST = c0.get("st")
        FINAL_SG = c0.get("sg")

        def sync_final(sync):
            sync.wait_ge(sems["st"], FINAL_ST)
            sync.wait_ge(sems["sg"], FINAL_SG)
        block.sync(sync_final)
    return nc



def emit_p1_half(P, half):
    nc, pr, c0, sems, block = P["nc"], P["pr"], P["c0"], P["sems"], P["block"]
    K, I, S = P["K"], P["I"], P["S"]
    psA, psB, psC, psD, psE, psF = P["PS"]
    t0 = half * TC
    _p = ExitStack()
    xb = _p.enter_context(nc.sbuf_tensor(f"p1x{half}", [128, 8 * TC], BF16))
    wqb = _p.enter_context(nc.sbuf_tensor(f"p1q{half}", [128, 8 * D], BF16))
    wkf = _p.enter_context(nc.sbuf_tensor(f"p1kf{half}", [128, 8 * HD], F32))
    wkp = _p.enter_context(nc.sbuf_tensor(f"p1kp{half}", [128, 8 * HD], BF16))
    wvb = _p.enter_context(nc.sbuf_tensor(f"p1wv{half}", [128, 8 * HD], BF16))
    Rb = _p.enter_context(nc.sbuf_tensor(f"p1R{half}", [128, 8 * 128], BF16))
    bqc = _p.enter_context(nc.sbuf_tensor(f"p1bq{half}", [128, 8], F32))
    bkc = _p.enter_context(nc.sbuf_tensor(f"p1bk{half}", [64, 1], F32))
    bvr = _p.enter_context(nc.sbuf_tensor(f"p1bv{half}", [1, HD], BF16))
    qpt = _p.enter_context(nc.sbuf_tensor(f"p1qp{half}", [128, 2 * 512], BF16))
    qrt = _p.enter_context(nc.sbuf_tensor(f"p1qr{half}", [128, 2 * 512], BF16))
    kst = _p.enter_context(nc.sbuf_tensor(f"p1ks{half}", [64, TC], BF16))
    vst = _p.enter_context(nc.sbuf_tensor(f"p1vs{half}", [128, 2 * HD], BF16))

    V0, T0_, S0 = c0.get("v"), c0.get("t"), c0.get("s")
    STP, SGP = c0.get("st"), c0.get("sg")

    def p_loads_sync(sync):
        sync.wait_ge(sems["v"], V0)
        sync.wait_ge(sems["t"], T0_)
        sync.wait_ge(sems["st"], STP)
        sync.wait_ge(sems["sg"], SGP)
        for cc in range(8):
            sync.dma_start(out=wkf[:, cc * HD:(cc + 1) * HD],
                           in_=I.wk[cc * 128:(cc + 1) * 128, :]).then_inc(sems["dh"], 16)
            c0.inc("dh", 16)
        for cc in range(8):
            sync.dma_start(out=bqc[:, cc:cc + 1],
                           in_=I.bq[cc * 128:(cc + 1) * 128].rearrange("(p o) -> p o", o=1)
                           ).then_inc(sems["dh"], 16)
            c0.inc("dh", 16)
        for i, base in enumerate((0, 1, 32, 33)):
            sync.dma_start(out=bkc[16 * i:16 * (i + 1), :],
                           in_=I.bk[base:min(base + 32, HD):2].rearrange("(p o) -> p o", o=1),
                           allow_slow_non_contiguous=True).then_inc(sems["dh"], 16)
            c0.inc("dh", 16)
        for cc in range(8):
            sync.dma_start(out=Rb[:, cc * 128:(cc + 1) * 128],
                           in_=I.ropeR[cc, :, :]).then_inc(sems["dh"], 16)
            c0.inc("dh", 16)
    block.sync(p_loads_sync)
    P_DH = c0.get("dh")

    def p_loads_gp(gpsimd):
        gpsimd.wait_ge(sems["v"], V0)
        gpsimd.wait_ge(sems["t"], T0_)
        gpsimd.wait_ge(sems["st"], STP)
        gpsimd.wait_ge(sems["sg"], SGP)
        for cc in range(8):
            gpsimd.dma_start(out=xb[:, cc * TC:(cc + 1) * TC],
                             in_=I.xT[cc * 128:(cc + 1) * 128, t0:t0 + TC]
                             ).then_inc(sems["ds"], 16)
            gpsimd.dma_start(out=wqb[:, cc * D:(cc + 1) * D],
                             in_=I.wq[cc * 128:(cc + 1) * 128, :]).then_inc(sems["ds"], 16)
            gpsimd.dma_start(out=wvb[:, cc * HD:(cc + 1) * HD],
                             in_=I.wv[cc * 128:(cc + 1) * 128, :]).then_inc(sems["ds"], 16)
            c0.inc("ds", 48)
        gpsimd.dma_start(out=bvr[:, :], in_=I.bv.rearrange("(o h) -> o h", o=1)
                         ).then_inc(sems["ds"], 16)
        c0.inc("ds", 16)
    block.gpsimd(p_loads_gp)
    P_DS = c0.get("ds")

    pr.wait("v", "dh", P_DH)
    for cc in range(8):
        src = wkf[:, cc * HD:(cc + 1) * HD]
        dst = wkp[:, cc * HD:(cc + 1) * HD]
        for i, base in enumerate((0, 1, 32, 33)):
            pr.run("v", (lambda src=src, dst=dst, i=i, base=base: lambda v:
                         v.tensor_copy(out=dst[:, 16 * i:16 * (i + 1)],
                                       in_=src[:, base:min(base + 32, HD):2]))())
    V_PERM = c0.get("v")

    pr.wait("t", "ds", P_DS)
    pr.wait("t", "dh", P_DH)
    pr.wait("t", "v", V_PERM)
    SB = c0.get("s")
    STB = c0.get("st")
    q_store = {}
    for oc in range(8):
        for tci in range(4):
            gi = oc * 4 + tci
            slot = gi % 2
            ps = psA if slot == 0 else psB
            if gi >= 2:
                pr.wait("t", "s", SB + gi - 1)
            def mk_qmm(oc=oc, tci=tci, ps=ps):
                def f(tensor):
                    mm = None
                    for ic in range(8):
                        mm = tensor.matmul(
                            ps[:, :],
                            wqb[:, ic * D + oc * 128: ic * D + (oc + 1) * 128],
                            xb[:, ic * TC + tci * 512: ic * TC + (tci + 1) * 512],
                            start=(ic == 0), stop=(ic == 7))
                    return mm
                return f
            tq = pr.run("t", mk_qmm())
            pr.wait("s", "t", tq)
            if gi >= 2:
                pr.wait("s", "t", c0.get("p1_rope_%d_%d" % (half, slot), 0))
            sd = pr.run("s", (lambda oc=oc, slot=slot, ps=ps: lambda s: s.activation(
                qpt[:, slot * 512:(slot + 1) * 512], ps[:, :], AF.Identity,
                bias=bqc[:, oc:oc + 1], scale=1.0))())
            pr.wait("t", "s", sd)
            if gi >= 1:
                pr.wait("t", "v", c0.get("p1_qr_%d_%d" % (half, gi - 1), 0))
            def mk_rope(oc=oc, slot=slot):
                def f(tensor):
                    return tensor.matmul(psC[:, :], Rb[:, oc * 128:(oc + 1) * 128],
                                         qpt[:, slot * 512:(slot + 1) * 512],
                                         start=True, stop=True)
                return f
            tr = pr.run("t", mk_rope())
            c0.v["p1_rope_%d_%d" % (half, slot)] = tr
            pr.wait("v", "t", tr)
            if gi >= 2:
                pr.wait("v", "st", STB + 16 * (gi - 1))
            vd = pr.run("v", (lambda slot=slot: lambda v: v.tensor_copy(
                out=qrt[:, slot * 512:(slot + 1) * 512], in_=psC[:, :]))())
            c0.v["p1_qr_%d_%d" % (half, gi)] = vd
            q_store[gi] = (vd, oc, tci, slot)

    def p_qstores(sync):
        for gi in range(32):
            vd, oc, tci, slot = q_store[gi]
            sync.wait_ge(sems["v"], vd)
            sync.dma_start(
                out=S.qd0[oc * 128:(oc + 1) * 128, t0 + tci * 512:t0 + (tci + 1) * 512],
                in_=qrt[:, slot * 512:(slot + 1) * 512]).then_inc(sems["st"], 16)
            c0.inc("st", 16)
    block.sync(p_qstores)

    for tci in range(4):
        if tci >= 1:
            pr.wait("t", "s", c0.get("p1_k_%d" % half, 0))
        def mk_kmm(tci=tci):
            def f(tensor):
                mm = None
                for ic in range(8):
                    mm = tensor.matmul(
                        psD[0:64, :], wkp[:, ic * HD:(ic + 1) * HD],
                        xb[:, ic * TC + tci * 512: ic * TC + (tci + 1) * 512],
                        start=(ic == 0), stop=(ic == 7))
                return mm
            return f
        tk = pr.run("t", mk_kmm())
        pr.wait("s", "t", tk)
        sd = pr.run("s", (lambda tci=tci: lambda s: s.activation(
            kst[:, tci * 512:(tci + 1) * 512], psD[0:64, :],
            AF.Identity, bias=bkc[:, :], scale=1.0))())
        c0.v["p1_k_%d" % half] = sd
    S_KDONE = c0.get("s")

    v_store = {}
    STB2 = c0.get("st")
    for tt in range(16):
        slot = tt % 2
        ps = psE if slot == 0 else psF
        if tt >= 2:
            pr.wait("t", "v", c0.get("p1_v_%d_%d" % (half, tt - 2), 0))
        def mk_vmm(tt=tt, ps=ps):
            def f(tensor):
                for ic in range(8):
                    tensor.matmul(ps[0:128, 0:HD],
                                  xb[:, ic * TC + tt * 128: ic * TC + (tt + 1) * 128],
                                  wvb[:, ic * HD:(ic + 1) * HD],
                                  start=(ic == 0), stop=False)
                return tensor.matmul(ps[0:128, 0:HD], K.or512[:, 0:128],
                                     bvr[:, :], start=False, stop=True)
            return f
        tv = pr.run("t", mk_vmm())
        pr.wait("v", "t", tv)
        if tt >= 2:
            pr.wait("v", "st", STB2 + 16 * tt)
        vd = pr.run("v", (lambda tt=tt, slot=slot, ps=ps: lambda v: v.tensor_copy(
            out=vst[:, slot * HD:(slot + 1) * HD], in_=ps[0:128, 0:HD]))())
        c0.v["p1_v_%d_%d" % (half, tt)] = vd
        v_store[tt] = (vd, tt, slot)

    def p_kvstores(sync):
        sync.wait_ge(sems["s"], S_KDONE)
        sync.dma_start(out=S.kd0[:, t0:t0 + TC], in_=kst[:, :]).then_inc(sems["st"], 16)
        c0.inc("st", 16)
        for tt in range(16):
            vd, _, slot = v_store[tt]
            sync.wait_ge(sems["v"], vd)
            sync.dma_start(out=S.vd0[t0 + tt * 128:t0 + (tt + 1) * 128, :],
                           in_=vst[:, slot * HD:(slot + 1) * HD]).then_inc(sems["st"], 16)
            c0.inc("st", 16)
    block.sync(p_kvstores)
    pr.barrier()
    pr.wait("v", "st", c0.get("st"))
    pr.wait("t", "st", c0.get("st"))
    pr.wait("s", "st", c0.get("st"))
    _p.close()



def emit_attn(P, mqa, q_src, k_src, v_src, att_dst):
    nc, pr, c0, sems, block = P["nc"], P["pr"], P["c0"], P["sems"], P["block"]
    K = P["K"]
    psA, psB, psC, psD, psE, psF = P["PS"]
    tri, i128, hmb, i8, ones8, onesf = (K.tri, K.i128, K.hmb, K.i8, K.ones8,
                                        K.onesf)
    psPV = [psC, psD, psE, psF]
    NQB = T // 512
    nvt = 32 if mqa else 256
    for p in range(2):
        sfx = f"{'ab'[p]}{int(mqa)}"
        _pa = ExitStack()
        qh = _pa.enter_context(nc.sbuf_tensor("qh" + sfx, [128, 4 * T], BF16))
        kf = _pa.enter_context(nc.sbuf_tensor("kf" + sfx,
                                              [128, (T if mqa else 4 * T)], BF16))
        va = _pa.enter_context(nc.sbuf_tensor("va" + sfx, [128, nvt * 65], BF16))
        n2 = _pa.enter_context(nc.sbuf_tensor("n2" + sfx, [8, 2 * T], BF16))
        mx = _pa.enter_context(nc.sbuf_tensor("mx" + sfx, [8, 4], F32))
        i8n = _pa.enter_context(nc.sbuf_tensor("i8n" + sfx, [8, 8], F32))
        negm = _pa.enter_context(nc.sbuf_tensor("ng" + sfx, [128, 8], F32))
        pbuf = _pa.enter_context(nc.sbuf_tensor("pb" + sfx, [128, 7 * 512], BF16))
        drow = _pa.enter_context(nc.sbuf_tensor("dr" + sfx, [1, 4 * 512], F32))
        drowb = _pa.enter_context(nc.sbuf_tensor("db" + sfx, [1, 4 * 512], BF16))
        drow2 = _pa.enter_context(nc.sbuf_tensor("d2" + sfx, [1, 4 * 512], F32))
        bco = _pa.enter_context(nc.sbuf_tensor("bc" + sfx, [128, 512], F32))
        psM = _pa.enter_context(nc.psum_tensor("psM" + sfx, [128, 512], F32))
        psX = _pa.enter_context(nc.psum_tensor("psX" + sfx, [128, 512], F32))
        psS = [psA, psB, psM]

        V_FREE = c0.get("v")
        T_FREE = c0.get("t")
        ST_RDY = c0.get("st")
        SG_RDY = c0.get("sg")

        def pass_loads(sync, p=p):
            sync.wait_ge(sems["st"], ST_RDY)
            sync.wait_ge(sems["sg"], SG_RDY)
            sync.wait_ge(sems["v"], V_FREE)
            sync.wait_ge(sems["t"], T_FREE)
            for t5 in range(4):
                sync.dma_start(out=qh[:, t5 * T:(t5 + 1) * T],
                               in_=q_src[p * 512 + t5 * 128: p * 512 + (t5 + 1) * 128, :]
                               ).then_inc(sems["dh"], 16)
                c0.inc("dh", 16)
            if mqa:
                sync.dma_start(out=kf[0:64, :], in_=k_src[:, :]).then_inc(sems["dh"], 16)
                sync.dma_start(out=kf[64:128, :], in_=k_src[:, :]).then_inc(sems["dh"], 16)
                c0.inc("dh", 32)
                for kt in range(32):
                    sync.dma_start(out=va[:, kt * 65:kt * 65 + 64],
                                   in_=v_src[kt * 128:(kt + 1) * 128, :]
                                   ).then_inc(sems["dh"], 16)
                    c0.inc("dh", 16)
            else:
                for t5 in range(4):
                    sync.dma_start(out=kf[:, t5 * T:(t5 + 1) * T],
                                   in_=k_src[p * 512 + t5 * 128: p * 512 + (t5 + 1) * 128, :]
                                   ).then_inc(sems["dh"], 16)
                    c0.inc("dh", 16)
                for hl in range(8):
                    hgl = p * 8 + hl
                    for kt in range(32):
                        sync.dma_start(
                            out=va[:, (hl * 32 + kt) * 65:(hl * 32 + kt) * 65 + 64],
                            in_=v_src[hgl * T + kt * 128: hgl * T + (kt + 1) * 128, :]
                        ).then_inc(sems["dh"], 16)
                        c0.inc("dh", 16)
        block.sync(pass_loads)
        LOADS_DH = c0.get("dh")

        pr.wait("v", "dh", LOADS_DH)
        for i in range(nvt):
            pr.run("v", (lambda i=i: lambda v: v.memset(
                va[:, i * 65 + 64:(i + 1) * 65], 1.0))())

        _cm_qsq = nc.sbuf_tensor("qsq" + sfx, [128, 4 * T], BF16)
        qsq = _cm_qsq.__enter__()
        for t5 in range(4):
            pr.run("v", (lambda t5=t5: lambda v: v.tensor_tensor(
                out=qsq[:, t5 * T:(t5 + 1) * T], in0=qh[:, t5 * T:(t5 + 1) * T],
                in1=qh[:, t5 * T:(t5 + 1) * T], op=AluOpType.mult))())
        V_QSQ = c0.get("v")
        pr.wait("t", "v", V_QSQ)
        pr.wait("t", "dh", LOADS_DH)
        for cc in range(8):
            if cc >= 1:
                pr.wait("t", "v", V_QSQ + cc)
            def mk_qn(cc=cc):
                def f(tensor):
                    mm = None
                    for t5 in range(4):
                        mm = tensor.matmul(psM[0:8, :], hmb[:, t5 * 8:(t5 + 1) * 8],
                                           qsq[:, t5 * T + cc * 512: t5 * T + (cc + 1) * 512],
                                           start=(t5 == 0), stop=(t5 == 3))
                    return mm
                return f
            tq = pr.run("t", mk_qn())
            pr.wait("v", "t", tq)
            pr.run("v", (lambda cc=cc: lambda v: v.tensor_copy(
                out=n2[0:8, cc * 512:(cc + 1) * 512], in_=psM[0:8, :]))())
        pr.wait("v", "t", c0.get("t"))
        for t5 in range(1 if mqa else 4):
            pr.run("v", (lambda t5=t5: lambda v: v.tensor_tensor(
                out=qsq[:, t5 * T:(t5 + 1) * T], in0=kf[:, t5 * T:(t5 + 1) * T],
                in1=kf[:, t5 * T:(t5 + 1) * T], op=AluOpType.mult))())
        V_KSQ = c0.get("v")
        pr.wait("t", "v", V_KSQ)
        for cc in range(8):
            if cc >= 1:
                pr.wait("t", "v", V_KSQ + cc)
            def mk_kn(cc=cc):
                def f(tensor):
                    if mqa:
                        return tensor.matmul(psM[0:8, :], hmb[:, 32:40],
                                             qsq[:, cc * 512:(cc + 1) * 512],
                                             start=True, stop=True)
                    mm = None
                    for t5 in range(4):
                        mm = tensor.matmul(psM[0:8, :], hmb[:, t5 * 8:(t5 + 1) * 8],
                                           qsq[:, t5 * T + cc * 512: t5 * T + (cc + 1) * 512],
                                           start=(t5 == 0), stop=(t5 == 3))
                    return mm
                return f
            tk = pr.run("t", mk_kn())
            pr.wait("v", "t", tk)
            pr.run("v", (lambda cc=cc: lambda v: v.tensor_copy(
                out=n2[0:8, T + cc * 512: T + (cc + 1) * 512], in_=psM[0:8, :]))())
        pr.run("v", lambda v: v.tensor_reduce(out=mx[0:8, 0:1], in_=n2[0:8, 0:T],
                                              axis=mybir.AxisListType.X, op=AluOpType.max))
        pr.run("v", lambda v: v.tensor_reduce(out=mx[0:8, 1:2], in_=n2[0:8, T:2 * T],
                                              axis=mybir.AxisListType.X, op=AluOpType.max))
        pr.run("v", lambda v: v.tensor_tensor(out=mx[0:8, 2:3], in0=mx[0:8, 0:1],
                                              in1=mx[0:8, 1:2], op=AluOpType.mult))
        V_M2 = c0.get("v")
        pr.wait("s", "v", V_M2)
        S_SQRT = pr.run("s", lambda s: s.activation(mx[0:8, 3:4], mx[0:8, 2:3],
                                                    AF.Sqrt, scale=1.0 / 64.0))
        pr.wait("v", "s", S_SQRT)
        pr.run("v", lambda v: v.tensor_scalar(out=i8n[:, :], in0=i8[:, :],
                                              scalar1=mx[0:8, 3:4], scalar2=-1.0,
                                              op0=AluOpType.mult, op1=AluOpType.mult))
        V_I8N = c0.get("v")
        pr.wait("t", "v", V_I8N)
        T_NEGM = pr.run("t", lambda t: t.matmul(psX[:, 0:8], ones8[:, :], i8n[:, :],
                                                start=True, stop=True))
        pr.wait("v", "t", T_NEGM)
        pr.run("v", lambda v: v.tensor_copy(out=negm[:, :], in_=psX[:, 0:8]))
        for slot, delta in ((4, 128), (5, 256), (6, 384)):
            pr.run("v", (lambda slot=slot, delta=delta: lambda v: v.memset(
                pbuf[:, slot * 512:slot * 512 + delta], 0.0))())
        PRE_V = c0.get("v")
        pr.wait("t", "v", PRE_V)
        pr.wait("s", "v", PRE_V)
        _cm_qsq.__exit__(None, None, None)
        _cm_att = nc.sbuf_tensor("att" + sfx, [128, 4 * T], BF16)
        att = _cm_att.__enter__()

        def k_lhsT(h, kt):
            p0 = (h % 2) * 64
            off = 0 if mqa else (h // 2) * T
            return kf[p0:p0 + 64, off + kt * 128:off + (kt + 1) * 128]

        def va_lhsT(h, kt):
            i = kt if mqa else (h * 32 + kt)
            return va[:, i * 65:(i + 1) * 65]

        exp_done_at = {}
        pbuf_last_pv = {i: 0 for i in range(7)}
        pv_stop_at = {}
        norm_done_at = {}
        last_psX_free = 0
        sc_idx = 0
        pend_pv = [None]

        def flush_pv():
            if pend_pv[0] is None:
                return
            i, hi, h, kt, slot, start, stop, qb_, hg_ = pend_pv[0]
            pend_pv[0] = None
            pr.wait("t", "s", exp_done_at[i])
            if kt == 0:
                pq = (qb_, hg_ - 1) if hg_ == 1 else (qb_ - 1, 1)
                if pq in norm_done_at:
                    pr.wait("t", "v", norm_done_at[pq])
            def mk_pv(hi=hi, h=h, kt=kt, slot=slot, start=start, stop=stop):
                def f(tensor):
                    return tensor.matmul(psPV[hi][0:65, :], va_lhsT(h, kt),
                                         pbuf[:, slot * 512:(slot + 1) * 512],
                                         start=start, stop=stop)
                return f
            pv_t = pr.run("t", mk_pv())
            pbuf_last_pv[slot] = pv_t
            if stop:
                pv_stop_at[(qb_, hg_, hi)] = pv_t

        for qb in range(NQB):
            nkt = 4 * (qb + 1)
            for hg in range(2):
                for kt in range(nkt):
                    dd = kt * 128 - qb * 512
                    for hi in range(4):
                        h = hg * 4 + hi
                        i = sc_idx
                        bank = psS[i % 3]
                        lo = max(0, dd)
                        if i >= 3:
                            pr.wait("t", "s", exp_done_at[i - 3])
                        def mk_sc(bank=bank, h=h, kt=kt, qb=qb, dd=dd, lo=lo):
                            def f(tensor):
                                if dd >= 0:
                                    tensor.matmul(bank[0:128, lo:512],
                                                  i128[:, :], tri[:, 0:512 - lo],
                                                  start=True, stop=False,
                                                  skip_group_check=True)
                                return tensor.matmul(
                                    bank[0:128, lo:512], k_lhsT(h, kt),
                                    qh[(h % 2) * 64:(h % 2) * 64 + 64,
                                       (h // 2) * T + qb * 512 + lo:(h // 2) * T + (qb + 1) * 512],
                                    start=(dd < 0), stop=True,
                                    skip_group_check=(dd >= 0))
                            return f
                        sc_t = pr.run("t", mk_sc())
                        slot = (i % 4) if dd <= 0 else 3 + dd // 128
                        pr.wait("s", "t", sc_t)
                        pr.wait("s", "t", pbuf_last_pv[slot])
                        def mk_exp(bank=bank, slot=slot, lo=lo, h=h):
                            def f(scalar):
                                return scalar.activation(
                                    pbuf[:, slot * 512 + lo:(slot + 1) * 512],
                                    bank[0:128, lo:512], AF.Exp,
                                    bias=negm[:, h:h + 1], scale=0.125)
                            return f
                        exp_done_at[i] = pr.run("s", mk_exp())
                        flush_pv()
                        pend_pv[0] = (i, hi, h, kt, slot,
                                      kt == 0, kt == nkt - 1, qb, hg)
                        sc_idx += 1
                flush_pv()
                for hi in range(4):
                    pr.wait("v", "t", pv_stop_at[(qb, hg, hi)])
                    pr.run("v", (lambda hi=hi: lambda v: v.tensor_copy(
                        out=drowb[:, hi * 512:(hi + 1) * 512],
                        in_=psPV[hi][64:65, :]))())
                V_RECIP = c0.get("v")
                for pp in range(2):
                    pr.wait("t", "v", V_RECIP)
                    if last_psX_free:
                        pr.wait("t", "v", last_psX_free)
                    def mk_bc(pp=pp):
                        def f(tensor):
                            tensor.matmul(psX[0:64, :], onesf[:, :],
                                          drowb[:, (2 * pp) * 512:(2 * pp + 1) * 512],
                                          start=True, stop=True)
                            return tensor.matmul(psX[64:128, :], onesf[:, :],
                                                 drowb[:, (2 * pp + 1) * 512:(2 * pp + 2) * 512],
                                                 start=True, stop=True)
                        return f
                    bc_t = pr.run("t", mk_bc())
                    pr.wait("v", "t", bc_t)
                    pr.run("v", lambda v: v.reciprocal(out=bco[:, :], in_=psX[:, :]))
                    last_psX_free = c0.get("v")
                    for sub in range(2):
                        hi = pp * 2 + sub
                        h = hg * 4 + hi
                        def mk_nm(hi=hi, h=h, sub=sub, qb=qb):
                            def f(vector):
                                return vector.tensor_tensor(
                                    out=att[(h % 2) * 64:(h % 2) * 64 + 64,
                                            (h // 2) * T + qb * 512:(h // 2) * T + (qb + 1) * 512],
                                    in0=psPV[hi][0:64, :],
                                    in1=bco[sub * 64:(sub + 1) * 64, :],
                                    op=AluOpType.mult)
                            return f
                        pr.run("v", mk_nm())
                norm_done_at[(qb, hg)] = c0.get("v")
        V_ATT_DONE = c0.get("v")

        def pass_stores(sync, p=p, V_ATT_DONE=V_ATT_DONE):
            sync.wait_ge(sems["v"], V_ATT_DONE)
            for t5 in range(4):
                sync.dma_start(
                    out=att_dst[p * 512 + t5 * 128: p * 512 + (t5 + 1) * 128, :],
                    in_=att[:, t5 * T:(t5 + 1) * T]).then_inc(sems["st"], 16)
                c0.inc("st", 16)
        block.sync(pass_stores)
        _cm_att.__exit__(None, None, None)
        _pa.close()
        pr.barrier()
        pr.wait("v", "st", c0.get("st"))
        pr.wait("t", "st", c0.get("st"))
        pr.wait("s", "st", c0.get("st"))



def emit_proj_ln_half(P, half, att_src, w_src, bias_src, x_src, xpark, tag):
    nc, pr, c0, sems, block = P["nc"], P["pr"], P["c0"], P["sems"], P["block"]
    K = P["K"]
    psA, psB, psC, psD, psE, psF = P["PS"]
    t0 = half * TC
    tg = f"{tag}{half}"
    _p = ExitStack()
    borow = _p.enter_context(nc.sbuf_tensor(tg + "bo", [1, D], BF16))
    rbuf = [_p.enter_context(nc.sbuf_tensor(f"{tg}rf{i}", [128, TC], F32))
            for i in range(8)]
    _cm_ab = nc.sbuf_tensor(tg + "ab", [128, 8 * TC], BF16); ab = _cm_ab.__enter__()
    _cm_xb = nc.sbuf_tensor(tg + "xb", [128, 8 * TC], F32); xb = _cm_xb.__enter__()
    _cm_wb = nc.sbuf_tensor(tg + "wb", [128, 8 * D], BF16); wb = _cm_wb.__enter__()

    V0, T0_ = c0.get("v"), c0.get("t")
    ST0, SG0 = c0.get("st"), c0.get("sg")
    DH0 = c0.get("dh")
    DS0 = c0.get("ds")

    def p_loads(sync):
        sync.wait_ge(sems["v"], V0)
        sync.wait_ge(sems["t"], T0_)
        sync.wait_ge(sems["st"], ST0)
        sync.wait_ge(sems["sg"], SG0)
        for cc in range(8):
            sync.dma_start(out=xb[:, cc * TC:(cc + 1) * TC],
                           in_=x_src[cc * 128:(cc + 1) * 128, t0:t0 + TC]
                           ).then_inc(sems["dh"], 16)
            sync.dma_start(out=ab[:, cc * TC:(cc + 1) * TC],
                           in_=att_src[cc * 128:(cc + 1) * 128, t0:t0 + TC]
                           ).then_inc(sems["dh"], 16)
            c0.inc("dh", 32)
    block.sync(p_loads)

    def p_loads_gp(gpsimd):
        gpsimd.wait_ge(sems["v"], V0)
        gpsimd.wait_ge(sems["t"], T0_)
        gpsimd.wait_ge(sems["st"], ST0)
        gpsimd.wait_ge(sems["sg"], SG0)
        for cc in range(8):
            gpsimd.dma_start(out=wb[:, cc * D:(cc + 1) * D],
                             in_=w_src[cc * 128:(cc + 1) * 128, :]).then_inc(sems["ds"], 16)
            c0.inc("ds", 16)
        if bias_src is not None:
            gpsimd.dma_start(out=borow[:, :],
                             in_=bias_src.rearrange("(o h) -> o h", o=1)
                             ).then_inc(sems["ds"], 16)
            c0.inc("ds", 16)
    block.gpsimd(p_loads_gp)
    DH1 = c0.get("dh")
    DS1 = c0.get("ds")

    pr.wait("t", "dh", DH1)
    pr.wait("t", "ds", DS1)
    VB = c0.get("v")
    psRot = [psA, psB, psE, psF]
    gi = 0
    for oc in range(8):
        for tc_i in range(4):
            ps = psRot[gi % 4]
            if gi >= 4:
                pr.wait("t", "v", VB + gi - 3)
            def mk_mm(oc=oc, tc_i=tc_i, ps=ps, with_bias=bias_src is not None):
                def f(tensor):
                    mm = None
                    for ic in range(8):
                        mm = tensor.matmul(
                            ps[:, :], wb[:, ic * D + oc * 128: ic * D + (oc + 1) * 128],
                            ab[:, ic * TC + tc_i * 512: ic * TC + (tc_i + 1) * 512],
                            start=(ic == 0), stop=(ic == 7) and not with_bias)
                    if with_bias:
                        mm = tensor.matmul(ps[:, :],
                                           borow[:, oc * 128:(oc + 1) * 128],
                                           K.or512[:, :], start=False, stop=True)
                    return mm
                return f
            td = pr.run("t", mk_mm())
            pr.wait("v", "t", td)
            pr.run("v", (lambda oc=oc, tc_i=tc_i, ps=ps: lambda v: v.tensor_tensor(
                out=rbuf[oc][:, tc_i * 512:(tc_i + 1) * 512], in0=ps[:, :],
                in1=xb[:, oc * TC + tc_i * 512: oc * TC + (tc_i + 1) * 512],
                op=AluOpType.add))())
            gi += 1
    T_MM_DONE = c0.get("t")
    _cm_wb.__exit__(None, None, None)
    _cm_xb.__exit__(None, None, None)
    _cm_ab.__exit__(None, None, None)
    _cm_xo = nc.sbuf_tensor(tg + "xo", [128, 8 * TC], F32); xout = _cm_xo.__enter__()
    _cm_scr = nc.sbuf_tensor(tg + "sc", [128, 4 * 512], BF16); scr = _cm_scr.__enter__()
    _cm_rw = nc.sbuf_tensor(tg + "rw", [1, 4 * TC], F32); rows_b = _cm_rw.__enter__()
    rows = [rows_b[:, k * TC:(k + 1) * TC] for k in range(4)]
    _cm_bm = nc.sbuf_tensor(tg + "bm", [128, TC], F32); bcm = _cm_bm.__enter__()
    _cm_bs = nc.sbuf_tensor(tg + "bs", [128, TC], F32); bcs = _cm_bs.__enter__()
    ready = emit_ln_tail(pr, rbuf, scr, rows, bcm, bcs,
                         lambda ch: xout[:, ch * TC:(ch + 1) * TC],
                         psC, psD, psE, K.ocb, K.oc1f, K.epsb,
                         pre_wait=("t", T_MM_DONE))

    def p_xstores(gpsimd):
        for ch in range(8):
            gpsimd.wait_ge(sems["v"], ready[ch])
            gpsimd.dma_start(out=xpark[ch * 128:(ch + 1) * 128, t0:t0 + TC],
                             in_=xout[:, ch * TC:(ch + 1) * TC]).then_inc(sems["sg"], 16)
            c0.inc("sg", 16)
    block.gpsimd(p_xstores)
    pr.barrier()
    pr.wait("v", "sg", c0.get("sg"))
    pr.wait("t", "sg", c0.get("sg"))
    _cm_bs.__exit__(None, None, None)
    _cm_bm.__exit__(None, None, None)
    _cm_rw.__exit__(None, None, None)
    _cm_scr.__exit__(None, None, None)
    _cm_xo.__exit__(None, None, None)
    _p.close()



def emit_ffn_a_half(P, half):
    nc, pr, c0, sems, block = P["nc"], P["pr"], P["c0"], P["sems"], P["block"]
    K, I, S = P["K"], P["I"], P["S"]
    psA, psB, psC, psD, psE, psF = P["PS"]
    t0 = half * TC
    tg = f"fa{half}"
    NG = 4
    NGS = 8
    _p = ExitStack()
    xq = _p.enter_context(nc.sbuf_tensor(tg + "xq", [128, 8 * TC], BF16))
    mrow = _p.enter_context(nc.sbuf_tensor(tg + "mr", [1, TC], F32))
    rrow = _p.enter_context(nc.sbuf_tensor(tg + "rr", [1, TC], F32))
    wsum = _p.enter_context(nc.sbuf_tensor(tg + "ws", [128, 1], F32))
    wcol = _p.enter_context(nc.sbuf_tensor(tg + "wc", [128, 1], F32))
    msc = _p.enter_context(nc.sbuf_tensor(tg + "ms", [1, 4], F32))
    wsc128 = _p.enter_context(nc.sbuf_tensor(tg + "w1", [128, 1], F32))
    bcX = _p.enter_context(nc.sbuf_tensor(tg + "bX", [128, TC], F32))
    bcR = _p.enter_context(nc.sbuf_tensor(tg + "bR", [128, TC], F32))
    psT = _p.enter_context(nc.psum_tensor(tg + "pT", [128, 1024], BF16))
    psTR = _p.enter_context(nc.psum_tensor(tg + "pR", [128, 1024], BF16))

    _cm_xb = nc.sbuf_tensor(tg + "xb", [128, 8 * TC], F32); xb = _cm_xb.__enter__()
    _cm_s32 = nc.sbuf_tensor(tg + "s3", [128, TC], F32); s32 = _cm_s32.__enter__()
    _cm_sq = nc.sbuf_tensor(tg + "sq", [128, TC], F32); sqt = _cm_sq.__enter__()
    _cm_sb = nc.sbuf_tensor(tg + "sb", [128, TC + 16], BF16); sb = _cm_sb.__enter__()

    V0, T0_ = c0.get("v"), c0.get("t")
    ST0_, SG0_ = c0.get("st"), c0.get("sg")

    def p_loads(sync):
        sync.wait_ge(sems["v"], V0)
        sync.wait_ge(sems["t"], T0_)
        sync.wait_ge(sems["st"], ST0_)
        sync.wait_ge(sems["sg"], SG0_)
        for cc in range(8):
            sync.dma_start(out=xb[:, cc * TC:(cc + 1) * TC],
                           in_=S.x1park[cc * 128:(cc + 1) * 128, t0:t0 + TC]
                           ).then_inc(sems["dh"], 16)
            c0.inc("dh", 16)
    block.sync(p_loads)
    DH0 = c0.get("dh")

    pr.wait("v", "dh", DH0)
    pr.wait("t", "dh", DH0)
    pr.wait("s", "dh", DH0)
    emit_absmax_row(pr, lambda ch: xb[:, ch * TC:(ch + 1) * TC], 8,
                    s32, sb, psT, psTR, mrow, K.i128, sqtmp=sqt)
    pr.run("v", lambda v: v.tensor_scalar(out=mrow[:, :], in0=mrow[:, :],
                                          scalar1=1e-5, scalar2=None,
                                          op0=AluOpType.max))
    pr.run("v", lambda v: v.reciprocal(out=rrow[:, :], in_=mrow[:, :]))
    pr.wait("t", "v", c0.get("v"))
    emit_row_bcast(pr, lambda tc_i: rrow[:, tc_i * 512:(tc_i + 1) * 512],
                   bcX, K.c127, psC)
    for ch in range(8):
        pr.run("v", (lambda ch=ch: lambda v: v.tensor_tensor(
            out=s32[:, :], in0=xb[:, ch * TC:(ch + 1) * TC],
            in1=bcX[:, :], op=AluOpType.mult))())
        pr.run("v", (lambda ch=ch: lambda v: v.tensor_scalar(
            out=xq[:, ch * TC:(ch + 1) * TC], in0=s32[:, :],
            scalar1=MAGIC, scalar2=MAGIC, op0=AluOpType.add,
            op1=AluOpType.subtract))())
    V_XQ = c0.get("v")

    _cm_sb.__exit__(None, None, None)
    _cm_sq.__exit__(None, None, None)
    _cm_s32.__exit__(None, None, None)
    _cm_xb.__exit__(None, None, None)
    _cm_wf = nc.sbuf_tensor(tg + "wf", [128, 2 * NGS * 1024], F32); wf = _cm_wf.__enter__()
    _cm_wq = nc.sbuf_tensor(tg + "wq", [128, 2 * NGS * 1024], BF16); wq1 = _cm_wq.__enter__()
    _cm_wt = nc.sbuf_tensor(tg + "wt", [128, 1024], F32); wtmp = _cm_wt.__enter__()
    _cm_hb = nc.sbuf_tensor(tg + "hb", [128, 8 * TC], BF16); hb = _cm_hb.__enter__()

    marks = {}
    W0 = {0: c0.get("w0"), 1: c0.get("w1")}

    def gp_loads(gpsimd):
        gpsimd.wait_ge(sems["v"], V_XQ)
        gpsimd.wait_ge(sems["st"], ST0_)
        gpsimd.wait_ge(sems["sg"], SG0_)
        for p in range(2):
            for gq in range(NG):
                hf = gq % 2
                if p == 0 and gq >= 2:
                    gpsimd.wait_ge(sems["v"], marks["absum"][gq - 2])
                elif p == 1 and gq < 2:
                    gpsimd.wait_ge(sems["v"], marks["absum"][NG - 2 + gq])
                elif p == 1:
                    gpsimd.wait_ge(sems["v"], marks["tern"][gq - 2])
                sem_h = sems["w0"] if hf == 0 else sems["w1"]
                for cc in range(NGS):
                    gpsimd.dma_start(
                        out=wf[:, (hf * NGS + cc) * 1024:(hf * NGS + cc + 1) * 1024],
                        in_=I.w1T[cc * 128:(cc + 1) * 128, gq * 1024:(gq + 1) * 1024]
                    ).then_inc(sem_h, 16)
                    c0.inc("w%d" % hf, 16)

    marks["absum"] = emit_wquant_pass1(pr, c0, wf, wsum, wcol, msc, wsc128,
                                       K.ocf, K.oc1f, psC, NGS, NG,
                                       4096.0 * 1024.0, W0)
    pr.run("v", lambda v: v.tensor_scalar(out=rrow[:, :], in0=mrow[:, :],
                                          scalar1=msc[:, 1:2], scalar2=1.0 / 127.0,
                                          op0=AluOpType.mult, op1=AluOpType.mult))
    pr.wait("t", "v", c0.get("v"))
    emit_row_bcast(pr, lambda tc_i: rrow[:, tc_i * 512:(tc_i + 1) * 512],
                   bcR, K.oc1f, psC)

    marks["tern"] = []
    store_ready = []
    mm_last = {}
    ST0 = c0.get("st")
    for gq in range(NG):
        hf = gq % 2
        wqp = gq % 2
        pr.wait("v", "w%d" % hf, W0[hf] + (NG // 2 + gq // 2 + 1) * NGS * 16)
        if gq > 1:
            pr.wait("v", "t", mm_last[gq - 2])
        tern_v = emit_ternarize_group(
            pr, wf,
            lambda cc, wqp=wqp: wq1[:, (wqp * NGS + cc) * 1024:(wqp * NGS + cc + 1) * 1024],
            wsc128, hf, NGS, wtmp)
        marks["tern"].append(tern_v)
        if gq > 0:
            pr.wait("t", "st", ST0 + 128 * gq)
        pr.wait("t", "v", tern_v)
        evac_at = []
        psRot = [psA, psB, psC, psD]
        gi = 0
        for hc in range(8):
            for tc_i in range(4):
                ps = psRot[gi % 4]
                if gi >= 4:
                    pr.wait("t", "v", evac_at[gi - 4])
                def mk(hc=hc, tc_i=tc_i, ps=ps, wqp=wqp):
                    def f(tensor):
                        mm = None
                        for ic in range(8):
                            mm = tensor.matmul(
                                ps[:, :],
                                wq1[:, (wqp * NGS + ic) * 1024 + hc * 128:
                                    (wqp * NGS + ic) * 1024 + (hc + 1) * 128],
                                xq[:, ic * TC + tc_i * 512: ic * TC + (tc_i + 1) * 512],
                                start=(ic == 0), stop=(ic == 7))
                        return mm
                    return f
                td = pr.run("t", mk())
                pr.wait("v", "t", td)
                pr.run("v", (lambda hc=hc, tc_i=tc_i, ps=ps: lambda v: v.tensor_tensor(
                    out=hb[:, hc * TC + tc_i * 512: hc * TC + (tc_i + 1) * 512],
                    in0=ps[:, :], in1=bcR[:, tc_i * 512:(tc_i + 1) * 512],
                    op=AluOpType.mult))())
                evac_at.append(c0.get("v"))
                gi += 1
        mm_last[gq] = c0.get("t")
        store_ready.append(c0.get("v"))

    def p_hstores(sync):
        for gq in range(NG):
            sync.wait_ge(sems["v"], store_ready[gq])
            for hc in range(8):
                sync.dma_start(
                    out=S.hD[(gq * 8 + hc) * 128:(gq * 8 + hc + 1) * 128, t0:t0 + TC],
                    in_=hb[:, hc * TC:(hc + 1) * TC]).then_inc(sems["st"], 16)
                c0.inc("st", 16)
    block.sync(p_hstores)
    block.gpsimd(gp_loads)
    pr.barrier()
    pr.wait("v", "st", c0.get("st"))
    pr.wait("t", "st", c0.get("st"))
    _cm_hb.__exit__(None, None, None)
    _cm_wt.__exit__(None, None, None)
    _cm_wq.__exit__(None, None, None)
    _cm_wf.__exit__(None, None, None)
    _p.close()



def emit_w2quant(P):
    nc, pr, c0, sems, block = P["nc"], P["pr"], P["c0"], P["sems"], P["block"]
    K, I = P["K"], P["I"]
    psA, psB, psC, psD, psE, psF = P["PS"]
    NG, NGS = 4, 8
    _w2q = nc.sbuf_tensor("w2q", [128, 32 * 1024], BF16)
    w2q = _w2q.__enter__()
    P["w2qcm"] = _w2q
    P["w2q"] = w2q
    msc2 = nc.sbuf_tensor("msc2", [1, 4], F32)
    P["msc2cm"] = msc2
    P["msc2"] = msc2.__enter__()
    _p = ExitStack()
    wf = _p.enter_context(nc.sbuf_tensor("w2f", [128, 2 * NGS * 1024], F32))
    wtmp = _p.enter_context(nc.sbuf_tensor("w2t", [128, 1024], F32))
    wsum = _p.enter_context(nc.sbuf_tensor("w2s", [128, 1], F32))
    wcol = _p.enter_context(nc.sbuf_tensor("w2c", [128, 1], F32))
    wsc128 = _p.enter_context(nc.sbuf_tensor("w2k", [128, 1], F32))
    marks = {}
    W0 = {0: c0.get("w0"), 1: c0.get("w1")}
    V0, T0_ = c0.get("v"), c0.get("t")
    ST0_, SG0_ = c0.get("st"), c0.get("sg")

    def gp_loads(gpsimd):
        gpsimd.wait_ge(sems["v"], V0)
        gpsimd.wait_ge(sems["t"], T0_)
        gpsimd.wait_ge(sems["st"], ST0_)
        gpsimd.wait_ge(sems["sg"], SG0_)
        for p in range(2):
            for gq in range(NG):
                hf = gq % 2
                if p == 0 and gq >= 2:
                    gpsimd.wait_ge(sems["v"], marks["absum"][gq - 2])
                elif p == 1 and gq < 2:
                    gpsimd.wait_ge(sems["v"], marks["absum"][NG - 2 + gq])
                elif p == 1:
                    gpsimd.wait_ge(sems["v"], marks["tern"][gq - 2])
                sem_h = sems["w0"] if hf == 0 else sems["w1"]
                for cc in range(NGS):
                    gpsimd.dma_start(
                        out=wf[:, (hf * NGS + cc) * 1024:(hf * NGS + cc + 1) * 1024],
                        in_=I.w2T[(gq * NGS + cc) * 128:(gq * NGS + cc + 1) * 128, :]
                    ).then_inc(sem_h, 16)
                    c0.inc("w%d" % hf, 16)

    marks["absum"] = emit_wquant_pass1(pr, c0, wf, wsum, wcol, P["msc2"],
                                       wsc128, K.ocf, K.oc1f, psC, NGS, NG,
                                       4096.0 * 1024.0, W0)
    marks["tern"] = []
    for gq in range(NG):
        hf = gq % 2
        pr.wait("v", "w%d" % hf, W0[hf] + (NG // 2 + gq // 2 + 1) * NGS * 16)
        tern_v = emit_ternarize_group(
            pr, wf,
            lambda cc, gq=gq: w2q[:, (gq * NGS + cc) * 1024:(gq * NGS + cc + 1) * 1024],
            wsc128, hf, NGS, wtmp)
        marks["tern"].append(tern_v)
    block.gpsimd(gp_loads)
    pr.barrier()
    _p.close()



def emit_conv_half(P, half):
    nc, pr, c0, sems, block = P["nc"], P["pr"], P["c0"], P["sems"], P["block"]
    K, I, S = P["K"], P["I"], P["S"]
    psA, psB, psC, psD, psE, psF = P["PS"]
    t0 = half * TC
    tg = f"cv{half}"
    _pm = ExitStack()
    mrow = _pm.enter_context(nc.sbuf_tensor(tg + "mr", [1, TC], F32))
    bcX = _pm.enter_context(nc.sbuf_tensor(tg + "bX", [128, TC], F32))
    P[f"mrow_cv{half}"] = mrow
    P[f"bcX_cv{half}"] = bcX
    P[f"mrow_cv{half}_cm"] = _pm
    _p = ExitStack()
    cwcol = _p.enter_context(nc.sbuf_tensor(tg + "cw", [128, 96], F32))
    cbcol = _p.enter_context(nc.sbuf_tensor(tg + "cb", [128, 32], F32))
    heb = _p.enter_context(nc.sbuf_tensor(tg + "he", [128, 2 * (TC + 2)], BF16))
    tb0 = _p.enter_context(nc.sbuf_tensor(tg + "t0", [128, TC], F32))
    tb1 = _p.enter_context(nc.sbuf_tensor(tg + "t1", [128, TC], F32))
    csb = _p.enter_context(nc.sbuf_tensor(tg + "cs", [128, 2 * TC], BF16))
    am = _p.enter_context(nc.sbuf_tensor(tg + "am", [128, TC], F32))
    tp = _p.enter_context(nc.sbuf_tensor(tg + "tp", [128, TC], F32))

    V0, T0_, S0 = c0.get("v"), c0.get("t"), c0.get("s")
    ST0_, SG0_ = c0.get("st"), c0.get("sg")

    def p_loads0(sync):
        sync.wait_ge(sems["v"], V0)
        sync.wait_ge(sems["t"], T0_)
        sync.wait_ge(sems["st"], ST0_)
        sync.wait_ge(sems["sg"], SG0_)
        for hc in range(32):
            sync.dma_start(out=cwcol[:, 3 * hc:3 * (hc + 1)],
                           in_=I.cw[hc * 128:(hc + 1) * 128, :]).then_inc(sems["dh"], 16)
            sync.dma_start(out=cbcol[:, hc:hc + 1],
                           in_=I.cb[hc * 128:(hc + 1) * 128].rearrange("(p o) -> p o", o=1)
                           ).then_inc(sems["dh"], 16)
            c0.inc("dh", 32)
    block.sync(p_loads0)
    DH0 = c0.get("dh")

    conv_done = {}
    silu_done = {}
    quant_done = {}
    pr.run("v", lambda v: v.memset(am[:, :], 0.0))
    if half == 0:
        pr.run("v", lambda v: v.memset(heb[:, 0:1], 0.0))
        pr.run("v", lambda v: v.memset(heb[:, (TC + 2):(TC + 3)], 0.0))
    else:
        pr.run("v", lambda v: v.memset(heb[:, TC + 1:TC + 2], 0.0))
        pr.run("v", lambda v: v.memset(heb[:, 2 * TC + 3:2 * TC + 4], 0.0))
    V_EDGE = c0.get("v")
    pr.wait("v", "dh", DH0)
    pr.wait("s", "dh", DH0)
    HB0 = {0: c0.get("h0"), 1: c0.get("h1")}
    CB0 = {0: c0.get("c0"), 1: c0.get("c1")}

    def p_hloads(sync):
        sync.wait_ge(sems["v"], V_EDGE)
        for hc in range(32):
            par = hc % 2
            if hc >= 2:
                sync.wait_ge(sems["v"], conv_done[hc - 2])
            base = par * (TC + 2)
            sync.dma_start(out=heb[:, base + 1:base + 1 + TC],
                           in_=S.hD[hc * 128:(hc + 1) * 128, t0:t0 + TC]
                           ).then_inc(sems["h0"] if par == 0 else sems["h1"], 16)
            c0.inc("h%d" % par, 16)
            if half == 0:
                sync.dma_start(out=heb[:, base + TC + 1:base + TC + 2],
                               in_=S.hD[hc * 128:(hc + 1) * 128, TC:TC + 1],
                               allow_slow_non_contiguous=True
                               ).then_inc(sems["h0"] if par == 0 else sems["h1"], 16)
            else:
                sync.dma_start(out=heb[:, base:base + 1],
                               in_=S.hD[hc * 128:(hc + 1) * 128, TC - 1:TC],
                               allow_slow_non_contiguous=True
                               ).then_inc(sems["h0"] if par == 0 else sems["h1"], 16)
            c0.inc("h%d" % par, 16)
            sync.wait_ge(sems["s"], silu_done[hc])
            sync.dma_start(out=S.csx[hc * 128:(hc + 1) * 128, t0:t0 + TC],
                           in_=csb[:, par * TC:(par + 1) * TC]
                           ).then_inc(sems["c0"] if par == 0 else sems["c1"], 16)
            c0.inc("c%d" % par, 16)

    SP0 = c0.get("sp")
    for hc in range(32):
        par = hc % 2
        he = heb[:, par * (TC + 2):(par + 1) * (TC + 2)]
        pr.wait("v", "h%d" % par, HB0[par] + (hc // 2 + 1) * 32)
        pr.run("v", (lambda he=he, hc=hc: lambda v: v.tensor_scalar(
            out=tb0[:, :], in0=he[:, 0:TC], scalar1=cwcol[:, 3 * hc:3 * hc + 1],
            scalar2=cbcol[:, hc:hc + 1], op0=AluOpType.mult, op1=AluOpType.add))())
        pr.run("v", (lambda he=he, hc=hc: lambda v: v.scalar_tensor_tensor(
            out=tb1[:, :], in0=he[:, 1:TC + 1], scalar=cwcol[:, 3 * hc + 1:3 * hc + 2],
            in1=tb0[:, :], op0=AluOpType.mult, op1=AluOpType.add))())
        vd = pr.run("v", (lambda he=he, hc=hc: lambda v: v.scalar_tensor_tensor(
            out=tb0[:, :], in0=he[:, 2:TC + 2], scalar=cwcol[:, 3 * hc + 2:3 * hc + 3],
            in1=tb1[:, :], op0=AluOpType.mult, op1=AluOpType.add))())
        conv_done[hc] = vd
        pr.wait("s", "v", vd)
        if hc >= 2:
            pr.wait("s", "c%d" % par, CB0[par] + (hc // 2) * 16)
        sd = pr.run("s", (lambda par=par: lambda s: s.activation(
            csb[:, par * TC:(par + 1) * TC], tb0[:, :], AF.Silu, scale=1.0))())
        silu_done[hc] = sd
        pr.wait("v", "s", sd)
        pr.run("v", (lambda par=par: lambda v: v.tensor_tensor(
            out=tb1[:, :], in0=csb[:, par * TC:(par + 1) * TC],
            in1=csb[:, par * TC:(par + 1) * TC], op=AluOpType.mult))())
        pr.run("v", lambda v: v.tensor_tensor(out=am[:, :], in0=am[:, :],
                                              in1=tb1[:, :], op=AluOpType.max))
    block.sync(p_hloads)

    _q = ExitStack()
    scb = _q.enter_context(nc.sbuf_tensor(tg + "sb", [128, TC + 16], BF16))
    psT = _q.enter_context(nc.psum_tensor(tg + "pT", [128, 1024], BF16))
    psTR = _q.enter_context(nc.psum_tensor(tg + "pR", [128, 1024], BF16))
    emit_absmax_row(pr, lambda ch: am[:, :], 1, tb0, scb, psT, psTR, mrow, K.i128)
    pr.run("v", lambda v: v.tensor_scalar(out=mrow[:, :], in0=mrow[:, :],
                                          scalar1=1e-5, scalar2=None,
                                          op0=AluOpType.max))
    pr.run("v", lambda v: v.reciprocal(out=bcX[0:1, 0:TC] if False else tb0[0:1, 0:TC],
                                       in_=mrow[:, :]))
    pr.wait("t", "v", c0.get("v"))
    emit_row_bcast(pr, lambda tc_i: tb0[0:1, tc_i * 512:(tc_i + 1) * 512],
                   bcX, K.c127, psC)

    pr.barrier()
    pr.wait("v", "c0", c0.get("c0"))
    pr.wait("v", "c1", c0.get("c1"))
    pr.wait("t", "c0", c0.get("c0"))
    pr.wait("t", "c1", c0.get("c1"))
    _q.close()
    _p.close()



def emit_bitlin2_ln_half(P, half):
    nc, pr, c0, sems, block = P["nc"], P["pr"], P["c0"], P["sems"], P["block"]
    K, I, S = P["K"], P["I"], P["S"]
    psA, psB, psC, psD, psE, psF = P["PS"]
    w2q = P["w2q"]
    msc2 = P["msc2"]
    mrow = P[f"mrow_cv{half}"]
    t0 = half * TC
    tg = f"b2{half}"
    _cm_xb = nc.sbuf_tensor(tg + "xb", [128, 8 * TC], F32); xb = _cm_xb.__enter__()
    _p = ExitStack()
    rrow = _p.enter_context(nc.sbuf_tensor(tg + "rr", [1, TC], F32))
    bcR = _p.enter_context(nc.sbuf_tensor(tg + "bR", [128, TC], F32))
    xqt = _p.enter_context(nc.sbuf_tensor(tg + "xq", [128, 32 * 512], BF16))
    sevac = _p.enter_context(nc.sbuf_tensor(tg + "se", [128, 512], F32))
    tq = _p.enter_context(nc.sbuf_tensor(tg + "tq", [128, 512], F32))
    bcX = P[f"bcX_cv{half}"]

    pr.run("v", lambda v: v.tensor_scalar(out=rrow[:, :], in0=mrow[:, :],
                                          scalar1=msc2[:, 1:2], scalar2=1.0 / 127.0,
                                          op0=AluOpType.mult, op1=AluOpType.mult))
    pr.wait("t", "v", c0.get("v"))
    emit_row_bcast(pr, lambda tc_i: rrow[:, tc_i * 512:(tc_i + 1) * 512],
                   bcR, K.oc1f, psC)

    V0, T0_ = c0.get("v"), c0.get("t")
    ST0_, SG0_ = c0.get("st"), c0.get("sg")
    C0B = {0: c0.get("c0"), 1: c0.get("c1")}
    DH0 = c0.get("dh")

    def p_xloads(sync):
        sync.wait_ge(sems["v"], V0)
        sync.wait_ge(sems["t"], T0_)
        sync.wait_ge(sems["st"], ST0_)
        sync.wait_ge(sems["sg"], SG0_)
        sync.wait_ge(sems["c0"], C0B[0])
        sync.wait_ge(sems["c1"], C0B[1])
        for cc in range(8):
            sync.dma_start(out=xb[:, cc * TC:(cc + 1) * TC],
                           in_=S.x1park[cc * 128:(cc + 1) * 128, t0:t0 + TC]
                           ).then_inc(sems["dh"], 16)
            c0.inc("dh", 16)
    block.sync(p_xloads)
    DH1 = c0.get("dh")

    mm_done_tc = {}
    HB = {0: c0.get("h0"), 1: c0.get("h1")}

    def p_xqloads(sync):
        for tc_i in range(4):
            par = tc_i % 2
            if tc_i >= 1:
                sync.wait_ge(sems["t"], mm_done_tc[tc_i - 1])
            for hc in range(32):
                sync.dma_start(
                    out=xqt[:, hc * 512:(hc + 1) * 512],
                    in_=S.csx[hc * 128:(hc + 1) * 128, t0 + tc_i * 512:t0 + (tc_i + 1) * 512]
                ).then_inc(sems["h0"] if par == 0 else sems["h1"], 16)
                c0.inc("h%d" % par, 16)

    rbuf = [xb[:, oc * TC:(oc + 1) * TC] for oc in range(8)]
    evac_marks = []
    quant_tc = {}
    psRot = [psA, psB, psC, psD]

    def emit_quant_tc(tc_i):
        par = tc_i % 2
        pr.wait("v", "h%d" % par, HB[par] + (tc_i // 2 + 1) * 32 * 16)
        for hc in range(32):
            sl = xqt[:, hc * 512:(hc + 1) * 512]
            pr.run("v", (lambda sl=sl, tc_i=tc_i: lambda v: v.tensor_tensor(
                out=tq[:, :], in0=sl,
                in1=bcX[:, tc_i * 512:(tc_i + 1) * 512], op=AluOpType.mult))())
            pr.run("v", (lambda sl=sl: lambda v: v.tensor_scalar(
                out=sl, in0=tq[:, :], scalar1=MAGIC, scalar2=MAGIC,
                op0=AluOpType.add, op1=AluOpType.subtract))())
        quant_tc[tc_i] = c0.get("v")

    def emit_mm_tc(tc_i):
        par = tc_i % 2
        pr.wait("t", "v", quant_tc[tc_i])
        pr.wait("t", "dh", DH1)
        for oc in range(8):
            gi = tc_i * 8 + oc
            ps = psRot[gi % 4]
            if gi >= 4:
                pr.wait("t", "v", evac_marks[gi - 4])
            def mkmm(oc=oc, ps=ps, par=par):
                def f(tensor):
                    mm = None
                    for hc in range(32):
                        mm = tensor.matmul(
                            ps[:, :],
                            w2q[:, hc * 1024 + oc * 128: hc * 1024 + (oc + 1) * 128],
                            xqt[:, hc * 512:(hc + 1) * 512],
                            start=(hc == 0), stop=(hc == 31))
                    return mm
                return f
            td = pr.run("t", mkmm())
            pr.wait("v", "t", td)
            pr.run("v", (lambda ps=ps, tc_i=tc_i: lambda v: v.tensor_tensor(
                out=sevac[:, :], in0=ps[:, :], in1=bcR[:, tc_i * 512:(tc_i + 1) * 512],
                op=AluOpType.mult))())
            vd = pr.run("v", (lambda oc=oc, tc_i=tc_i: lambda v: v.tensor_tensor(
                out=xb[:, oc * TC + tc_i * 512: oc * TC + (tc_i + 1) * 512],
                in0=sevac[:, :],
                in1=xb[:, oc * TC + tc_i * 512: oc * TC + (tc_i + 1) * 512],
                op=AluOpType.add))())
            evac_marks.append(vd)
        mm_done_tc[tc_i] = c0.get("t")

    for tc_i in range(4):
        emit_quant_tc(tc_i)
        emit_mm_tc(tc_i)
    block.sync(p_xqloads)
    pr.barrier()
    _p.close()

    _cm_scr = nc.sbuf_tensor(tg + "sc", [128, 4 * 512], BF16); scr = _cm_scr.__enter__()
    _cm_rw = nc.sbuf_tensor(tg + "rw", [1, 3 * TC], F32); rows_b = _cm_rw.__enter__()
    rows = [rows_b[:, 0:TC], rows_b[:, TC:2 * TC], rows_b[:, 2 * TC:3 * TC],
            rows_b[:, 2 * TC:3 * TC]]
    _cm_bm = nc.sbuf_tensor(tg + "bm", [128, TC], F32); bcm = _cm_bm.__enter__()
    _cm_bs = nc.sbuf_tensor(tg + "bs", [128, TC], F32); bcs = _cm_bs.__enter__()
    _cm_xo = nc.sbuf_tensor(tg + "xo", [128, TC], F32); xo2 = _cm_xo.__enter__()
    SG0 = c0.get("sg")

    def chunk_prewait(ch):
        if ch >= 1:
            return ("sg", SG0 + 16 * ch)
        return None
    ready = emit_ln_tail(pr, rbuf, scr, rows, bcm, bcs,
                         lambda ch: xo2[:, :],
                         psC, psD, psE, K.ocb, K.oc1f, K.epsb,
                         pre_wait=("t", c0.get("t")),
                         chunk_prewait=chunk_prewait)

    def p_xstores(gpsimd):
        for ch in range(8):
            gpsimd.wait_ge(sems["v"], ready[ch])
            gpsimd.dma_start(out=S.x2park[ch * 128:(ch + 1) * 128, t0:t0 + TC],
                             in_=xo2[:, :]).then_inc(sems["sg"], 16)
            c0.inc("sg", 16)
    block.gpsimd(p_xstores)
    pr.barrier()
    pr.wait("v", "sg", c0.get("sg"))
    pr.wait("t", "sg", c0.get("sg"))
    _cm_xo.__exit__(None, None, None)
    _cm_bs.__exit__(None, None, None)
    _cm_bm.__exit__(None, None, None)
    _cm_rw.__exit__(None, None, None)
    _cm_scr.__exit__(None, None, None)
    _cm_xb.__exit__(None, None, None)
    P[f"mrow_cv{half}_cm"].close()



def emit_l1_qkv_half(P, half):
    nc, pr, c0, sems, block = P["nc"], P["pr"], P["c0"], P["sems"], P["block"]
    K, I, S = P["K"], P["I"], P["S"]
    psA, psB, psC, psD, psE, psF = P["PS"]
    t0 = half * TC
    tg = f"l1q{half}"
    _p = ExitStack()
    xb = _p.enter_context(nc.sbuf_tensor(tg + "xb", [128, 8 * TC], BF16))
    wb = _p.enter_context(nc.sbuf_tensor(tg + "wb", [128, 8 * 3 * D], BF16))
    qs = _p.enter_context(nc.sbuf_tensor(tg + "qs", [128, 8 * TC], BF16))
    ksb = _p.enter_context(nc.sbuf_tensor(tg + "ks", [128, 8 * TC], BF16))
    vs = _p.enter_context(nc.sbuf_tensor(tg + "vs", [128, 16 * D], BF16))
    V0, T0_ = c0.get("v"), c0.get("t")
    ST0_, SG0_ = c0.get("st"), c0.get("sg")

    def p_loads(gpsimd):
        gpsimd.wait_ge(sems["v"], V0)
        gpsimd.wait_ge(sems["t"], T0_)
        gpsimd.wait_ge(sems["st"], ST0_)
        gpsimd.wait_ge(sems["sg"], SG0_)
        for cc in range(8):
            gpsimd.dma_start(out=xb[:, cc * TC:(cc + 1) * TC],
                             in_=S.x2park[cc * 128:(cc + 1) * 128, t0:t0 + TC]
                             ).then_inc(sems["ds"], 16)
            c0.inc("ds", 16)
        for cc in range(8):
            gpsimd.dma_start(out=wb[:, cc * 3 * D:(cc + 1) * 3 * D],
                             in_=I.wqkv[cc * 128:(cc + 1) * 128, :]).then_inc(sems["ds"], 16)
            c0.inc("ds", 16)
    block.gpsimd(p_loads)
    DS0 = c0.get("ds")

    pr.wait("t", "ds", DS0)
    evs = []
    psRot = [psA, psB, psC, psD]
    gi = 0
    for which, outb in ((0, qs), (1, ksb)):
        for oc in range(8):
            for tci in range(4):
                ps = psRot[gi % 4]
                if gi >= 4:
                    pr.wait("t", "v", evs[gi - 4])
                def mk(which=which, oc=oc, tci=tci, ps=ps):
                    def f(tensor):
                        mm = None
                        for ic in range(8):
                            off = ic * 3 * D + which * D + oc * 128
                            mm = tensor.matmul(
                                ps[:, :], wb[:, off:off + 128],
                                xb[:, ic * TC + tci * 512: ic * TC + (tci + 1) * 512],
                                start=(ic == 0), stop=(ic == 7))
                        return mm
                    return f
                td = pr.run("t", mk())
                pr.wait("v", "t", td)
                pr.run("v", (lambda outb=outb, oc=oc, tci=tci, ps=ps: lambda v:
                             v.tensor_copy(
                                 out=outb[:, oc * TC + tci * 512: oc * TC + (tci + 1) * 512],
                                 in_=ps[:, :]))())
                evs.append(c0.get("v"))
                gi += 1
    for tt in range(16):
        for vh in range(2):
            ps = psRot[gi % 4]
            if gi >= 4:
                pr.wait("t", "v", evs[gi - 4])
            def mkv(tt=tt, vh=vh, ps=ps):
                def f(tensor):
                    mm = None
                    for ic in range(8):
                        off = ic * 3 * D + 2 * D + vh * 512
                        mm = tensor.matmul(
                            ps[:, :],
                            xb[:, ic * TC + tt * 128: ic * TC + (tt + 1) * 128],
                            wb[:, off:off + 512],
                            start=(ic == 0), stop=(ic == 7))
                    return mm
                return f
            td = pr.run("t", mkv())
            pr.wait("v", "t", td)
            pr.run("v", (lambda tt=tt, vh=vh, ps=ps: lambda v: v.tensor_copy(
                out=vs[:, tt * D + vh * 512: tt * D + (vh + 1) * 512],
                in_=ps[:, :]))())
            evs.append(c0.get("v"))
            gi += 1
    V_ALL = c0.get("v")

    def p_stores(sync):
        sync.wait_ge(sems["v"], V_ALL)
        for cc in range(8):
            sync.dma_start(out=S.qd1[cc * 128:(cc + 1) * 128, t0:t0 + TC],
                           in_=qs[:, cc * TC:(cc + 1) * TC]).then_inc(sems["st"], 16)
            sync.dma_start(out=S.kd1[cc * 128:(cc + 1) * 128, t0:t0 + TC],
                           in_=ksb[:, cc * TC:(cc + 1) * TC]).then_inc(sems["st"], 16)
            c0.inc("st", 32)
        for tt in range(16):
            for hh in range(NH):
                sync.dma_start(
                    out=S.vd1[hh * T + t0 + tt * 128: hh * T + t0 + (tt + 1) * 128, :],
                    in_=vs[:, tt * D + hh * HD: tt * D + (hh + 1) * HD]
                ).then_inc(sems["st"], 16)
                c0.inc("st", 16)
    block.sync(p_stores)
    pr.barrier()
    pr.wait("v", "st", c0.get("st"))
    pr.wait("t", "st", c0.get("st"))
    _p.close()



def emit_l1_ffn_half(P, half):
    nc, pr, c0, sems, block = P["nc"], P["pr"], P["c0"], P["sems"], P["block"]
    K, I, S = P["K"], P["I"], P["S"]
    psA, psB, psE_, psH_, psC_, psSum_ = P["PS"][0], P["PS"][1], P["PS"][2], P["PS"][3], P["PS"][4], P["PS"][5]
    psE = P["PS"][2]
    psH = P["PS"][3]
    psC = P["PS"][4]
    psSum = P["PS"][4]
    psSq = P["PS"][5]
    H, M, NE, EH = 2048, 64, 8, 1024
    t0 = half * TC
    tg = f"lf{half}"
    _p = ExitStack()
    bencc = _p.enter_context(nc.sbuf_tensor(tg + "be", [128, 16], F32))
    bs2cc = _p.enter_context(nc.sbuf_tensor(tg + "b2", [64, 1], F32))
    bc2sc = _p.enter_context(nc.sbuf_tensor(tg + "bc", [128, 16], F32))
    lgb = _p.enter_context(nc.sbuf_tensor(tg + "lg", [128, 16], F32))
    bdrow = _p.enter_context(nc.sbuf_tensor(tg + "bd", [1, D], BF16))
    wrb = _p.enter_context(nc.sbuf_tensor(tg + "wr", [64, NE], BF16))
    brr = _p.enter_context(nc.sbuf_tensor(tg + "br", [1, NE], BF16))
    rtc = _p.enter_context(nc.sbuf_tensor(tg + "rt", [128, 8], F32))
    grt = _p.enter_context(nc.sbuf_tensor(tg + "gr8", [128, 16], F32))
    gbt = _p.enter_context(nc.sbuf_tensor(tg + "gb", [128, 8], BF16))
    gtmp = _p.enter_context(nc.sbuf_tensor(tg + "gt", [8, 128], BF16))
    ws2cb = _p.enter_context(nc.sbuf_tensor(tg + "w2c", [128, 16 * 64], BF16))
    wc2sb = _p.enter_context(nc.sbuf_tensor(tg + "wcs", [64, H], BF16))
    wdecb = _p.enter_context(nc.sbuf_tensor(tg + "wd", [128, 16 * 1024], BF16))
    eo = _p.enter_context(nc.sbuf_tensor(tg + "eo", [64, TC], BF16))
    psG = _p.enter_context(nc.psum_tensor(tg + "pG", [128, 1024], BF16))
    psY = _p.enter_context(nc.psum_tensor(tg + "pY", [128, 512], F32))

    _cm_contb = nc.sbuf_tensor(tg + "ct", [64, TC], BF16); contb = _cm_contb.__enter__()
    _cm_gr = nc.sbuf_tensor(tg + "gr", [1, NE * TC], BF16); gr = _cm_gr.__enter__()
    _cm_xbb = nc.sbuf_tensor(tg + "xbb", [128, 8 * TC], BF16); xbb = _cm_xbb.__enter__()
    _cm_wen = nc.sbuf_tensor(tg + "we", [128, 8 * H], BF16); wencb = _cm_wen.__enter__()
    _cm_spk = nc.sbuf_tensor(tg + "sp", [128, 2 * 512], BF16); spkb = _cm_spk.__enter__()
    _cm_w1b = nc.sbuf_tensor(tg + "w1b", [64, NE * EH], BF16); w1b = _cm_w1b.__enter__()
    _cm_b1r = nc.sbuf_tensor(tg + "b1r", [1, NE * EH], BF16); b1r = _cm_b1r.__enter__()
    _cm_w2b = nc.sbuf_tensor(tg + "w2b", [128, NE * 8 * M], BF16); w2b = _cm_w2b.__enter__()
    _cm_b2r = nc.sbuf_tensor(tg + "b2r", [1, NE * M], BF16); b2r = _cm_b2r.__enter__()
    _cm_cg = nc.sbuf_tensor(tg + "cg", [64, 512], BF16); cg = _cm_cg.__enter__()
    _cm_hsb = nc.sbuf_tensor(tg + "hs", [128, 2 * 512], BF16); hsb = _cm_hsb.__enter__()

    V0, T0_ = c0.get("v"), c0.get("t")
    ST0_, SG0_ = c0.get("st"), c0.get("sg")

    def p_loads(sync):
        sync.wait_ge(sems["v"], V0)
        sync.wait_ge(sems["t"], T0_)
        sync.wait_ge(sems["st"], ST0_)
        sync.wait_ge(sems["sg"], SG0_)
        for hc in range(16):
            sync.dma_start(out=bencc[:, hc:hc + 1],
                           in_=I.benc[hc * 128:(hc + 1) * 128].rearrange("(p o) -> p o", o=1)
                           ).then_inc(sems["dh"], 16)
            sync.dma_start(out=bc2sc[:, hc:hc + 1],
                           in_=I.bc2s[hc * 128:(hc + 1) * 128].rearrange("(p o) -> p o", o=1)
                           ).then_inc(sems["dh"], 16)
            c0.inc("dh", 32)
        sync.dma_start(out=bs2cc[:, :],
                       in_=I.bs2c.rearrange("(p o) -> p o", o=1)).then_inc(sems["dh"], 16)
        c0.inc("dh", 16)
        for cc in range(8):
            sync.dma_start(out=lgb[:, cc:cc + 1],
                           in_=I.lng[cc * 128:(cc + 1) * 128].rearrange("(p o) -> p o", o=1)
                           ).then_inc(sems["dh"], 16)
            sync.dma_start(out=lgb[:, 8 + cc:9 + cc],
                           in_=I.lnb[cc * 128:(cc + 1) * 128].rearrange("(p o) -> p o", o=1)
                           ).then_inc(sems["dh"], 16)
            c0.inc("dh", 32)
    block.sync(p_loads)

    def p_gploads(gpsimd):
        gpsimd.wait_ge(sems["v"], V0)
        gpsimd.wait_ge(sems["t"], T0_)
        gpsimd.wait_ge(sems["st"], ST0_)
        gpsimd.wait_ge(sems["sg"], SG0_)
        for cc in range(8):
            gpsimd.dma_start(out=xbb[:, cc * TC:(cc + 1) * TC],
                             in_=S.x3park[cc * 128:(cc + 1) * 128, t0:t0 + TC]
                             ).then_inc(sems["ds"], 16)
            gpsimd.dma_start(out=wencb[:, cc * H:(cc + 1) * H],
                             in_=I.wenc[cc * 128:(cc + 1) * 128, :]).then_inc(sems["ds"], 16)
            c0.inc("ds", 32)
        for hc in range(16):
            gpsimd.dma_start(out=ws2cb[:, hc * 64:(hc + 1) * 64],
                             in_=I.ws2c[hc * 128:(hc + 1) * 128, :]).then_inc(sems["ds"], 16)
            gpsimd.dma_start(out=wdecb[:, hc * 1024:(hc + 1) * 1024],
                             in_=I.wdec[hc * 128:(hc + 1) * 128, :]).then_inc(sems["ds"], 16)
            c0.inc("ds", 32)
        gpsimd.dma_start(out=wc2sb[:, :], in_=I.wc2s[:, :]).then_inc(sems["ds"], 16)
        gpsimd.dma_start(out=wrb[:, :], in_=I.wr[:, :]).then_inc(sems["ds"], 16)
        gpsimd.dma_start(out=brr[:, :], in_=I.br.rearrange("(o e) -> o e", o=1)
                         ).then_inc(sems["ds"], 16)
        gpsimd.dma_start(out=bdrow[:, :], in_=I.bdec.rearrange("(o d) -> o d", o=1)
                         ).then_inc(sems["ds"], 16)
        c0.inc("ds", 64)
        for e in range(NE):
            gpsimd.dma_start(out=w1b[:, e * EH:(e + 1) * EH],
                             in_=I.w1e[e, :, :]).then_inc(sems["ds"], 16)
            gpsimd.dma_start(out=b1r[:, e * EH:(e + 1) * EH],
                             in_=I.b1e[e, :].rearrange("(o h) -> o h", o=1)
                             ).then_inc(sems["ds"], 16)
            gpsimd.dma_start(out=b2r[:, e * M:(e + 1) * M],
                             in_=I.b2e[e, :].rearrange("(o m) -> o m", o=1)
                             ).then_inc(sems["ds"], 16)
            c0.inc("ds", 48)
            for hc8 in range(8):
                gpsimd.dma_start(out=w2b[:, (e * 8 + hc8) * M:(e * 8 + hc8 + 1) * M],
                                 in_=I.w2e[e, hc8 * 128:(hc8 + 1) * 128, :]
                                 ).then_inc(sems["ds"], 16)
                c0.inc("ds", 16)
    block.gpsimd(p_gploads)
    DH0 = c0.get("dh")
    DS0 = c0.get("ds")
    for eng in ("t", "s", "v"):
        pr.wait(eng, "dh", DH0)
        pr.wait(eng, "ds", DS0)
    pr.run("v", lambda v: v.memset(K.epsb[:, :], EPS))

    psCt = [psE, psH, psSum, psSq]
    psSpk = [psA, psB, psY]
    sig_done = {}
    cont_mm_done = {}
    spk_gi = 0
    pend_cont = [None]

    def flush_cont():
        if pend_cont[0] is None:
            return
        g, hc_, tci_, par_ = pend_cont[0]
        pend_cont[0] = None
        pr.wait("t", "s", sig_done[g])
        def mkcont(hc=hc_, tci=tci_, par=par_):
            def f(tensor):
                return tensor.matmul(
                    psCt[tci][0:64, :], ws2cb[:, hc * 64:(hc + 1) * 64],
                    spkb[:, par * 512:(par + 1) * 512],
                    start=(hc == 0), stop=(hc == 15))
            return f
        cont_mm_done[(hc_, tci_)] = pr.run("t", mkcont())

    for hc in range(16):
        for tci in range(4):
            bank = psSpk[spk_gi % 3]
            if spk_gi >= 3:
                pr.wait("t", "s", sig_done[spk_gi - 3])
            def mkspk(hc=hc, tci=tci, bank=bank):
                def f(tensor):
                    mm = None
                    for ic in range(8):
                        mm = tensor.matmul(
                            bank[:, :],
                            wencb[:, ic * H + hc * 128: ic * H + (hc + 1) * 128],
                            xbb[:, ic * TC + tci * 512: ic * TC + (tci + 1) * 512],
                            start=(ic == 0), stop=(ic == 7))
                    return mm
                return f
            td = pr.run("t", mkspk())
            pr.wait("s", "t", td)
            sd = pr.run("s", (lambda hc=hc, bank=bank, par=spk_gi % 2: lambda s:
                              s.activation(spkb[:, par * 512:(par + 1) * 512],
                                           bank[:, :], AF.Sigmoid,
                                           bias=bencc[:, hc:hc + 1], scale=1.0))())
            sig_done[spk_gi] = sd
            flush_cont()
            pend_cont[0] = (spk_gi, hc, tci, spk_gi % 2)
            spk_gi += 1
    flush_cont()
    for tci in range(4):
        pr.wait("s", "t", cont_mm_done[(15, tci)])
        pr.run("s", (lambda tci=tci: lambda s: s.activation(
            contb[:, tci * 512:(tci + 1) * 512], psCt[tci][0:64, :], AF.Identity,
            bias=bs2cc[:, :], scale=1.0))())
    S_CONT = c0.get("s")

    pr.wait("t", "s", S_CONT)
    H0B = c0.get("h0")
    gstore_marks = []
    for tt in range(16):
        def mklog(tt=tt):
            def f(tensor):
                tensor.matmul(psC[0:128, 0:NE], contb[:, tt * 128:(tt + 1) * 128],
                              wrb[:, :], start=True, stop=False)
                return tensor.matmul(psC[0:128, 0:NE], K.or512[:, 0:128], brr[:, :],
                                     start=False, stop=True)
            return f
        td = pr.run("t", mklog())
        pr.wait("v", "t", td)
        pr.run("v", lambda v: v.tensor_reduce(out=rtc[:, 0:1], in_=psC[0:128, 0:NE],
                                              axis=mybir.AxisListType.X, op=AluOpType.max))
        pr.run("v", lambda v: v.tensor_scalar(out=rtc[:, 1:2], in0=rtc[:, 0:1],
                                              scalar1=-1.0, scalar2=None,
                                              op0=AluOpType.mult))
        V_NM = c0.get("v")
        pr.wait("s", "v", V_NM)
        sd = pr.run("s", lambda s: s.activation(grt[:, 0:8], psC[0:128, 0:NE], AF.Exp,
                                                bias=rtc[:, 1:2], scale=1.0))
        pr.wait("v", "s", sd)
        pr.run("v", lambda v: v.tensor_reduce(out=rtc[:, 2:3], in_=grt[:, 0:8],
                                              axis=mybir.AxisListType.X, op=AluOpType.add))
        pr.run("v", lambda v: v.reciprocal(out=rtc[:, 3:4], in_=rtc[:, 2:3]))
        pr.run("v", lambda v: v.tensor_scalar(out=grt[:, 0:8], in0=grt[:, 0:8],
                                              scalar1=rtc[:, 3:4], scalar2=None,
                                              op0=AluOpType.mult))
        pr.run("v", lambda v: v.tensor_reduce(out=rtc[:, 4:5], in_=grt[:, 0:8],
                                              axis=mybir.AxisListType.X, op=AluOpType.max))
        pr.run("v", lambda v: v.tensor_scalar(out=grt[:, 8:16], in0=grt[:, 0:8],
                                              scalar1=rtc[:, 4:5], scalar2=None,
                                              op0=AluOpType.is_ge))
        pr.run("v", lambda v: v.scalar_tensor_tensor(out=grt[:, 8:16], in0=grt[:, 8:16],
                                                     scalar=-1e9, in1=grt[:, 0:8],
                                                     op0=AluOpType.mult, op1=AluOpType.add))
        pr.run("v", lambda v: v.tensor_reduce(out=rtc[:, 5:6], in_=grt[:, 8:16],
                                              axis=mybir.AxisListType.X, op=AluOpType.max))
        pr.run("v", lambda v: v.tensor_tensor(out=rtc[:, 6:7], in0=rtc[:, 4:5],
                                              in1=rtc[:, 5:6], op=AluOpType.add))
        pr.run("v", lambda v: v.reciprocal(out=rtc[:, 7:8], in_=rtc[:, 6:7]))
        pr.run("v", lambda v: v.tensor_scalar(out=grt[:, 8:16], in0=grt[:, 0:8],
                                              scalar1=rtc[:, 5:6], scalar2=None,
                                              op0=AluOpType.is_ge))
        pr.run("v", lambda v: v.tensor_scalar(out=grt[:, 0:8], in0=grt[:, 0:8],
                                              scalar1=rtc[:, 7:8], scalar2=None,
                                              op0=AluOpType.mult))
        vg = pr.run("v", lambda v: v.tensor_tensor(out=gbt[:, 0:8], in0=grt[:, 0:8],
                                                   in1=grt[:, 8:16], op=AluOpType.mult))
        pr.wait("t", "v", vg)
        tg_ = pr.run("t", lambda t: t.transpose(psG[0:8, 0:128], gbt[:, 0:8],
                                                K.i128[:, :]))
        pr.wait("v", "t", tg_)
        if tt >= 1:
            pr.wait("v", "h0", H0B + 16 * tt)
        pr.run("v", lambda v: v.tensor_copy(out=gtmp[:, :], in_=psG[0:8, 0:128]))
        gstore_marks.append(c0.get("v"))

    def sync_gstore(sync):
        for tt in range(16):
            sync.wait_ge(sems["v"], gstore_marks[tt])
            sync.dma_start(out=S.gdram[:, tt * 128:(tt + 1) * 128],
                           in_=gtmp[:, :]).then_inc(sems["h0"], 16)
            c0.inc("h0", 16)
        sync.wait_ge(sems["h0"], H0B + 256)
        sync.dma_start(out=gr[:, :],
                       in_=S.gdram.rearrange("e t -> (e t)").rearrange("(o n) -> o n", o=1)
                       ).then_inc(sems["h0"], 16)
        c0.inc("h0", 16)
    block.sync(sync_gstore)
    H0_GR = c0.get("h0")

    pr.wait("t", "h0", H0_GR)
    cg_done = {}
    eo_done = {}
    relu_done = {}
    psHrot = [psH, psY]
    ehg = 0
    pend_w2 = [None]

    def flush_w2():
        if pend_w2[0] is None:
            return
        g, e_, eh_, par_, start_ = pend_w2[0]
        pend_w2[0] = None
        pr.wait("t", "s", relu_done[g])
        def mkw2(e=e_, eh=eh_, par=par_, start=start_):
            def f(tensor):
                return tensor.matmul(
                    psE[0:64, :], w2b[:, (e * 8 + eh) * M:(e * 8 + eh + 1) * M],
                    hsb[:, par * 512:(par + 1) * 512],
                    start=start, stop=False)
            return f
        pr.run("t", mkw2())

    def emit_b2(e_, tci_, stop_):
        def f(tensor):
            return tensor.matmul(psE[0:64, :], b2r[:, e_ * M:(e_ + 1) * M],
                                 gr[:, e_ * TC + tci_ * 512: e_ * TC + (tci_ + 1) * 512],
                                 start=False, stop=stop_)
        pr.run("t", f)

    for tci in range(4):
        for e in range(NE):
            if (tci, e) != (0, 0):
                prev = cg_done.get((tci, e - 1)) if e > 0 else cg_done.get((tci - 1, NE - 1))
                pr.wait("t", "v", prev or 0)
            def mkgb(e=e, tci=tci):
                def f(tensor):
                    return tensor.matmul(psC[0:64, :], K.oc64[:, :],
                                         gr[:, e * TC + tci * 512: e * TC + (tci + 1) * 512],
                                         start=True, stop=True)
                return f
            tb = pr.run("t", mkgb())
            pr.wait("v", "t", tb)
            vd = pr.run("v", (lambda tci=tci: lambda v: v.tensor_tensor(
                out=cg[:, :], in0=psC[0:64, :], in1=contb[:, tci * 512:(tci + 1) * 512],
                op=AluOpType.mult))())
            cg_done[(tci, e)] = vd
            pr.wait("t", "v", vd)
            for eh in range(8):
                par = ehg % 2
                bank = psHrot[par]
                if ehg >= 2:
                    pr.wait("t", "s", relu_done[ehg - 2])
                def mkh(e=e, eh=eh, tci=tci, bank=bank):
                    def f(tensor):
                        tensor.matmul(bank[:, :],
                                      w1b[:, e * EH + eh * 128: e * EH + (eh + 1) * 128],
                                      cg[:, :], start=True, stop=False)
                        return tensor.matmul(
                            bank[:, :], b1r[:, e * EH + eh * 128: e * EH + (eh + 1) * 128],
                            gr[:, e * TC + tci * 512: e * TC + (tci + 1) * 512],
                            start=False, stop=True)
                    return f
                th = pr.run("t", mkh())
                pr.wait("s", "t", th)
                sd = pr.run("s", (lambda par=par, bank=bank: lambda s: s.activation(
                    hsb[:, par * 512:(par + 1) * 512], bank[:, :], AF.Relu,
                    scale=1.0))())
                relu_done[ehg] = sd
                flush_w2()
                pend_w2[0] = (ehg, e, eh, par, e == 0 and eh == 0)
                ehg += 1
            flush_w2()
            emit_b2(e, tci, e == NE - 1)
        TEOD = c0.get("t")
        pr.wait("v", "t", TEOD)
        ed = pr.run("v", (lambda tci=tci: lambda v: v.tensor_copy(
            out=eo[:, tci * 512:(tci + 1) * 512], in_=psE[0:64, :]))())
        eo_done[tci] = ed
    V_EO = c0.get("v")

    _cm_hsb.__exit__(None, None, None)
    _cm_cg.__exit__(None, None, None)
    _cm_b2r.__exit__(None, None, None)
    _cm_w2b.__exit__(None, None, None)
    _cm_b1r.__exit__(None, None, None)
    _cm_w1b.__exit__(None, None, None)
    _cm_spk.__exit__(None, None, None)
    _cm_wen.__exit__(None, None, None)
    _cm_xbb.__exit__(None, None, None)
    _cm_gr.__exit__(None, None, None)
    _cm_contb.__exit__(None, None, None)
    _cm_dec = nc.sbuf_tensor(tg + "dec", [128, 8 * TC], BF16); dec = _cm_dec.__enter__()
    _cm_xb = nc.sbuf_tensor(tg + "xb", [128, 8 * TC], F32); xb = _cm_xb.__enter__()
    _cm_smtc = nc.sbuf_tensor(tg + "sm", [128, 16 * 512], BF16); smtc = _cm_smtc.__enter__()

    def p_xloads(sync):
        sync.wait_ge(sems["v"], V_EO)
        for cc in range(8):
            sync.dma_start(out=xb[:, cc * TC:(cc + 1) * TC],
                           in_=S.x3park[cc * 128:(cc + 1) * 128, t0:t0 + TC]
                           ).then_inc(sems["dh"], 16)
            c0.inc("dh", 16)
    block.sync(p_xloads)
    DH1 = c0.get("dh")

    evac_marks = []
    dec_gi = 0
    psSm = [psC, psH, psY]
    for tci in range(4):
        sm_marks = {}
        for hc in range(16):
            bank = psSm[hc % 3]
            if hc >= 3:
                pr.wait("t", "s", sm_marks[hc - 3])
            def mksm(hc=hc, tci=tci, bank=bank):
                def f(tensor):
                    return tensor.matmul(bank[0:128, :], wc2sb[:, hc * 128:(hc + 1) * 128],
                                         eo[:, tci * 512:(tci + 1) * 512],
                                         start=True, stop=True)
                return f
            tsm = pr.run("t", mksm())
            pr.wait("s", "t", tsm)
            if tci > 0:
                pr.wait("s", "t", c0.get("%s_dec_%d" % (tg, tci - 1), 0))
            sd = pr.run("s", (lambda hc=hc, bank=bank: lambda s: s.activation(
                smtc[:, hc * 512:(hc + 1) * 512], bank[0:128, :], AF.Sigmoid,
                bias=bc2sc[:, hc:hc + 1], scale=1.0))())
            sm_marks[hc] = sd
        pr.wait("t", "s", sm_marks[15])
        for oc in range(8):
            bank = psA if dec_gi % 2 == 0 else psB
            if dec_gi >= 2:
                pr.wait("t", "v", evac_marks[dec_gi - 2])
            def mkdec(oc=oc, bank=bank):
                def f(tensor):
                    for hc in range(16):
                        tensor.matmul(bank[:, :],
                                      wdecb[:, hc * 1024 + oc * 128: hc * 1024 + (oc + 1) * 128],
                                      smtc[:, hc * 512:(hc + 1) * 512],
                                      start=(hc == 0), stop=False)
                    return tensor.matmul(bank[:, :], bdrow[:, oc * 128:(oc + 1) * 128],
                                         K.or512[:, :], start=False, stop=True)
                return f
            td = pr.run("t", mkdec())
            pr.wait("v", "t", td)
            vd = pr.run("v", (lambda oc=oc, tci=tci, bank=bank: lambda v: v.tensor_copy(
                out=dec[:, oc * TC + tci * 512: oc * TC + (tci + 1) * 512],
                in_=bank[:, :]))())
            evac_marks.append(vd)
            dec_gi += 1
        c0.v["%s_dec_%d" % (tg, tci)] = c0.get("t")

    _cm_smtc.__exit__(None, None, None)
    _cm_scr = nc.sbuf_tensor(tg + "sc", [128, 4 * 512], BF16); scr = _cm_scr.__enter__()
    _cm_rw = nc.sbuf_tensor(tg + "rw", [1, 4 * TC], F32); rows_b = _cm_rw.__enter__()
    rows = [rows_b[:, k * TC:(k + 1) * TC] for k in range(4)]
    _cm_bm = nc.sbuf_tensor(tg + "bm", [128, TC], F32); bcm = _cm_bm.__enter__()
    _cm_bs = nc.sbuf_tensor(tg + "bs", [128, TC], F32); bcs = _cm_bs.__enter__()
    _cm_xo = nc.sbuf_tensor(tg + "xo", [128, TC], F32); xo2 = _cm_xo.__enter__()

    decbuf = [dec[:, oc * TC:(oc + 1) * TC] for oc in range(8)]
    pr.wait("v", "dh", DH1)
    emit_ln_tail(pr, decbuf, scr, rows, bcm, bcs,
                 lambda ch: xo2[:, :],
                 psSum, psSq, psC, K.ocb, K.oc1f, K.epsb,
                 gcol=lgb[:, 0:8], bcol=lgb[:, 8:16],
                 pre_wait=("t", c0.get("t")),
                 post_add_ap=lambda ch: xb[:, ch * TC:(ch + 1) * TC])
    rbuf2 = [xb[:, oc * TC:(oc + 1) * TC] for oc in range(8)]
    ST0 = c0.get("st")

    def chunk_prewait(ch):
        if ch >= 1:
            return ("st", ST0 + 16 * ch)
        return None
    ready = emit_ln_tail(pr, rbuf2, scr, rows, bcm, bcs,
                         lambda ch: xo2[:, :],
                         psSum, psSq, psC, K.ocb, K.oc1f, K.epsb,
                         chunk_prewait=chunk_prewait)

    def p_outstores(sync):
        for ch in range(8):
            sync.wait_ge(sems["v"], ready[ch])
            sync.dma_start(out=I.xo[ch * 128:(ch + 1) * 128, t0:t0 + TC],
                           in_=xo2[:, :]).then_inc(sems["st"], 16)
            c0.inc("st", 16)
    block.sync(p_outstores)
    pr.barrier()
    pr.wait("v", "st", c0.get("st"))
    pr.wait("t", "st", c0.get("st"))
    _cm_xo.__exit__(None, None, None)
    _cm_bs.__exit__(None, None, None)
    _cm_bm.__exit__(None, None, None)
    _cm_rw.__exit__(None, None, None)
    _cm_scr.__exit__(None, None, None)
    _cm_xb.__exit__(None, None, None)
    _cm_dec.__exit__(None, None, None)
    _p.close()



_PROG = None


def kernel(**inputs):
    global _PROG
    z = inputs
    if _PROG is None:
        _PROG = build_fused()
    nc = _PROG
    consts = host_consts()
    x = np.ascontiguousarray(np.asarray(z["L0_attn_x"], np.float32))
    w1T = np.ascontiguousarray(np.asarray(z["L0_ffn_w1"], np.float32).T)
    w2T = np.ascontiguousarray(np.asarray(z["L0_ffn_w2"], np.float32).T)
    cwf = np.ascontiguousarray(np.asarray(z["L0_ffn_conv_w"], np.float32)[:, 0, :])
    in_maps = []
    for core in range(NCORE):
        b = core // 2
        m = dict(consts)
        m.update({
            "xT": np.ascontiguousarray(x[b].T),
            "wq": z["L0_attn_Wq"], "bq": z["L0_attn_bq"],
            "wk": z["L0_attn_Wk"], "bk": z["L0_attn_bk"],
            "wv": z["L0_attn_Wv"], "bv": z["L0_attn_bv"],
            "wo": z["L0_attn_Wo"], "bo": z["L0_attn_bo"],
            "w1T": w1T, "cw": cwf, "cb": z["L0_ffn_conv_b"], "w2T": w2T,
            "wqkv": z["L1_attn_w_qkv"], "wproj": z["L1_attn_w_proj"],
            "wenc": z["L1_ffn_W_enc"], "benc": z["L1_ffn_b_enc"],
            "ws2c": z["L1_ffn_W_s2c"], "bs2c": z["L1_ffn_b_s2c"],
            "wr": z["L1_ffn_Wr"], "br": z["L1_ffn_br"],
            "w1e": z["L1_ffn_W1"], "b1e": z["L1_ffn_b1"],
            "w2e": z["L1_ffn_W2"], "b2e": z["L1_ffn_b2"],
            "wc2s": z["L1_ffn_W_c2s"], "bc2s": z["L1_ffn_b_c2s"],
            "wdec": z["L1_ffn_W_dec"], "bdec": z["L1_ffn_b_dec"],
            "lng": z["L1_ffn_ln_g"], "lnb": z["L1_ffn_ln_b"],
        })
        in_maps.append(m)
    res = run_bass_kernel_spmd(nc, in_maps, core_ids=list(range(NCORE)))
    out = np.zeros((B, T, D), np.float32)
    for b in range(B):
        out[b] = res.results[2 * b]["xo"].T
    return out


# version 11
# speedup vs baseline: 1.0225x; 1.0225x over previous
from contextlib import ExitStack

import numpy as np
import ml_dtypes
import concourse.bass as bass
import concourse.mybir as mybir
from concourse.bass_utils import run_bass_kernel_spmd
from concourse.alu_op_type import AluOpType

F32 = mybir.dt.float32
BF16 = mybir.dt.bfloat16
AF = mybir.ActivationFunctionType
NP_BF16 = ml_dtypes.bfloat16

B, T, D = 4, 4096, 1024
NCORE = 8
NH, HD = 16, 64
TC = 2048
NEG = -30000.0
MAGIC = 12582912.0
EPS = 1e-5


def host_rope_R():
    theta = 1.0 / 10000.0 ** (np.arange(0, 32, 2, dtype=np.float64) / 32.0)
    out = np.zeros((8, 128, 128), np.float32)
    for h in range(NH):
        c, off = divmod(h * 64, 128)
        cs = np.cos(h * theta).astype(np.float32)
        sn = np.sin(h * theta).astype(np.float32)
        blk = np.zeros((64, 64), np.float32)
        for half in (0, 32):
            for j in range(16):
                e, o = half + 2 * j, half + 2 * j + 1
                blk[e, half + j] = cs[j]
                blk[o, half + j] = -sn[j]
                blk[e, half + 16 + j] = sn[j]
                blk[o, half + 16 + j] = cs[j]
        out[c, off:off + 64, off:off + 64] = blk
    return out.astype(NP_BF16)


def host_consts():
    tri = np.zeros((128, 512), np.float32)
    tri[:, :128] = np.where(np.arange(128)[:, None] > np.arange(128)[None, :],
                            NEG, 0.0)
    hm = np.zeros((5, 128, 8), np.float32)
    for t in range(4):
        for sub in range(2):
            hm[t, sub * 64:(sub + 1) * 64, 2 * t + sub] = 1.0
    hm[4, :, :] = 0.5
    return {
        "ropeR": host_rope_R(),
        "tri": tri.astype(NP_BF16),
        "i128": np.eye(128, dtype=NP_BF16),
        "hm": hm.astype(NP_BF16),
        "i8": np.eye(8, dtype=np.float32),
        "ones8": np.ones((8, 128), np.float32),
        "onesf": np.ones((1, 64), dtype=NP_BF16),
        "oc1f": np.ones((1, 128), np.float32),
        "c127": np.full((1, 128), 127.0, np.float32),
        "ocf": np.ones((128, 1), np.float32),
        "ocb": np.ones((128, 1), dtype=NP_BF16),
        "or512": np.ones((1, 512), dtype=NP_BF16),
        "oc64": np.ones((1, 64), dtype=NP_BF16),
    }



class Cnt:
    def __init__(self):
        self.v = {}

    def inc(self, name, amt=1):
        self.v[name] = self.v.get(name, 0) + amt
        return self.v[name]

    def get(self, name, default=0):
        return self.v.get(name, default)


def _emit(engine, prog, sems):
    for item in prog:
        if item[0] == "wait":
            engine.wait_ge(sems[item[1]], item[2])
        else:
            inst = item[1](engine)
            if item[2] is not None:
                inst.then_inc(sems[item[2][0]], item[2][1])


class Prog:

    def __init__(self, nc):
        self.nc = nc
        self.c = Cnt()
        self.progs = {"t": [], "s": [], "v": []}

    def wait(self, eng, sem, val):
        if val > 0:
            self.progs[eng].append(("wait", sem, val))

    def run(self, eng, fn):
        if eng in ("s", "v") and self.c.get(eng) > 0:
            self.progs[eng].append(("wait", eng, self.c.get(eng)))
        self.c.inc(eng, 1)
        self.progs[eng].append(("op", fn, (eng, 1)))
        return self.c.get(eng)

    def barrier(self):
        t, s, v = self.c.get("t"), self.c.get("s"), self.c.get("v")
        self.wait("t", "s", s); self.wait("t", "v", v)
        self.wait("s", "t", t); self.wait("s", "v", v)
        self.wait("v", "t", t); self.wait("v", "s", s)
        return t, s, v

    def emit(self, block, sems):
        block.tensor(lambda e: _emit(e, self.progs["t"], sems))
        block.scalar(lambda e: _emit(e, self.progs["s"], sems))
        block.vector(lambda e: _emit(e, self.progs["v"], sems))


class Ctx:
    pass



def emit_ln_tail(pr, rbuf, scr, rows, bcm, bcs, xout_ap, psSum, psSq, psBc,
                 onescolb, ones1f, epsb, gcol=None, bcol=None, nchunk=8, ntc=4,
                 pre_wait=None, chunk_prewait=None, post_add_ap=None):
    c = pr.c
    CW = 512
    if pre_wait is not None:
        pr.wait("v", pre_wait[0], pre_wait[1])
    pr.wait("s", "v", c.get("v"))
    if pre_wait is not None:
        pr.wait("s", pre_wait[0], pre_wait[1])
    mm_done = {}
    for tc_i in range(ntc):
        for ch in range(nchunk):
            slot = ch % 2
            if (tc_i, ch) >= (0, 2) and mm_done:
                key = (tc_i, ch - 2) if ch >= 2 else (tc_i - 1, nchunk - 2 + ch)
                if key in mm_done:
                    pr.wait("v", "t", mm_done[key])
                    pr.wait("s", "t", mm_done[key])
            cv = pr.run("v", (lambda ch=ch, tc_i=tc_i, slot=slot: lambda v: v.tensor_copy(
                out=scr[:, (2 * slot) * CW:(2 * slot + 1) * CW],
                in_=rbuf[ch][:, tc_i * CW:(tc_i + 1) * CW]))())
            sd = pr.run("s", (lambda ch=ch, tc_i=tc_i, slot=slot: lambda s: s.activation(
                scr[:, (2 * slot + 1) * CW:(2 * slot + 2) * CW],
                rbuf[ch][:, tc_i * CW:(tc_i + 1) * CW], AF.Square, scale=1.0))())
            pr.wait("t", "v", cv)
            pr.wait("t", "s", sd)
            def mk_mm(ch=ch, slot=slot, first=(ch == 0), last=(ch == nchunk - 1)):
                def f(tensor):
                    tensor.matmul(psSum[0:1, :], onescolb[:, :],
                                  scr[:, (2 * slot) * CW:(2 * slot + 1) * CW],
                                  start=first, stop=last)
                    return tensor.matmul(psSq[0:1, :], onescolb[:, :],
                                         scr[:, (2 * slot + 1) * CW:(2 * slot + 2) * CW],
                                         start=first, stop=last)
                return f
            mm_done[(tc_i, ch)] = pr.run("t", mk_mm())
        pr.wait("v", "t", mm_done[(tc_i, nchunk - 1)])
        pr.run("v", (lambda tc_i=tc_i: lambda v: v.tensor_copy(
            out=rows[0][:, tc_i * CW:(tc_i + 1) * CW], in_=psSum[0:1, :]))())
        V_EV = pr.run("v", (lambda tc_i=tc_i: lambda v: v.tensor_copy(
            out=rows[1][:, tc_i * CW:(tc_i + 1) * CW], in_=psSq[0:1, :]))())
        if tc_i < ntc - 1:
            pr.wait("t", "v", V_EV)
    n = float(nchunk * 128)
    pr.run("v", lambda v: v.tensor_scalar(out=rows[0][:, :], in0=rows[0][:, :],
                                          scalar1=1.0 / n, scalar2=None,
                                          op0=AluOpType.mult))
    pr.run("v", lambda v: v.tensor_scalar(out=rows[1][:, :], in0=rows[1][:, :],
                                          scalar1=1.0 / n, scalar2=None,
                                          op0=AluOpType.mult))
    pr.run("v", lambda v: v.tensor_tensor(out=rows[2][:, :], in0=rows[0][:, :],
                                          in1=rows[0][:, :], op=AluOpType.mult))
    pr.run("v", lambda v: v.tensor_tensor(out=rows[1][:, :], in0=rows[1][:, :],
                                          in1=rows[2][:, :], op=AluOpType.subtract))
    pr.run("v", lambda v: v.memset(epsb[:, :], EPS))
    V_VAR = c.get("v")
    pr.wait("s", "v", V_VAR)
    S_SQ = pr.run("s", lambda s: s.activation(rows[2][:, :], rows[1][:, :],
                                              AF.Sqrt, bias=epsb[:, :], scale=1.0))
    pr.wait("t", "s", S_SQ)
    for tc_i in range(ntc):
        td = pr.run("t", (lambda tc_i=tc_i: lambda t: t.matmul(
            psBc[0:128, :], ones1f[:, :],
            rows[2][:, tc_i * CW:(tc_i + 1) * CW],
            start=True, stop=True))())
        pr.wait("v", "t", td)
        pr.run("v", (lambda tc_i=tc_i: lambda v: v.reciprocal(
            out=bcs[:, tc_i * CW:(tc_i + 1) * CW], in_=psBc[0:128, :]))())
        pr.wait("t", "v", c.get("v"))
        td = pr.run("t", (lambda tc_i=tc_i: lambda t: t.matmul(
            psBc[0:128, :], ones1f[:, :],
            rows[0][:, tc_i * CW:(tc_i + 1) * CW],
            start=True, stop=True))())
        pr.wait("v", "t", td)
        pr.run("v", (lambda tc_i=tc_i: lambda v: v.tensor_copy(
            out=bcm[:, tc_i * CW:(tc_i + 1) * CW], in_=psBc[0:128, :]))())
        pr.run("v", (lambda tc_i=tc_i: lambda v: v.tensor_tensor(
            out=bcm[:, tc_i * CW:(tc_i + 1) * CW],
            in0=bcm[:, tc_i * CW:(tc_i + 1) * CW],
            in1=bcs[:, tc_i * CW:(tc_i + 1) * CW], op=AluOpType.mult))())
        if tc_i < ntc - 1:
            pr.wait("t", "v", c.get("v"))
    ready = []
    for ch in range(nchunk):
        if chunk_prewait is not None:
            pw = chunk_prewait(ch)
            if pw is not None:
                pr.wait("v", pw[0], pw[1])
        xo = xout_ap(ch)
        pr.run("v", (lambda ch=ch, xo=xo: lambda v: v.tensor_tensor(
            out=xo, in0=rbuf[ch][:, :], in1=bcs[:, :], op=AluOpType.mult))())
        pr.run("v", (lambda ch=ch, xo=xo: lambda v: v.tensor_tensor(
            out=xo, in0=xo, in1=bcm[:, :], op=AluOpType.subtract))())
        if gcol is not None:
            pr.run("v", (lambda ch=ch, xo=xo: lambda v: v.tensor_scalar(
                out=xo, in0=xo, scalar1=gcol[:, ch:ch + 1], scalar2=bcol[:, ch:ch + 1],
                op0=AluOpType.mult, op1=AluOpType.add))())
        if post_add_ap is not None:
            pa = post_add_ap(ch)
            pr.run("v", (lambda xo=xo, pa=pa: lambda v: v.tensor_tensor(
                out=pa, in0=xo, in1=pa, op=AluOpType.add))())
        ready.append(c.get("v"))
    return ready



def emit_absmax_row(pr, xch, nch, sc32, scb, psT, psTR, mrow, i128, sqtmp=None):
    c = pr.c
    if nch == 1:
        pr.run("v", lambda v: v.tensor_copy(out=scb[:, 0:TC], in_=xch(0)))
    else:
        pr.run("v", lambda v: v.tensor_tensor(out=sc32[:, :], in0=xch(0), in1=xch(0),
                                              op=AluOpType.mult))
        for ch in range(1, nch):
            pr.run("v", (lambda ch=ch: lambda v: v.tensor_tensor(
                out=sqtmp[:, :], in0=xch(ch), in1=xch(ch), op=AluOpType.mult))())
            pr.run("v", lambda v: v.tensor_tensor(out=sc32[:, :], in0=sc32[:, :],
                                                  in1=sqtmp[:, :], op=AluOpType.max))
        pr.run("v", lambda v: v.tensor_copy(out=scb[:, 0:TC], in_=sc32[:, :]))
    V_AM = c.get("v")
    pr.wait("t", "v", V_AM)
    for tt in range(16):
        td = pr.run("t", (lambda tt=tt: lambda t: t.transpose(
            psT[0:128, 0:128], scb[:, tt * 128:(tt + 1) * 128], i128[:, :]))())
        pr.wait("v", "t", td)
        pr.run("v", (lambda tt=tt: lambda v: v.tensor_reduce(
            out=sc32[:, tt:tt + 1], in_=psT[0:128, 0:128],
            axis=mybir.AxisListType.X, op=AluOpType.max))())
        vd = pr.run("v", (lambda tt=tt: lambda v: v.tensor_copy(
            out=scb[:, TC + tt:TC + tt + 1], in_=sc32[:, tt:tt + 1]))())
        pr.wait("t", "v", vd)
        td2 = pr.run("t", (lambda tt=tt: lambda t: t.transpose(
            psTR[0:1, 0:128], scb[:, TC + tt:TC + tt + 1], i128[:, :]))())
        pr.wait("v", "t", td2)
        pr.run("v", (lambda tt=tt: lambda v: v.tensor_copy(
            out=mrow[:, tt * 128:(tt + 1) * 128], in_=psTR[0:1, 0:128]))())
    pr.run("v", lambda v: v.tensor_scalar(out=mrow[:, :], in0=mrow[:, :],
                                          scalar1=1e-30, scalar2=None,
                                          op0=AluOpType.max))
    V_PRE = c.get("v")
    pr.wait("s", "v", V_PRE)
    S_RT = pr.run("s", lambda s: s.activation(mrow[:, :], mrow[:, :], AF.Sqrt,
                                              scale=1.0))
    pr.wait("v", "s", S_RT)
    return c.get("v")


def emit_row_bcast(pr, row_sl, dst, lhs, psX, ntc=4):
    for tc_i in range(ntc):
        td = pr.run("t", (lambda tc_i=tc_i: lambda t: t.matmul(
            psX[0:128, :], lhs[:, :], row_sl(tc_i), start=True, stop=True))())
        pr.wait("v", "t", td)
        pr.run("v", (lambda tc_i=tc_i: lambda v: v.tensor_copy(
            out=dst[:, tc_i * 512:(tc_i + 1) * 512], in_=psX[0:128, :]))())
        pr.wait("t", "v", pr.c.get("v"))


def emit_wquant_pass1(pr, c0, wf, wsum, wcol, msc, wsc128, ocf, oc1f, psX,
                      group_sz, ng, n_el, wbase):
    pr.run("v", lambda v: v.memset(wsum[:, :], 0.0))
    absum_done = []
    for gq in range(ng):
        half = gq % 2
        pr.wait("v", "w%d" % half, wbase[half] + (gq // 2 + 1) * group_sz * 16)
        for cc in range(group_sz):
            pr.run("v", (lambda half=half, cc=cc: lambda v: v.tensor_reduce(
                out=wcol[:, :], in_=wf[:, (half * group_sz + cc) * 1024:
                                       (half * group_sz + cc + 1) * 1024],
                axis=mybir.AxisListType.X, op=AluOpType.add,
                apply_absolute_value=True))())
            pr.run("v", lambda v: v.tensor_tensor(out=wsum[:, :], in0=wsum[:, :],
                                                  in1=wcol[:, :], op=AluOpType.add))
        absum_done.append(pr.c.get("v"))
    pr.wait("t", "v", pr.c.get("v"))
    td = pr.run("t", lambda t: t.matmul(psX[0:1, 0:1], wsum[:, :], ocf[:, :],
                                        start=True, stop=True))
    pr.wait("v", "t", td)
    pr.run("v", lambda v: v.tensor_copy(out=msc[:, 0:1], in_=psX[0:1, 0:1]))
    pr.run("v", (lambda n_el=n_el: lambda v: v.tensor_scalar(
        out=msc[:, 1:2], in0=msc[:, 0:1], scalar1=1.0 / n_el, scalar2=1e-5,
        op0=AluOpType.mult, op1=AluOpType.max))())
    pr.run("v", lambda v: v.reciprocal(out=msc[:, 2:3], in_=msc[:, 1:2]))
    V_SC = pr.c.get("v")
    pr.wait("t", "v", V_SC)
    td = pr.run("t", lambda t: t.matmul(psX[0:128, 0:1], oc1f[:, :], msc[:, 2:3],
                                        start=True, stop=True))
    pr.wait("v", "t", td)
    pr.run("v", lambda v: v.tensor_copy(out=wsc128[:, :], in_=psX[0:128, 0:1]))
    return absum_done


def emit_ternarize_group(pr, wf, wq_ap, wsc128, half, group_sz, wtmp):
    for cc in range(group_sz):
        src_ap = wf[:, (half * group_sz + cc) * 1024:(half * group_sz + cc + 1) * 1024]
        dst = wq_ap(cc)
        pr.run("v", (lambda src_ap=src_ap: lambda v: v.tensor_scalar(
            out=wtmp[:, :], in0=src_ap, scalar1=wsc128[:, :], scalar2=1.0,
            op0=AluOpType.mult, op1=AluOpType.min))())
        pr.run("v", lambda v: v.tensor_scalar(
            out=wtmp[:, :], in0=wtmp[:, :], scalar1=-1.0, scalar2=MAGIC,
            op0=AluOpType.max, op1=AluOpType.add))
        pr.run("v", (lambda dst=dst: lambda v: v.tensor_scalar(
            out=dst, in0=wtmp[:, :], scalar1=MAGIC, scalar2=None,
            op0=AluOpType.subtract))())
    return pr.c.get("v")



def build_fused():
    nc = bass.Bass("TRN2", target_bir_lowering=False)

    def din(name, shape, dt=F32):
        return nc.dram_tensor(name, shape, dt, kind="ExternalInput").ap()

    I = Ctx()
    I.xT = din("xT", [D, T])
    I.wq = din("wq", [D, D]); I.bq = din("bq", [D])
    I.wk = din("wk", [D, HD]); I.bk = din("bk", [HD])
    I.wv = din("wv", [D, HD]); I.bv = din("bv", [HD])
    I.wo = din("wo", [D, D]); I.bo = din("bo", [D])
    I.w1T = din("w1T", [D, 4096])
    I.cw = din("cw", [4096, 3]); I.cb = din("cb", [4096])
    I.w2T = din("w2T", [4096, D])
    I.wqkv = din("wqkv", [D, 3 * D])
    I.wproj = din("wproj", [D, D])
    I.wenc = din("wenc", [D, 2048]); I.benc = din("benc", [2048])
    I.ws2c = din("ws2c", [2048, 64]); I.bs2c = din("bs2c", [64])
    I.wr = din("wr", [64, 8]); I.br = din("br", [8])
    I.w1e = din("w1e", [8, 64, 1024]); I.b1e = din("b1e", [8, 1024])
    I.w2e = din("w2e", [8, 1024, 64]); I.b2e = din("b2e", [8, 64])
    I.wc2s = din("wc2s", [64, 2048]); I.bc2s = din("bc2s", [2048])
    I.wdec = din("wdec", [2048, D]); I.bdec = din("bdec", [D])
    I.lng = din("lng", [D]); I.lnb = din("lnb", [D])
    I.ropeR = din("ropeR", [8, 128, 128], BF16)
    I.tri_t = din("tri", [128, 512], BF16)
    I.i128_t = din("i128", [128, 128], BF16)
    I.hm_t = din("hm", [5, 128, 8], BF16)
    I.i8_t = din("i8", [8, 8], F32)
    I.ones8_t = din("ones8", [8, 128], F32)
    I.onesf_t = din("onesf", [1, 64], BF16)
    I.oc1f_t = din("oc1f", [1, 128], F32)
    I.c127_t = din("c127", [1, 128], F32)
    I.ocf_t = din("ocf", [128, 1], F32)
    I.ocb_t = din("ocb", [128, 1], BF16)
    I.or512_t = din("or512", [1, 512], BF16)
    I.oc64_t = din("oc64", [1, 64], BF16)
    I.xo = nc.dram_tensor("xo", [D, T], F32, kind="ExternalOutput").ap()

    S = Ctx()
    S.qd0 = nc.dram_tensor("qd0", [D, T], BF16).ap()
    S.kd0 = nc.dram_tensor("kd0", [HD, T], BF16).ap()
    S.vd0 = nc.dram_tensor("vd0", [T, HD], BF16).ap()
    S.attd = nc.dram_tensor("attd", [D, T], BF16).ap()
    S.x1park = nc.dram_tensor("x1park", [D, T], F32).ap()
    S.hD = nc.dram_tensor("hD", [4096, T], BF16).ap()
    S.csx = nc.dram_tensor("csx", [4096, T], BF16).ap()
    S.qx = nc.dram_tensor("qx", [4096, T], BF16).ap()
    S.x2park = nc.dram_tensor("x2park", [D, T], F32).ap()
    S.qd1 = nc.dram_tensor("qd1", [D, T], BF16).ap()
    S.kd1 = nc.dram_tensor("kd1", [D, T], BF16).ap()
    S.vd1 = nc.dram_tensor("vd1", [T, D], BF16).ap()
    S.x3park = nc.dram_tensor("x3park", [D, T], F32).ap()
    S.gdram = nc.dram_tensor("gdram", [8, TC], BF16).ap()

    pr = Prog(nc)
    c0 = pr.c

    with ExitStack() as _st:
        E = _st.enter_context
        K = Ctx()
        K.i128 = E(nc.sbuf_tensor("i128_s", [128, 128], BF16))
        K.tri = E(nc.sbuf_tensor("tri_s", [128, 512], BF16))
        K.hmb = E(nc.sbuf_tensor("hm_s", [128, 5 * 8], BF16))
        K.i8 = E(nc.sbuf_tensor("i8_s", [8, 8], F32))
        K.ones8 = E(nc.sbuf_tensor("ones8_s", [8, 128], F32))
        K.onesf = E(nc.sbuf_tensor("onesf_s", [1, 64], BF16))
        K.oc1f = E(nc.sbuf_tensor("oc1f_s", [1, 128], F32))
        K.c127 = E(nc.sbuf_tensor("c127_s", [1, 128], F32))
        K.ocf = E(nc.sbuf_tensor("ocf_s", [128, 1], F32))
        K.ocb = E(nc.sbuf_tensor("ocb_s", [128, 1], BF16))
        K.or512 = E(nc.sbuf_tensor("or512_s", [1, 512], BF16))
        K.oc64 = E(nc.sbuf_tensor("oc64_s", [1, 64], BF16))
        K.epsb = E(nc.sbuf_tensor("epsb", [1, 1], F32))
        psA = E(nc.psum_tensor("psA", [128, 512], F32))
        psB = E(nc.psum_tensor("psB", [128, 512], F32))
        psC = E(nc.psum_tensor("psC", [128, 512], F32))
        psD = E(nc.psum_tensor("psD", [128, 512], F32))
        psE = E(nc.psum_tensor("psE", [128, 512], F32))
        psF = E(nc.psum_tensor("psF", [128, 512], F32))
        PS = [psA, psB, psC, psD, psE, psF]
        sem_names = ["dh", "ds", "t", "s", "v", "st", "sg",
                     "w0", "w1", "h0", "h1", "c0", "c1", "sp"]
        sems = {n: E(nc.semaphore("s_" + n)) for n in sem_names}
        block = E(nc.Block())

        P = dict(nc=nc, pr=pr, c0=c0, sems=sems, block=block, K=K, I=I, S=S,
                 PS=PS)

        def sync_consts(sync):
            for dst, srct in ((K.i128, I.i128_t), (K.tri, I.tri_t),
                              (K.i8, I.i8_t), (K.ones8, I.ones8_t),
                              (K.onesf, I.onesf_t), (K.oc1f, I.oc1f_t),
                              (K.c127, I.c127_t), (K.ocf, I.ocf_t),
                              (K.ocb, I.ocb_t), (K.or512, I.or512_t),
                              (K.oc64, I.oc64_t)):
                sync.dma_start(out=dst[:, :], in_=srct[:, :]).then_inc(sems["dh"], 16)
                c0.inc("dh", 16)
            for t5 in range(5):
                sync.dma_start(out=K.hmb[:, t5 * 8:(t5 + 1) * 8],
                               in_=I.hm_t[t5, :, :]).then_inc(sems["dh"], 16)
                c0.inc("dh", 16)
        block.sync(sync_consts)
        pr.wait("t", "dh", c0.get("dh"))
        pr.wait("v", "dh", c0.get("dh"))
        pr.wait("s", "dh", c0.get("dh"))

        for half in range(2):
            emit_p1_half(P, half)
        emit_attn(P, mqa=True, q_src=S.qd0, k_src=S.kd0, v_src=S.vd0,
                  att_dst=S.attd)
        for half in range(2):
            emit_proj_ln_half(P, half, att_src=S.attd, w_src=I.wo,
                              bias_src=I.bo, x_src=I.xT, xpark=S.x1park,
                              tag="p3")
        for half in range(2):
            emit_ffn_a_half(P, half)
        emit_w2quant(P)
        for half in range(2):
            emit_conv_half(P, half)
            emit_bitlin2_ln_half(P, half)
        P["msc2cm"].__exit__(None, None, None)
        P["w2qcm"].__exit__(None, None, None)
        for half in range(2):
            emit_l1_qkv_half(P, half)
        emit_attn(P, mqa=False, q_src=S.qd1, k_src=S.kd1, v_src=S.vd1,
                  att_dst=S.attd)
        for half in range(2):
            emit_proj_ln_half(P, half, att_src=S.attd, w_src=I.wproj,
                              bias_src=None, x_src=S.x2park, xpark=S.x3park,
                              tag="p9")
        for half in range(2):
            emit_l1_ffn_half(P, half)

        pr.emit(block, sems)
        FINAL_ST = c0.get("st")
        FINAL_SG = c0.get("sg")

        def sync_final(sync):
            sync.wait_ge(sems["st"], FINAL_ST)
            sync.wait_ge(sems["sg"], FINAL_SG)
        block.sync(sync_final)
    return nc



def emit_p1_half(P, half):
    nc, pr, c0, sems, block = P["nc"], P["pr"], P["c0"], P["sems"], P["block"]
    K, I, S = P["K"], P["I"], P["S"]
    psA, psB, psC, psD, psE, psF = P["PS"]
    t0 = half * TC
    _p = ExitStack()
    xb = _p.enter_context(nc.sbuf_tensor(f"p1x{half}", [128, 8 * TC], BF16))
    wqb = _p.enter_context(nc.sbuf_tensor(f"p1q{half}", [128, 8 * D], BF16))
    wkf = _p.enter_context(nc.sbuf_tensor(f"p1kf{half}", [128, 8 * HD], F32))
    wkp = _p.enter_context(nc.sbuf_tensor(f"p1kp{half}", [128, 8 * HD], BF16))
    wvb = _p.enter_context(nc.sbuf_tensor(f"p1wv{half}", [128, 8 * HD], BF16))
    Rb = _p.enter_context(nc.sbuf_tensor(f"p1R{half}", [128, 8 * 128], BF16))
    bqc = _p.enter_context(nc.sbuf_tensor(f"p1bq{half}", [128, 8], F32))
    bkc = _p.enter_context(nc.sbuf_tensor(f"p1bk{half}", [64, 1], F32))
    bvr = _p.enter_context(nc.sbuf_tensor(f"p1bv{half}", [1, HD], BF16))
    qpt = _p.enter_context(nc.sbuf_tensor(f"p1qp{half}", [128, 2 * 512], BF16))
    qrt = _p.enter_context(nc.sbuf_tensor(f"p1qr{half}", [128, 2 * 512], BF16))
    kst = _p.enter_context(nc.sbuf_tensor(f"p1ks{half}", [64, TC], BF16))
    vst = _p.enter_context(nc.sbuf_tensor(f"p1vs{half}", [128, 2 * HD], BF16))

    V0, T0_, S0 = c0.get("v"), c0.get("t"), c0.get("s")
    STP, SGP = c0.get("st"), c0.get("sg")

    def p_loads_sync(sync):
        sync.wait_ge(sems["v"], V0)
        sync.wait_ge(sems["t"], T0_)
        sync.wait_ge(sems["st"], STP)
        sync.wait_ge(sems["sg"], SGP)
        for cc in range(8):
            sync.dma_start(out=wkf[:, cc * HD:(cc + 1) * HD],
                           in_=I.wk[cc * 128:(cc + 1) * 128, :]).then_inc(sems["dh"], 16)
            c0.inc("dh", 16)
        for cc in range(8):
            sync.dma_start(out=bqc[:, cc:cc + 1],
                           in_=I.bq[cc * 128:(cc + 1) * 128].rearrange("(p o) -> p o", o=1)
                           ).then_inc(sems["dh"], 16)
            c0.inc("dh", 16)
        for i, base in enumerate((0, 1, 32, 33)):
            sync.dma_start(out=bkc[16 * i:16 * (i + 1), :],
                           in_=I.bk[base:min(base + 32, HD):2].rearrange("(p o) -> p o", o=1),
                           allow_slow_non_contiguous=True).then_inc(sems["dh"], 16)
            c0.inc("dh", 16)
        for cc in range(8):
            sync.dma_start(out=Rb[:, cc * 128:(cc + 1) * 128],
                           in_=I.ropeR[cc, :, :]).then_inc(sems["dh"], 16)
            c0.inc("dh", 16)
    block.sync(p_loads_sync)
    P_DH = c0.get("dh")

    def p_loads_gp(gpsimd):
        gpsimd.wait_ge(sems["v"], V0)
        gpsimd.wait_ge(sems["t"], T0_)
        gpsimd.wait_ge(sems["st"], STP)
        gpsimd.wait_ge(sems["sg"], SGP)
        for cc in range(8):
            gpsimd.dma_start(out=xb[:, cc * TC:(cc + 1) * TC],
                             in_=I.xT[cc * 128:(cc + 1) * 128, t0:t0 + TC]
                             ).then_inc(sems["ds"], 16)
            gpsimd.dma_start(out=wqb[:, cc * D:(cc + 1) * D],
                             in_=I.wq[cc * 128:(cc + 1) * 128, :]).then_inc(sems["ds"], 16)
            gpsimd.dma_start(out=wvb[:, cc * HD:(cc + 1) * HD],
                             in_=I.wv[cc * 128:(cc + 1) * 128, :]).then_inc(sems["ds"], 16)
            c0.inc("ds", 48)
        gpsimd.dma_start(out=bvr[:, :], in_=I.bv.rearrange("(o h) -> o h", o=1)
                         ).then_inc(sems["ds"], 16)
        c0.inc("ds", 16)
    block.gpsimd(p_loads_gp)
    P_DS = c0.get("ds")

    pr.wait("v", "dh", P_DH)
    for cc in range(8):
        src = wkf[:, cc * HD:(cc + 1) * HD]
        dst = wkp[:, cc * HD:(cc + 1) * HD]
        for i, base in enumerate((0, 1, 32, 33)):
            pr.run("v", (lambda src=src, dst=dst, i=i, base=base: lambda v:
                         v.tensor_copy(out=dst[:, 16 * i:16 * (i + 1)],
                                       in_=src[:, base:min(base + 32, HD):2]))())
    V_PERM = c0.get("v")

    pr.wait("t", "ds", P_DS)
    pr.wait("t", "dh", P_DH)
    pr.wait("t", "v", V_PERM)
    SB = c0.get("s")
    STB = c0.get("st")
    q_store = {}
    for oc in range(8):
        for tci in range(4):
            gi = oc * 4 + tci
            slot = gi % 2
            ps = psA if slot == 0 else psB
            if gi >= 2:
                pr.wait("t", "s", SB + gi - 1)
            def mk_qmm(oc=oc, tci=tci, ps=ps):
                def f(tensor):
                    mm = None
                    for ic in range(8):
                        mm = tensor.matmul(
                            ps[:, :],
                            wqb[:, ic * D + oc * 128: ic * D + (oc + 1) * 128],
                            xb[:, ic * TC + tci * 512: ic * TC + (tci + 1) * 512],
                            start=(ic == 0), stop=(ic == 7))
                    return mm
                return f
            tq = pr.run("t", mk_qmm())
            pr.wait("s", "t", tq)
            if gi >= 2:
                pr.wait("s", "t", c0.get("p1_rope_%d_%d" % (half, slot), 0))
            sd = pr.run("s", (lambda oc=oc, slot=slot, ps=ps: lambda s: s.activation(
                qpt[:, slot * 512:(slot + 1) * 512], ps[:, :], AF.Identity,
                bias=bqc[:, oc:oc + 1], scale=1.0))())
            pr.wait("t", "s", sd)
            if gi >= 1:
                pr.wait("t", "v", c0.get("p1_qr_%d_%d" % (half, gi - 1), 0))
            def mk_rope(oc=oc, slot=slot):
                def f(tensor):
                    return tensor.matmul(psC[:, :], Rb[:, oc * 128:(oc + 1) * 128],
                                         qpt[:, slot * 512:(slot + 1) * 512],
                                         start=True, stop=True)
                return f
            tr = pr.run("t", mk_rope())
            c0.v["p1_rope_%d_%d" % (half, slot)] = tr
            pr.wait("v", "t", tr)
            if gi >= 2:
                pr.wait("v", "st", STB + 16 * (gi - 1))
            vd = pr.run("v", (lambda slot=slot: lambda v: v.tensor_copy(
                out=qrt[:, slot * 512:(slot + 1) * 512], in_=psC[:, :]))())
            c0.v["p1_qr_%d_%d" % (half, gi)] = vd
            q_store[gi] = (vd, oc, tci, slot)

    def p_qstores(sync):
        for gi in range(32):
            vd, oc, tci, slot = q_store[gi]
            sync.wait_ge(sems["v"], vd)
            sync.dma_start(
                out=S.qd0[oc * 128:(oc + 1) * 128, t0 + tci * 512:t0 + (tci + 1) * 512],
                in_=qrt[:, slot * 512:(slot + 1) * 512]).then_inc(sems["st"], 16)
            c0.inc("st", 16)
    block.sync(p_qstores)

    for tci in range(4):
        if tci >= 1:
            pr.wait("t", "s", c0.get("p1_k_%d" % half, 0))
        def mk_kmm(tci=tci):
            def f(tensor):
                mm = None
                for ic in range(8):
                    mm = tensor.matmul(
                        psD[0:64, :], wkp[:, ic * HD:(ic + 1) * HD],
                        xb[:, ic * TC + tci * 512: ic * TC + (tci + 1) * 512],
                        start=(ic == 0), stop=(ic == 7))
                return mm
            return f
        tk = pr.run("t", mk_kmm())
        pr.wait("s", "t", tk)
        sd = pr.run("s", (lambda tci=tci: lambda s: s.activation(
            kst[:, tci * 512:(tci + 1) * 512], psD[0:64, :],
            AF.Identity, bias=bkc[:, :], scale=1.0))())
        c0.v["p1_k_%d" % half] = sd
    S_KDONE = c0.get("s")

    v_store = {}
    STB2 = c0.get("st")
    for tt in range(16):
        slot = tt % 2
        ps = psE if slot == 0 else psF
        if tt >= 2:
            pr.wait("t", "v", c0.get("p1_v_%d_%d" % (half, tt - 2), 0))
        def mk_vmm(tt=tt, ps=ps):
            def f(tensor):
                for ic in range(8):
                    tensor.matmul(ps[0:128, 0:HD],
                                  xb[:, ic * TC + tt * 128: ic * TC + (tt + 1) * 128],
                                  wvb[:, ic * HD:(ic + 1) * HD],
                                  start=(ic == 0), stop=False)
                return tensor.matmul(ps[0:128, 0:HD], K.or512[:, 0:128],
                                     bvr[:, :], start=False, stop=True)
            return f
        tv = pr.run("t", mk_vmm())
        pr.wait("v", "t", tv)
        if tt >= 2:
            pr.wait("v", "st", STB2 + 16 * tt)
        vd = pr.run("v", (lambda tt=tt, slot=slot, ps=ps: lambda v: v.tensor_copy(
            out=vst[:, slot * HD:(slot + 1) * HD], in_=ps[0:128, 0:HD]))())
        c0.v["p1_v_%d_%d" % (half, tt)] = vd
        v_store[tt] = (vd, tt, slot)

    def p_kvstores(sync):
        sync.wait_ge(sems["s"], S_KDONE)
        sync.dma_start(out=S.kd0[:, t0:t0 + TC], in_=kst[:, :]).then_inc(sems["st"], 16)
        c0.inc("st", 16)
        for tt in range(16):
            vd, _, slot = v_store[tt]
            sync.wait_ge(sems["v"], vd)
            sync.dma_start(out=S.vd0[t0 + tt * 128:t0 + (tt + 1) * 128, :],
                           in_=vst[:, slot * HD:(slot + 1) * HD]).then_inc(sems["st"], 16)
            c0.inc("st", 16)
    block.sync(p_kvstores)
    pr.barrier()
    pr.wait("v", "st", c0.get("st"))
    pr.wait("t", "st", c0.get("st"))
    pr.wait("s", "st", c0.get("st"))
    _p.close()



def emit_attn(P, mqa, q_src, k_src, v_src, att_dst):
    nc, pr, c0, sems, block = P["nc"], P["pr"], P["c0"], P["sems"], P["block"]
    K = P["K"]
    psA, psB, psC, psD, psE, psF = P["PS"]
    tri, i128, hmb, i8, ones8, onesf = (K.tri, K.i128, K.hmb, K.i8, K.ones8,
                                        K.onesf)
    psPV = [psC, psD, psE, psF]
    NQB = T // 512
    nvt = 32 if mqa else 256
    for p in range(2):
        sfx = f"{'ab'[p]}{int(mqa)}"
        _pa = ExitStack()
        qh = _pa.enter_context(nc.sbuf_tensor("qh" + sfx, [128, 4 * T], BF16))
        kf = _pa.enter_context(nc.sbuf_tensor("kf" + sfx,
                                              [128, (T if mqa else 4 * T)], BF16))
        va = _pa.enter_context(nc.sbuf_tensor("va" + sfx, [128, nvt * 65], BF16))
        n2 = _pa.enter_context(nc.sbuf_tensor("n2" + sfx, [8, 2 * T], BF16))
        mx = _pa.enter_context(nc.sbuf_tensor("mx" + sfx, [8, 4], F32))
        i8n = _pa.enter_context(nc.sbuf_tensor("i8n" + sfx, [8, 8], F32))
        negm = _pa.enter_context(nc.sbuf_tensor("ng" + sfx, [128, 8], F32))
        pbuf = _pa.enter_context(nc.sbuf_tensor("pb" + sfx, [128, 7 * 512], BF16))
        drow = _pa.enter_context(nc.sbuf_tensor("dr" + sfx, [1, 4 * 512], F32))
        drowb = _pa.enter_context(nc.sbuf_tensor("db" + sfx, [1, 4 * 512], BF16))
        drow2 = _pa.enter_context(nc.sbuf_tensor("d2" + sfx, [1, 4 * 512], F32))
        bco = _pa.enter_context(nc.sbuf_tensor("bc" + sfx, [128, 512], F32))
        psM = _pa.enter_context(nc.psum_tensor("psM" + sfx, [128, 512], F32))
        psX = _pa.enter_context(nc.psum_tensor("psX" + sfx, [128, 512], F32))
        psS = [psA, psB, psM]

        V_FREE = c0.get("v")
        T_FREE = c0.get("t")
        ST_RDY = c0.get("st")
        SG_RDY = c0.get("sg")

        def pass_loads(sync, p=p):
            sync.wait_ge(sems["st"], ST_RDY)
            sync.wait_ge(sems["sg"], SG_RDY)
            sync.wait_ge(sems["v"], V_FREE)
            sync.wait_ge(sems["t"], T_FREE)
            for t5 in range(4):
                sync.dma_start(out=qh[:, t5 * T:(t5 + 1) * T],
                               in_=q_src[p * 512 + t5 * 128: p * 512 + (t5 + 1) * 128, :]
                               ).then_inc(sems["dh"], 16)
                c0.inc("dh", 16)
            if mqa:
                sync.dma_start(out=kf[0:64, :], in_=k_src[:, :]).then_inc(sems["dh"], 16)
                sync.dma_start(out=kf[64:128, :], in_=k_src[:, :]).then_inc(sems["dh"], 16)
                c0.inc("dh", 32)
                for kt in range(32):
                    sync.dma_start(out=va[:, kt * 65:kt * 65 + 64],
                                   in_=v_src[kt * 128:(kt + 1) * 128, :]
                                   ).then_inc(sems["dh"], 16)
                    c0.inc("dh", 16)
            else:
                for t5 in range(4):
                    sync.dma_start(out=kf[:, t5 * T:(t5 + 1) * T],
                                   in_=k_src[p * 512 + t5 * 128: p * 512 + (t5 + 1) * 128, :]
                                   ).then_inc(sems["dh"], 16)
                    c0.inc("dh", 16)
                for hl in range(8):
                    hgl = p * 8 + hl
                    for kt in range(32):
                        sync.dma_start(
                            out=va[:, (hl * 32 + kt) * 65:(hl * 32 + kt) * 65 + 64],
                            in_=v_src[kt * 128:(kt + 1) * 128, hgl * 64:(hgl + 1) * 64]
                        ).then_inc(sems["dh"], 16)
                        c0.inc("dh", 16)
        block.sync(pass_loads)
        LOADS_DH = c0.get("dh")

        pr.wait("v", "dh", LOADS_DH)
        for i in range(nvt):
            pr.run("v", (lambda i=i: lambda v: v.memset(
                va[:, i * 65 + 64:(i + 1) * 65], 1.0))())

        _cm_qsq = nc.sbuf_tensor("qsq" + sfx, [128, 4 * T], BF16)
        qsq = _cm_qsq.__enter__()
        for t5 in range(4):
            pr.run("v", (lambda t5=t5: lambda v: v.tensor_tensor(
                out=qsq[:, t5 * T:(t5 + 1) * T], in0=qh[:, t5 * T:(t5 + 1) * T],
                in1=qh[:, t5 * T:(t5 + 1) * T], op=AluOpType.mult))())
        V_QSQ = c0.get("v")
        pr.wait("t", "v", V_QSQ)
        pr.wait("t", "dh", LOADS_DH)
        for cc in range(8):
            if cc >= 1:
                pr.wait("t", "v", V_QSQ + cc)
            def mk_qn(cc=cc):
                def f(tensor):
                    mm = None
                    for t5 in range(4):
                        mm = tensor.matmul(psM[0:8, :], hmb[:, t5 * 8:(t5 + 1) * 8],
                                           qsq[:, t5 * T + cc * 512: t5 * T + (cc + 1) * 512],
                                           start=(t5 == 0), stop=(t5 == 3))
                    return mm
                return f
            tq = pr.run("t", mk_qn())
            pr.wait("v", "t", tq)
            pr.run("v", (lambda cc=cc: lambda v: v.tensor_copy(
                out=n2[0:8, cc * 512:(cc + 1) * 512], in_=psM[0:8, :]))())
        pr.wait("v", "t", c0.get("t"))
        for t5 in range(1 if mqa else 4):
            pr.run("v", (lambda t5=t5: lambda v: v.tensor_tensor(
                out=qsq[:, t5 * T:(t5 + 1) * T], in0=kf[:, t5 * T:(t5 + 1) * T],
                in1=kf[:, t5 * T:(t5 + 1) * T], op=AluOpType.mult))())
        V_KSQ = c0.get("v")
        pr.wait("t", "v", V_KSQ)
        for cc in range(8):
            if cc >= 1:
                pr.wait("t", "v", V_KSQ + cc)
            def mk_kn(cc=cc):
                def f(tensor):
                    if mqa:
                        return tensor.matmul(psM[0:8, :], hmb[:, 32:40],
                                             qsq[:, cc * 512:(cc + 1) * 512],
                                             start=True, stop=True)
                    mm = None
                    for t5 in range(4):
                        mm = tensor.matmul(psM[0:8, :], hmb[:, t5 * 8:(t5 + 1) * 8],
                                           qsq[:, t5 * T + cc * 512: t5 * T + (cc + 1) * 512],
                                           start=(t5 == 0), stop=(t5 == 3))
                    return mm
                return f
            tk = pr.run("t", mk_kn())
            pr.wait("v", "t", tk)
            pr.run("v", (lambda cc=cc: lambda v: v.tensor_copy(
                out=n2[0:8, T + cc * 512: T + (cc + 1) * 512], in_=psM[0:8, :]))())
        pr.run("v", lambda v: v.tensor_reduce(out=mx[0:8, 0:1], in_=n2[0:8, 0:T],
                                              axis=mybir.AxisListType.X, op=AluOpType.max))
        pr.run("v", lambda v: v.tensor_reduce(out=mx[0:8, 1:2], in_=n2[0:8, T:2 * T],
                                              axis=mybir.AxisListType.X, op=AluOpType.max))
        pr.run("v", lambda v: v.tensor_tensor(out=mx[0:8, 2:3], in0=mx[0:8, 0:1],
                                              in1=mx[0:8, 1:2], op=AluOpType.mult))
        V_M2 = c0.get("v")
        pr.wait("s", "v", V_M2)
        S_SQRT = pr.run("s", lambda s: s.activation(mx[0:8, 3:4], mx[0:8, 2:3],
                                                    AF.Sqrt, scale=1.0 / 64.0))
        pr.wait("v", "s", S_SQRT)
        pr.run("v", lambda v: v.tensor_scalar(out=i8n[:, :], in0=i8[:, :],
                                              scalar1=mx[0:8, 3:4], scalar2=-1.0,
                                              op0=AluOpType.mult, op1=AluOpType.mult))
        V_I8N = c0.get("v")
        pr.wait("t", "v", V_I8N)
        T_NEGM = pr.run("t", lambda t: t.matmul(psX[:, 0:8], ones8[:, :], i8n[:, :],
                                                start=True, stop=True))
        pr.wait("v", "t", T_NEGM)
        pr.run("v", lambda v: v.tensor_copy(out=negm[:, :], in_=psX[:, 0:8]))
        for slot, delta in ((4, 128), (5, 256), (6, 384)):
            pr.run("v", (lambda slot=slot, delta=delta: lambda v: v.memset(
                pbuf[:, slot * 512:slot * 512 + delta], 0.0))())
        PRE_V = c0.get("v")
        pr.wait("t", "v", PRE_V)
        pr.wait("s", "v", PRE_V)
        _cm_qsq.__exit__(None, None, None)
        _cm_att = nc.sbuf_tensor("att" + sfx, [128, 4 * T], BF16)
        att = _cm_att.__enter__()

        def k_lhsT(h, kt):
            p0 = (h % 2) * 64
            off = 0 if mqa else (h // 2) * T
            return kf[p0:p0 + 64, off + kt * 128:off + (kt + 1) * 128]

        def va_lhsT(h, kt):
            i = kt if mqa else (h * 32 + kt)
            return va[:, i * 65:(i + 1) * 65]

        exp_done_at = {}
        pbuf_last_pv = {i: 0 for i in range(7)}
        pv_stop_at = {}
        norm_done_at = {}
        last_psX_free = 0
        sc_idx = 0
        pend_pv = [None]

        def flush_pv():
            if pend_pv[0] is None:
                return
            i, hi, h, kt, slot, start, stop, qb_, hg_ = pend_pv[0]
            pend_pv[0] = None
            pr.wait("t", "s", exp_done_at[i])
            if kt == 0:
                pq = (qb_, hg_ - 1) if hg_ == 1 else (qb_ - 1, 1)
                if pq in norm_done_at:
                    pr.wait("t", "v", norm_done_at[pq])
            def mk_pv(hi=hi, h=h, kt=kt, slot=slot, start=start, stop=stop):
                def f(tensor):
                    return tensor.matmul(psPV[hi][0:65, :], va_lhsT(h, kt),
                                         pbuf[:, slot * 512:(slot + 1) * 512],
                                         start=start, stop=stop)
                return f
            pv_t = pr.run("t", mk_pv())
            pbuf_last_pv[slot] = pv_t
            if stop:
                pv_stop_at[(qb_, hg_, hi)] = pv_t

        for qb in range(NQB):
            nkt = 4 * (qb + 1)
            for hg in range(2):
                for kt in range(nkt):
                    dd = kt * 128 - qb * 512
                    for hi in range(4):
                        h = hg * 4 + hi
                        i = sc_idx
                        bank = psS[i % 3]
                        lo = max(0, dd)
                        if i >= 3:
                            pr.wait("t", "s", exp_done_at[i - 3])
                        def mk_sc(bank=bank, h=h, kt=kt, qb=qb, dd=dd, lo=lo):
                            def f(tensor):
                                if dd >= 0:
                                    tensor.matmul(bank[0:128, lo:512],
                                                  i128[:, :], tri[:, 0:512 - lo],
                                                  start=True, stop=False,
                                                  skip_group_check=True)
                                return tensor.matmul(
                                    bank[0:128, lo:512], k_lhsT(h, kt),
                                    qh[(h % 2) * 64:(h % 2) * 64 + 64,
                                       (h // 2) * T + qb * 512 + lo:(h // 2) * T + (qb + 1) * 512],
                                    start=(dd < 0), stop=True,
                                    skip_group_check=(dd >= 0))
                            return f
                        sc_t = pr.run("t", mk_sc())
                        slot = (i % 4) if dd <= 0 else 3 + dd // 128
                        pr.wait("s", "t", sc_t)
                        pr.wait("s", "t", pbuf_last_pv[slot])
                        def mk_exp(bank=bank, slot=slot, lo=lo, h=h):
                            def f(scalar):
                                return scalar.activation(
                                    pbuf[:, slot * 512 + lo:(slot + 1) * 512],
                                    bank[0:128, lo:512], AF.Exp,
                                    bias=negm[:, h:h + 1], scale=0.125)
                            return f
                        exp_done_at[i] = pr.run("s", mk_exp())
                        flush_pv()
                        pend_pv[0] = (i, hi, h, kt, slot,
                                      kt == 0, kt == nkt - 1, qb, hg)
                        sc_idx += 1
                flush_pv()
                for hi in range(4):
                    pr.wait("v", "t", pv_stop_at[(qb, hg, hi)])
                    pr.run("v", (lambda hi=hi: lambda v: v.tensor_copy(
                        out=drowb[:, hi * 512:(hi + 1) * 512],
                        in_=psPV[hi][64:65, :]))())
                V_RECIP = c0.get("v")
                for pp in range(2):
                    pr.wait("t", "v", V_RECIP)
                    if last_psX_free:
                        pr.wait("t", "v", last_psX_free)
                    def mk_bc(pp=pp):
                        def f(tensor):
                            tensor.matmul(psX[0:64, :], onesf[:, :],
                                          drowb[:, (2 * pp) * 512:(2 * pp + 1) * 512],
                                          start=True, stop=True)
                            return tensor.matmul(psX[64:128, :], onesf[:, :],
                                                 drowb[:, (2 * pp + 1) * 512:(2 * pp + 2) * 512],
                                                 start=True, stop=True)
                        return f
                    bc_t = pr.run("t", mk_bc())
                    pr.wait("v", "t", bc_t)
                    pr.run("v", lambda v: v.reciprocal(out=bco[:, :], in_=psX[:, :]))
                    last_psX_free = c0.get("v")
                    for sub in range(2):
                        hi = pp * 2 + sub
                        h = hg * 4 + hi
                        def mk_nm(hi=hi, h=h, sub=sub, qb=qb):
                            def f(vector):
                                return vector.tensor_tensor(
                                    out=att[(h % 2) * 64:(h % 2) * 64 + 64,
                                            (h // 2) * T + qb * 512:(h // 2) * T + (qb + 1) * 512],
                                    in0=psPV[hi][0:64, :],
                                    in1=bco[sub * 64:(sub + 1) * 64, :],
                                    op=AluOpType.mult)
                            return f
                        pr.run("v", mk_nm())
                norm_done_at[(qb, hg)] = c0.get("v")
        V_ATT_DONE = c0.get("v")

        def pass_stores(sync, p=p, V_ATT_DONE=V_ATT_DONE):
            sync.wait_ge(sems["v"], V_ATT_DONE)
            for t5 in range(4):
                sync.dma_start(
                    out=att_dst[p * 512 + t5 * 128: p * 512 + (t5 + 1) * 128, :],
                    in_=att[:, t5 * T:(t5 + 1) * T]).then_inc(sems["st"], 16)
                c0.inc("st", 16)
        block.sync(pass_stores)
        _cm_att.__exit__(None, None, None)
        _pa.close()
        pr.barrier()
        pr.wait("v", "st", c0.get("st"))
        pr.wait("t", "st", c0.get("st"))
        pr.wait("s", "st", c0.get("st"))



def emit_proj_ln_half(P, half, att_src, w_src, bias_src, x_src, xpark, tag):
    nc, pr, c0, sems, block = P["nc"], P["pr"], P["c0"], P["sems"], P["block"]
    K = P["K"]
    psA, psB, psC, psD, psE, psF = P["PS"]
    t0 = half * TC
    tg = f"{tag}{half}"
    _p = ExitStack()
    borow = _p.enter_context(nc.sbuf_tensor(tg + "bo", [1, D], BF16))
    rbuf = [_p.enter_context(nc.sbuf_tensor(f"{tg}rf{i}", [128, TC], F32))
            for i in range(8)]
    _cm_ab = nc.sbuf_tensor(tg + "ab", [128, 8 * TC], BF16); ab = _cm_ab.__enter__()
    _cm_xb = nc.sbuf_tensor(tg + "xb", [128, 8 * TC], F32); xb = _cm_xb.__enter__()
    _cm_wb = nc.sbuf_tensor(tg + "wb", [128, 8 * D], BF16); wb = _cm_wb.__enter__()

    V0, T0_ = c0.get("v"), c0.get("t")
    ST0, SG0 = c0.get("st"), c0.get("sg")
    DH0 = c0.get("dh")
    DS0 = c0.get("ds")

    def p_loads(sync):
        sync.wait_ge(sems["v"], V0)
        sync.wait_ge(sems["t"], T0_)
        sync.wait_ge(sems["st"], ST0)
        sync.wait_ge(sems["sg"], SG0)
        for cc in range(8):
            sync.dma_start(out=xb[:, cc * TC:(cc + 1) * TC],
                           in_=x_src[cc * 128:(cc + 1) * 128, t0:t0 + TC]
                           ).then_inc(sems["dh"], 16)
            sync.dma_start(out=ab[:, cc * TC:(cc + 1) * TC],
                           in_=att_src[cc * 128:(cc + 1) * 128, t0:t0 + TC]
                           ).then_inc(sems["dh"], 16)
            c0.inc("dh", 32)
    block.sync(p_loads)

    def p_loads_gp(gpsimd):
        gpsimd.wait_ge(sems["v"], V0)
        gpsimd.wait_ge(sems["t"], T0_)
        gpsimd.wait_ge(sems["st"], ST0)
        gpsimd.wait_ge(sems["sg"], SG0)
        for cc in range(8):
            gpsimd.dma_start(out=wb[:, cc * D:(cc + 1) * D],
                             in_=w_src[cc * 128:(cc + 1) * 128, :]).then_inc(sems["ds"], 16)
            c0.inc("ds", 16)
        if bias_src is not None:
            gpsimd.dma_start(out=borow[:, :],
                             in_=bias_src.rearrange("(o h) -> o h", o=1)
                             ).then_inc(sems["ds"], 16)
            c0.inc("ds", 16)
    block.gpsimd(p_loads_gp)
    DH1 = c0.get("dh")
    DS1 = c0.get("ds")

    pr.wait("t", "dh", DH1)
    pr.wait("t", "ds", DS1)
    VB = c0.get("v")
    psRot = [psA, psB, psE, psF]
    gi = 0
    for oc in range(8):
        for tc_i in range(4):
            ps = psRot[gi % 4]
            if gi >= 4:
                pr.wait("t", "v", VB + gi - 3)
            def mk_mm(oc=oc, tc_i=tc_i, ps=ps, with_bias=bias_src is not None):
                def f(tensor):
                    mm = None
                    for ic in range(8):
                        mm = tensor.matmul(
                            ps[:, :], wb[:, ic * D + oc * 128: ic * D + (oc + 1) * 128],
                            ab[:, ic * TC + tc_i * 512: ic * TC + (tc_i + 1) * 512],
                            start=(ic == 0), stop=(ic == 7) and not with_bias)
                    if with_bias:
                        mm = tensor.matmul(ps[:, :],
                                           borow[:, oc * 128:(oc + 1) * 128],
                                           K.or512[:, :], start=False, stop=True)
                    return mm
                return f
            td = pr.run("t", mk_mm())
            pr.wait("v", "t", td)
            pr.run("v", (lambda oc=oc, tc_i=tc_i, ps=ps: lambda v: v.tensor_tensor(
                out=rbuf[oc][:, tc_i * 512:(tc_i + 1) * 512], in0=ps[:, :],
                in1=xb[:, oc * TC + tc_i * 512: oc * TC + (tc_i + 1) * 512],
                op=AluOpType.add))())
            gi += 1
    T_MM_DONE = c0.get("t")
    _cm_wb.__exit__(None, None, None)
    _cm_xb.__exit__(None, None, None)
    _cm_ab.__exit__(None, None, None)
    _cm_xo = nc.sbuf_tensor(tg + "xo", [128, 8 * TC], F32); xout = _cm_xo.__enter__()
    _cm_scr = nc.sbuf_tensor(tg + "sc", [128, 4 * 512], BF16); scr = _cm_scr.__enter__()
    _cm_rw = nc.sbuf_tensor(tg + "rw", [1, 4 * TC], F32); rows_b = _cm_rw.__enter__()
    rows = [rows_b[:, k * TC:(k + 1) * TC] for k in range(4)]
    _cm_bm = nc.sbuf_tensor(tg + "bm", [128, TC], F32); bcm = _cm_bm.__enter__()
    _cm_bs = nc.sbuf_tensor(tg + "bs", [128, TC], F32); bcs = _cm_bs.__enter__()
    ready = emit_ln_tail(pr, rbuf, scr, rows, bcm, bcs,
                         lambda ch: xout[:, ch * TC:(ch + 1) * TC],
                         psC, psD, psE, K.ocb, K.oc1f, K.epsb,
                         pre_wait=("t", T_MM_DONE))

    def p_xstores(gpsimd):
        for ch in range(8):
            gpsimd.wait_ge(sems["v"], ready[ch])
            gpsimd.dma_start(out=xpark[ch * 128:(ch + 1) * 128, t0:t0 + TC],
                             in_=xout[:, ch * TC:(ch + 1) * TC]).then_inc(sems["sg"], 16)
            c0.inc("sg", 16)
    block.gpsimd(p_xstores)
    pr.barrier()
    pr.wait("v", "sg", c0.get("sg"))
    pr.wait("t", "sg", c0.get("sg"))
    _cm_bs.__exit__(None, None, None)
    _cm_bm.__exit__(None, None, None)
    _cm_rw.__exit__(None, None, None)
    _cm_scr.__exit__(None, None, None)
    _cm_xo.__exit__(None, None, None)
    _p.close()



def emit_ffn_a_half(P, half):
    nc, pr, c0, sems, block = P["nc"], P["pr"], P["c0"], P["sems"], P["block"]
    K, I, S = P["K"], P["I"], P["S"]
    psA, psB, psC, psD, psE, psF = P["PS"]
    t0 = half * TC
    tg = f"fa{half}"
    NG = 4
    NGS = 8
    _p = ExitStack()
    xq = _p.enter_context(nc.sbuf_tensor(tg + "xq", [128, 8 * TC], BF16))
    mrow = _p.enter_context(nc.sbuf_tensor(tg + "mr", [1, TC], F32))
    rrow = _p.enter_context(nc.sbuf_tensor(tg + "rr", [1, TC], F32))
    wsum = _p.enter_context(nc.sbuf_tensor(tg + "ws", [128, 1], F32))
    wcol = _p.enter_context(nc.sbuf_tensor(tg + "wc", [128, 1], F32))
    msc = _p.enter_context(nc.sbuf_tensor(tg + "ms", [1, 4], F32))
    wsc128 = _p.enter_context(nc.sbuf_tensor(tg + "w1", [128, 1], F32))
    bcX = _p.enter_context(nc.sbuf_tensor(tg + "bX", [128, TC], F32))
    bcR = _p.enter_context(nc.sbuf_tensor(tg + "bR", [128, TC], F32))
    psT = _p.enter_context(nc.psum_tensor(tg + "pT", [128, 1024], BF16))
    psTR = _p.enter_context(nc.psum_tensor(tg + "pR", [128, 1024], BF16))

    _cm_xb = nc.sbuf_tensor(tg + "xb", [128, 8 * TC], F32); xb = _cm_xb.__enter__()
    _cm_s32 = nc.sbuf_tensor(tg + "s3", [128, TC], F32); s32 = _cm_s32.__enter__()
    _cm_sq = nc.sbuf_tensor(tg + "sq", [128, TC], F32); sqt = _cm_sq.__enter__()
    _cm_sb = nc.sbuf_tensor(tg + "sb", [128, TC + 16], BF16); sb = _cm_sb.__enter__()

    V0, T0_ = c0.get("v"), c0.get("t")
    ST0_, SG0_ = c0.get("st"), c0.get("sg")

    def p_loads(sync):
        sync.wait_ge(sems["v"], V0)
        sync.wait_ge(sems["t"], T0_)
        sync.wait_ge(sems["st"], ST0_)
        sync.wait_ge(sems["sg"], SG0_)
        for cc in range(8):
            sync.dma_start(out=xb[:, cc * TC:(cc + 1) * TC],
                           in_=S.x1park[cc * 128:(cc + 1) * 128, t0:t0 + TC]
                           ).then_inc(sems["dh"], 16)
            c0.inc("dh", 16)
    block.sync(p_loads)
    DH0 = c0.get("dh")

    pr.wait("v", "dh", DH0)
    pr.wait("t", "dh", DH0)
    pr.wait("s", "dh", DH0)
    emit_absmax_row(pr, lambda ch: xb[:, ch * TC:(ch + 1) * TC], 8,
                    s32, sb, psT, psTR, mrow, K.i128, sqtmp=sqt)
    pr.run("v", lambda v: v.tensor_scalar(out=mrow[:, :], in0=mrow[:, :],
                                          scalar1=1e-5, scalar2=None,
                                          op0=AluOpType.max))
    pr.run("v", lambda v: v.reciprocal(out=rrow[:, :], in_=mrow[:, :]))
    pr.wait("t", "v", c0.get("v"))
    emit_row_bcast(pr, lambda tc_i: rrow[:, tc_i * 512:(tc_i + 1) * 512],
                   bcX, K.c127, psC)
    for ch in range(8):
        pr.run("v", (lambda ch=ch: lambda v: v.tensor_tensor(
            out=s32[:, :], in0=xb[:, ch * TC:(ch + 1) * TC],
            in1=bcX[:, :], op=AluOpType.mult))())
        pr.run("v", (lambda ch=ch: lambda v: v.tensor_scalar(
            out=xq[:, ch * TC:(ch + 1) * TC], in0=s32[:, :],
            scalar1=MAGIC, scalar2=MAGIC, op0=AluOpType.add,
            op1=AluOpType.subtract))())
    V_XQ = c0.get("v")

    _cm_sb.__exit__(None, None, None)
    _cm_sq.__exit__(None, None, None)
    _cm_s32.__exit__(None, None, None)
    _cm_xb.__exit__(None, None, None)
    _cm_wf = nc.sbuf_tensor(tg + "wf", [128, 2 * NGS * 1024], F32); wf = _cm_wf.__enter__()
    _cm_wq = nc.sbuf_tensor(tg + "wq", [128, 2 * NGS * 1024], BF16); wq1 = _cm_wq.__enter__()
    _cm_wt = nc.sbuf_tensor(tg + "wt", [128, 1024], F32); wtmp = _cm_wt.__enter__()
    _cm_hb = nc.sbuf_tensor(tg + "hb", [128, 8 * TC], BF16); hb = _cm_hb.__enter__()

    marks = {}
    W0 = {0: c0.get("w0"), 1: c0.get("w1")}

    def gp_loads(gpsimd):
        gpsimd.wait_ge(sems["v"], V_XQ)
        gpsimd.wait_ge(sems["st"], ST0_)
        gpsimd.wait_ge(sems["sg"], SG0_)
        for p in range(2):
            for gq in range(NG):
                hf = gq % 2
                if p == 0 and gq >= 2:
                    gpsimd.wait_ge(sems["v"], marks["absum"][gq - 2])
                elif p == 1 and gq < 2:
                    gpsimd.wait_ge(sems["v"], marks["absum"][NG - 2 + gq])
                elif p == 1:
                    gpsimd.wait_ge(sems["v"], marks["tern"][gq - 2])
                sem_h = sems["w0"] if hf == 0 else sems["w1"]
                for cc in range(NGS):
                    gpsimd.dma_start(
                        out=wf[:, (hf * NGS + cc) * 1024:(hf * NGS + cc + 1) * 1024],
                        in_=I.w1T[cc * 128:(cc + 1) * 128, gq * 1024:(gq + 1) * 1024]
                    ).then_inc(sem_h, 16)
                    c0.inc("w%d" % hf, 16)

    marks["absum"] = emit_wquant_pass1(pr, c0, wf, wsum, wcol, msc, wsc128,
                                       K.ocf, K.oc1f, psC, NGS, NG,
                                       4096.0 * 1024.0, W0)
    pr.run("v", lambda v: v.tensor_scalar(out=rrow[:, :], in0=mrow[:, :],
                                          scalar1=msc[:, 1:2], scalar2=1.0 / 127.0,
                                          op0=AluOpType.mult, op1=AluOpType.mult))
    pr.wait("t", "v", c0.get("v"))
    emit_row_bcast(pr, lambda tc_i: rrow[:, tc_i * 512:(tc_i + 1) * 512],
                   bcR, K.oc1f, psC)

    marks["tern"] = []
    store_ready = []
    mm_last = {}
    ST0 = c0.get("st")
    for gq in range(NG):
        hf = gq % 2
        wqp = gq % 2
        pr.wait("v", "w%d" % hf, W0[hf] + (NG // 2 + gq // 2 + 1) * NGS * 16)
        if gq > 1:
            pr.wait("v", "t", mm_last[gq - 2])
        tern_v = emit_ternarize_group(
            pr, wf,
            lambda cc, wqp=wqp: wq1[:, (wqp * NGS + cc) * 1024:(wqp * NGS + cc + 1) * 1024],
            wsc128, hf, NGS, wtmp)
        marks["tern"].append(tern_v)
        if gq > 0:
            pr.wait("t", "st", ST0 + 128 * gq)
        pr.wait("t", "v", tern_v)
        evac_at = []
        psRot = [psA, psB, psC, psD]
        gi = 0
        for hc in range(8):
            for tc_i in range(4):
                ps = psRot[gi % 4]
                if gi >= 4:
                    pr.wait("t", "v", evac_at[gi - 4])
                def mk(hc=hc, tc_i=tc_i, ps=ps, wqp=wqp):
                    def f(tensor):
                        mm = None
                        for ic in range(8):
                            mm = tensor.matmul(
                                ps[:, :],
                                wq1[:, (wqp * NGS + ic) * 1024 + hc * 128:
                                    (wqp * NGS + ic) * 1024 + (hc + 1) * 128],
                                xq[:, ic * TC + tc_i * 512: ic * TC + (tc_i + 1) * 512],
                                start=(ic == 0), stop=(ic == 7))
                        return mm
                    return f
                td = pr.run("t", mk())
                pr.wait("v", "t", td)
                pr.run("v", (lambda hc=hc, tc_i=tc_i, ps=ps: lambda v: v.tensor_tensor(
                    out=hb[:, hc * TC + tc_i * 512: hc * TC + (tc_i + 1) * 512],
                    in0=ps[:, :], in1=bcR[:, tc_i * 512:(tc_i + 1) * 512],
                    op=AluOpType.mult))())
                evac_at.append(c0.get("v"))
                gi += 1
        mm_last[gq] = c0.get("t")
        store_ready.append(c0.get("v"))

    def p_hstores(sync):
        for gq in range(NG):
            sync.wait_ge(sems["v"], store_ready[gq])
            for hc in range(8):
                sync.dma_start(
                    out=S.hD[(gq * 8 + hc) * 128:(gq * 8 + hc + 1) * 128, t0:t0 + TC],
                    in_=hb[:, hc * TC:(hc + 1) * TC]).then_inc(sems["st"], 16)
                c0.inc("st", 16)
    block.sync(p_hstores)
    block.gpsimd(gp_loads)
    pr.barrier()
    pr.wait("v", "st", c0.get("st"))
    pr.wait("t", "st", c0.get("st"))
    _cm_hb.__exit__(None, None, None)
    _cm_wt.__exit__(None, None, None)
    _cm_wq.__exit__(None, None, None)
    _cm_wf.__exit__(None, None, None)
    _p.close()



def emit_w2quant(P):
    nc, pr, c0, sems, block = P["nc"], P["pr"], P["c0"], P["sems"], P["block"]
    K, I = P["K"], P["I"]
    psA, psB, psC, psD, psE, psF = P["PS"]
    NG, NGS = 4, 8
    _w2q = nc.sbuf_tensor("w2q", [128, 32 * 1024], BF16)
    w2q = _w2q.__enter__()
    P["w2qcm"] = _w2q
    P["w2q"] = w2q
    msc2 = nc.sbuf_tensor("msc2", [1, 4], F32)
    P["msc2cm"] = msc2
    P["msc2"] = msc2.__enter__()
    _p = ExitStack()
    wf = _p.enter_context(nc.sbuf_tensor("w2f", [128, 2 * NGS * 1024], F32))
    wtmp = _p.enter_context(nc.sbuf_tensor("w2t", [128, 1024], F32))
    wsum = _p.enter_context(nc.sbuf_tensor("w2s", [128, 1], F32))
    wcol = _p.enter_context(nc.sbuf_tensor("w2c", [128, 1], F32))
    wsc128 = _p.enter_context(nc.sbuf_tensor("w2k", [128, 1], F32))
    marks = {}
    W0 = {0: c0.get("w0"), 1: c0.get("w1")}
    V0, T0_ = c0.get("v"), c0.get("t")
    ST0_, SG0_ = c0.get("st"), c0.get("sg")

    def gp_loads(gpsimd):
        gpsimd.wait_ge(sems["v"], V0)
        gpsimd.wait_ge(sems["t"], T0_)
        gpsimd.wait_ge(sems["st"], ST0_)
        gpsimd.wait_ge(sems["sg"], SG0_)
        for p in range(2):
            for gq in range(NG):
                hf = gq % 2
                if p == 0 and gq >= 2:
                    gpsimd.wait_ge(sems["v"], marks["absum"][gq - 2])
                elif p == 1 and gq < 2:
                    gpsimd.wait_ge(sems["v"], marks["absum"][NG - 2 + gq])
                elif p == 1:
                    gpsimd.wait_ge(sems["v"], marks["tern"][gq - 2])
                sem_h = sems["w0"] if hf == 0 else sems["w1"]
                for cc in range(NGS):
                    gpsimd.dma_start(
                        out=wf[:, (hf * NGS + cc) * 1024:(hf * NGS + cc + 1) * 1024],
                        in_=I.w2T[(gq * NGS + cc) * 128:(gq * NGS + cc + 1) * 128, :]
                    ).then_inc(sem_h, 16)
                    c0.inc("w%d" % hf, 16)

    marks["absum"] = emit_wquant_pass1(pr, c0, wf, wsum, wcol, P["msc2"],
                                       wsc128, K.ocf, K.oc1f, psC, NGS, NG,
                                       4096.0 * 1024.0, W0)
    marks["tern"] = []
    for gq in range(NG):
        hf = gq % 2
        pr.wait("v", "w%d" % hf, W0[hf] + (NG // 2 + gq // 2 + 1) * NGS * 16)
        tern_v = emit_ternarize_group(
            pr, wf,
            lambda cc, gq=gq: w2q[:, (gq * NGS + cc) * 1024:(gq * NGS + cc + 1) * 1024],
            wsc128, hf, NGS, wtmp)
        marks["tern"].append(tern_v)
    block.gpsimd(gp_loads)
    pr.barrier()
    _p.close()



def emit_conv_half(P, half):
    nc, pr, c0, sems, block = P["nc"], P["pr"], P["c0"], P["sems"], P["block"]
    K, I, S = P["K"], P["I"], P["S"]
    psA, psB, psC, psD, psE, psF = P["PS"]
    t0 = half * TC
    tg = f"cv{half}"
    _pm = ExitStack()
    mrow = _pm.enter_context(nc.sbuf_tensor(tg + "mr", [1, TC], F32))
    bcX = _pm.enter_context(nc.sbuf_tensor(tg + "bX", [128, TC], F32))
    P[f"mrow_cv{half}"] = mrow
    P[f"bcX_cv{half}"] = bcX
    P[f"mrow_cv{half}_cm"] = _pm
    _p = ExitStack()
    cwcol = _p.enter_context(nc.sbuf_tensor(tg + "cw", [128, 96], F32))
    cbcol = _p.enter_context(nc.sbuf_tensor(tg + "cb", [128, 32], F32))
    heb = _p.enter_context(nc.sbuf_tensor(tg + "he", [128, 2 * (TC + 2)], BF16))
    tb0 = _p.enter_context(nc.sbuf_tensor(tg + "t0", [128, TC], F32))
    tb1 = _p.enter_context(nc.sbuf_tensor(tg + "t1", [128, TC], F32))
    csb = _p.enter_context(nc.sbuf_tensor(tg + "cs", [128, 2 * TC], BF16))
    am = _p.enter_context(nc.sbuf_tensor(tg + "am", [128, TC], F32))
    csq = _p.enter_context(nc.sbuf_tensor(tg + "cq", [128, TC], BF16))

    V0, T0_, S0 = c0.get("v"), c0.get("t"), c0.get("s")
    ST0_, SG0_ = c0.get("st"), c0.get("sg")

    def p_loads0(sync):
        sync.wait_ge(sems["v"], V0)
        sync.wait_ge(sems["t"], T0_)
        sync.wait_ge(sems["st"], ST0_)
        sync.wait_ge(sems["sg"], SG0_)
        for hc in range(32):
            sync.dma_start(out=cwcol[:, 3 * hc:3 * (hc + 1)],
                           in_=I.cw[hc * 128:(hc + 1) * 128, :]).then_inc(sems["dh"], 16)
            sync.dma_start(out=cbcol[:, hc:hc + 1],
                           in_=I.cb[hc * 128:(hc + 1) * 128].rearrange("(p o) -> p o", o=1)
                           ).then_inc(sems["dh"], 16)
            c0.inc("dh", 32)
    block.sync(p_loads0)
    DH0 = c0.get("dh")

    conv_done = {}
    silu_done = {}
    quant_done = {}
    pr.run("v", lambda v: v.memset(am[:, :], 0.0))
    if half == 0:
        pr.run("v", lambda v: v.memset(heb[:, 0:1], 0.0))
        pr.run("v", lambda v: v.memset(heb[:, (TC + 2):(TC + 3)], 0.0))
    else:
        pr.run("v", lambda v: v.memset(heb[:, TC + 1:TC + 2], 0.0))
        pr.run("v", lambda v: v.memset(heb[:, 2 * TC + 3:2 * TC + 4], 0.0))
    V_EDGE = c0.get("v")
    pr.wait("v", "dh", DH0)
    pr.wait("s", "dh", DH0)
    HB0 = {0: c0.get("h0"), 1: c0.get("h1")}
    CB0 = {0: c0.get("c0"), 1: c0.get("c1")}

    def p_hloads(sync):
        sync.wait_ge(sems["v"], V_EDGE)
        for hc in range(32):
            par = hc % 2
            if hc >= 2:
                sync.wait_ge(sems["v"], conv_done[hc - 2])
            base = par * (TC + 2)
            sync.dma_start(out=heb[:, base + 1:base + 1 + TC],
                           in_=S.hD[hc * 128:(hc + 1) * 128, t0:t0 + TC]
                           ).then_inc(sems["h0"] if par == 0 else sems["h1"], 16)
            c0.inc("h%d" % par, 16)
            if half == 0:
                sync.dma_start(out=heb[:, base + TC + 1:base + TC + 2],
                               in_=S.hD[hc * 128:(hc + 1) * 128, TC:TC + 1],
                               allow_slow_non_contiguous=True
                               ).then_inc(sems["h0"] if par == 0 else sems["h1"], 16)
            else:
                sync.dma_start(out=heb[:, base:base + 1],
                               in_=S.hD[hc * 128:(hc + 1) * 128, TC - 1:TC],
                               allow_slow_non_contiguous=True
                               ).then_inc(sems["h0"] if par == 0 else sems["h1"], 16)
            c0.inc("h%d" % par, 16)
            sync.wait_ge(sems["s"], silu_done[hc])
            sync.dma_start(out=S.csx[hc * 128:(hc + 1) * 128, t0:t0 + TC],
                           in_=csb[:, par * TC:(par + 1) * TC]
                           ).then_inc(sems["c0"] if par == 0 else sems["c1"], 16)
            c0.inc("c%d" % par, 16)

    SP0 = c0.get("sp")
    for hc in range(32):
        par = hc % 2
        he = heb[:, par * (TC + 2):(par + 1) * (TC + 2)]
        pr.wait("v", "h%d" % par, HB0[par] + (hc // 2 + 1) * 32)
        pr.run("v", (lambda he=he, hc=hc: lambda v: v.tensor_scalar(
            out=tb0[:, :], in0=he[:, 0:TC], scalar1=cwcol[:, 3 * hc:3 * hc + 1],
            scalar2=cbcol[:, hc:hc + 1], op0=AluOpType.mult, op1=AluOpType.add))())
        pr.run("v", (lambda he=he, hc=hc: lambda v: v.scalar_tensor_tensor(
            out=tb1[:, :], in0=he[:, 1:TC + 1], scalar=cwcol[:, 3 * hc + 1:3 * hc + 2],
            in1=tb0[:, :], op0=AluOpType.mult, op1=AluOpType.add))())
        vd = pr.run("v", (lambda he=he, hc=hc: lambda v: v.scalar_tensor_tensor(
            out=tb0[:, :], in0=he[:, 2:TC + 2], scalar=cwcol[:, 3 * hc + 2:3 * hc + 3],
            in1=tb1[:, :], op0=AluOpType.mult, op1=AluOpType.add))())
        conv_done[hc] = vd
        pr.wait("s", "v", vd)
        if hc >= 2:
            pr.wait("s", "c%d" % par, CB0[par] + (hc // 2) * 16)
        sd = pr.run("s", (lambda par=par: lambda s: s.activation(
            csb[:, par * TC:(par + 1) * TC], tb0[:, :], AF.Silu, scale=1.0))())
        silu_done[hc] = sd
        sq = pr.run("s", (lambda par=par: lambda s: s.activation(
            csq[:, :], csb[:, par * TC:(par + 1) * TC], AF.Square, scale=1.0))())
        pr.wait("v", "s", sq)
        pr.run("v", lambda v: v.tensor_tensor(out=am[:, :], in0=am[:, :],
                                              in1=csq[:, :], op=AluOpType.max))
    block.sync(p_hloads)

    _q = ExitStack()
    scb = _q.enter_context(nc.sbuf_tensor(tg + "sb", [128, TC + 16], BF16))
    psT = _q.enter_context(nc.psum_tensor(tg + "pT", [128, 1024], BF16))
    psTR = _q.enter_context(nc.psum_tensor(tg + "pR", [128, 1024], BF16))
    emit_absmax_row(pr, lambda ch: am[:, :], 1, tb0, scb, psT, psTR, mrow, K.i128)
    pr.run("v", lambda v: v.tensor_scalar(out=mrow[:, :], in0=mrow[:, :],
                                          scalar1=1e-5, scalar2=None,
                                          op0=AluOpType.max))
    pr.run("v", lambda v: v.reciprocal(out=bcX[0:1, 0:TC] if False else tb0[0:1, 0:TC],
                                       in_=mrow[:, :]))
    pr.wait("t", "v", c0.get("v"))
    emit_row_bcast(pr, lambda tc_i: tb0[0:1, tc_i * 512:(tc_i + 1) * 512],
                   bcX, K.c127, psC)

    pr.barrier()
    pr.wait("v", "c0", c0.get("c0"))
    pr.wait("v", "c1", c0.get("c1"))
    pr.wait("t", "c0", c0.get("c0"))
    pr.wait("t", "c1", c0.get("c1"))
    _q.close()
    _p.close()



def emit_bitlin2_ln_half(P, half):
    nc, pr, c0, sems, block = P["nc"], P["pr"], P["c0"], P["sems"], P["block"]
    K, I, S = P["K"], P["I"], P["S"]
    psA, psB, psC, psD, psE, psF = P["PS"]
    w2q = P["w2q"]
    msc2 = P["msc2"]
    mrow = P[f"mrow_cv{half}"]
    t0 = half * TC
    tg = f"b2{half}"
    _cm_xb = nc.sbuf_tensor(tg + "xb", [128, 8 * TC], F32); xb = _cm_xb.__enter__()
    _p = ExitStack()
    rrow = _p.enter_context(nc.sbuf_tensor(tg + "rr", [1, TC], F32))
    bcR = _p.enter_context(nc.sbuf_tensor(tg + "bR", [128, TC], F32))
    xqt = _p.enter_context(nc.sbuf_tensor(tg + "xq", [128, 32 * 512], BF16))
    sevac = _p.enter_context(nc.sbuf_tensor(tg + "se", [128, 512], F32))
    tq = _p.enter_context(nc.sbuf_tensor(tg + "tq", [128, 512], F32))
    bcX = P[f"bcX_cv{half}"]

    pr.run("v", lambda v: v.tensor_scalar(out=rrow[:, :], in0=mrow[:, :],
                                          scalar1=msc2[:, 1:2], scalar2=1.0 / 127.0,
                                          op0=AluOpType.mult, op1=AluOpType.mult))
    pr.wait("t", "v", c0.get("v"))
    emit_row_bcast(pr, lambda tc_i: rrow[:, tc_i * 512:(tc_i + 1) * 512],
                   bcR, K.oc1f, psC)

    V0, T0_ = c0.get("v"), c0.get("t")
    ST0_, SG0_ = c0.get("st"), c0.get("sg")
    C0B = {0: c0.get("c0"), 1: c0.get("c1")}
    DH0 = c0.get("dh")

    def p_xloads(sync):
        sync.wait_ge(sems["v"], V0)
        sync.wait_ge(sems["t"], T0_)
        sync.wait_ge(sems["st"], ST0_)
        sync.wait_ge(sems["sg"], SG0_)
        sync.wait_ge(sems["c0"], C0B[0])
        sync.wait_ge(sems["c1"], C0B[1])
        for cc in range(8):
            sync.dma_start(out=xb[:, cc * TC:(cc + 1) * TC],
                           in_=S.x1park[cc * 128:(cc + 1) * 128, t0:t0 + TC]
                           ).then_inc(sems["dh"], 16)
            c0.inc("dh", 16)
    block.sync(p_xloads)
    DH1 = c0.get("dh")

    mm_done_tc = {}
    HB = {0: c0.get("h0"), 1: c0.get("h1")}

    def p_xqloads(sync):
        for tc_i in range(4):
            par = tc_i % 2
            if tc_i >= 1:
                sync.wait_ge(sems["t"], mm_done_tc[tc_i - 1])
            for hc in range(32):
                sync.dma_start(
                    out=xqt[:, hc * 512:(hc + 1) * 512],
                    in_=S.csx[hc * 128:(hc + 1) * 128, t0 + tc_i * 512:t0 + (tc_i + 1) * 512]
                ).then_inc(sems["h0"] if par == 0 else sems["h1"], 16)
                c0.inc("h%d" % par, 16)

    rbuf = [xb[:, oc * TC:(oc + 1) * TC] for oc in range(8)]
    evac_marks = []
    quant_tc = {}
    psRot = [psA, psB, psC, psD]

    def emit_quant_tc(tc_i):
        par = tc_i % 2
        pr.wait("v", "h%d" % par, HB[par] + (tc_i // 2 + 1) * 32 * 16)
        for hc in range(32):
            sl = xqt[:, hc * 512:(hc + 1) * 512]
            pr.run("v", (lambda sl=sl, tc_i=tc_i: lambda v: v.tensor_tensor(
                out=tq[:, :], in0=sl,
                in1=bcX[:, tc_i * 512:(tc_i + 1) * 512], op=AluOpType.mult))())
            pr.run("v", (lambda sl=sl: lambda v: v.tensor_scalar(
                out=sl, in0=tq[:, :], scalar1=MAGIC, scalar2=MAGIC,
                op0=AluOpType.add, op1=AluOpType.subtract))())
        quant_tc[tc_i] = c0.get("v")

    def emit_mm_tc(tc_i):
        par = tc_i % 2
        pr.wait("t", "v", quant_tc[tc_i])
        pr.wait("t", "dh", DH1)
        for oc in range(8):
            gi = tc_i * 8 + oc
            ps = psRot[gi % 4]
            if gi >= 4:
                pr.wait("t", "v", evac_marks[gi - 4])
            def mkmm(oc=oc, ps=ps, par=par):
                def f(tensor):
                    mm = None
                    for hc in range(32):
                        mm = tensor.matmul(
                            ps[:, :],
                            w2q[:, hc * 1024 + oc * 128: hc * 1024 + (oc + 1) * 128],
                            xqt[:, hc * 512:(hc + 1) * 512],
                            start=(hc == 0), stop=(hc == 31))
                    return mm
                return f
            td = pr.run("t", mkmm())
            pr.wait("v", "t", td)
            pr.run("v", (lambda ps=ps, tc_i=tc_i: lambda v: v.tensor_tensor(
                out=sevac[:, :], in0=ps[:, :], in1=bcR[:, tc_i * 512:(tc_i + 1) * 512],
                op=AluOpType.mult))())
            vd = pr.run("v", (lambda oc=oc, tc_i=tc_i: lambda v: v.tensor_tensor(
                out=xb[:, oc * TC + tc_i * 512: oc * TC + (tc_i + 1) * 512],
                in0=sevac[:, :],
                in1=xb[:, oc * TC + tc_i * 512: oc * TC + (tc_i + 1) * 512],
                op=AluOpType.add))())
            evac_marks.append(vd)
        mm_done_tc[tc_i] = c0.get("t")

    for tc_i in range(4):
        emit_quant_tc(tc_i)
        emit_mm_tc(tc_i)
    block.sync(p_xqloads)
    pr.barrier()
    _p.close()

    _cm_scr = nc.sbuf_tensor(tg + "sc", [128, 4 * 512], BF16); scr = _cm_scr.__enter__()
    _cm_rw = nc.sbuf_tensor(tg + "rw", [1, 3 * TC], F32); rows_b = _cm_rw.__enter__()
    rows = [rows_b[:, 0:TC], rows_b[:, TC:2 * TC], rows_b[:, 2 * TC:3 * TC],
            rows_b[:, 2 * TC:3 * TC]]
    _cm_bm = nc.sbuf_tensor(tg + "bm", [128, TC], F32); bcm = _cm_bm.__enter__()
    _cm_bs = nc.sbuf_tensor(tg + "bs", [128, TC], F32); bcs = _cm_bs.__enter__()
    _cm_xo = nc.sbuf_tensor(tg + "xo", [128, TC], F32); xo2 = _cm_xo.__enter__()
    SG0 = c0.get("sg")

    def chunk_prewait(ch):
        if ch >= 1:
            return ("sg", SG0 + 16 * ch)
        return None
    ready = emit_ln_tail(pr, rbuf, scr, rows, bcm, bcs,
                         lambda ch: xo2[:, :],
                         psC, psD, psE, K.ocb, K.oc1f, K.epsb,
                         pre_wait=("t", c0.get("t")),
                         chunk_prewait=chunk_prewait)

    def p_xstores(gpsimd):
        for ch in range(8):
            gpsimd.wait_ge(sems["v"], ready[ch])
            gpsimd.dma_start(out=S.x2park[ch * 128:(ch + 1) * 128, t0:t0 + TC],
                             in_=xo2[:, :]).then_inc(sems["sg"], 16)
            c0.inc("sg", 16)
    block.gpsimd(p_xstores)
    pr.barrier()
    pr.wait("v", "sg", c0.get("sg"))
    pr.wait("t", "sg", c0.get("sg"))
    _cm_xo.__exit__(None, None, None)
    _cm_bs.__exit__(None, None, None)
    _cm_bm.__exit__(None, None, None)
    _cm_rw.__exit__(None, None, None)
    _cm_scr.__exit__(None, None, None)
    _cm_xb.__exit__(None, None, None)
    P[f"mrow_cv{half}_cm"].close()



def emit_l1_qkv_half(P, half):
    nc, pr, c0, sems, block = P["nc"], P["pr"], P["c0"], P["sems"], P["block"]
    K, I, S = P["K"], P["I"], P["S"]
    psA, psB, psC, psD, psE, psF = P["PS"]
    t0 = half * TC
    tg = f"l1q{half}"
    _p = ExitStack()
    xb = _p.enter_context(nc.sbuf_tensor(tg + "xb", [128, 8 * TC], BF16))
    wb = _p.enter_context(nc.sbuf_tensor(tg + "wb", [128, 8 * 3 * D], BF16))
    qs = _p.enter_context(nc.sbuf_tensor(tg + "qs", [128, 8 * TC], BF16))
    ksb = _p.enter_context(nc.sbuf_tensor(tg + "ks", [128, 8 * TC], BF16))
    vs = _p.enter_context(nc.sbuf_tensor(tg + "vs", [128, 16 * D], BF16))
    V0, T0_ = c0.get("v"), c0.get("t")
    ST0_, SG0_ = c0.get("st"), c0.get("sg")

    def p_loads(gpsimd):
        gpsimd.wait_ge(sems["v"], V0)
        gpsimd.wait_ge(sems["t"], T0_)
        gpsimd.wait_ge(sems["st"], ST0_)
        gpsimd.wait_ge(sems["sg"], SG0_)
        for cc in range(8):
            gpsimd.dma_start(out=xb[:, cc * TC:(cc + 1) * TC],
                             in_=S.x2park[cc * 128:(cc + 1) * 128, t0:t0 + TC]
                             ).then_inc(sems["ds"], 16)
            c0.inc("ds", 16)
        for cc in range(8):
            gpsimd.dma_start(out=wb[:, cc * 3 * D:(cc + 1) * 3 * D],
                             in_=I.wqkv[cc * 128:(cc + 1) * 128, :]).then_inc(sems["ds"], 16)
            c0.inc("ds", 16)
    block.gpsimd(p_loads)
    DS0 = c0.get("ds")

    pr.wait("t", "ds", DS0)
    evs = []
    psRot = [psA, psB, psC, psD]
    gi = 0
    for which, outb in ((0, qs), (1, ksb)):
        for oc in range(8):
            for tci in range(4):
                ps = psRot[gi % 4]
                if gi >= 4:
                    pr.wait("t", "v", evs[gi - 4])
                def mk(which=which, oc=oc, tci=tci, ps=ps):
                    def f(tensor):
                        mm = None
                        for ic in range(8):
                            off = ic * 3 * D + which * D + oc * 128
                            mm = tensor.matmul(
                                ps[:, :], wb[:, off:off + 128],
                                xb[:, ic * TC + tci * 512: ic * TC + (tci + 1) * 512],
                                start=(ic == 0), stop=(ic == 7))
                        return mm
                    return f
                td = pr.run("t", mk())
                pr.wait("v", "t", td)
                pr.run("v", (lambda outb=outb, oc=oc, tci=tci, ps=ps: lambda v:
                             v.tensor_copy(
                                 out=outb[:, oc * TC + tci * 512: oc * TC + (tci + 1) * 512],
                                 in_=ps[:, :]))())
                evs.append(c0.get("v"))
                gi += 1
    for tt in range(16):
        for vh in range(2):
            ps = psRot[gi % 4]
            if gi >= 4:
                pr.wait("t", "v", evs[gi - 4])
            def mkv(tt=tt, vh=vh, ps=ps):
                def f(tensor):
                    mm = None
                    for ic in range(8):
                        off = ic * 3 * D + 2 * D + vh * 512
                        mm = tensor.matmul(
                            ps[:, :],
                            xb[:, ic * TC + tt * 128: ic * TC + (tt + 1) * 128],
                            wb[:, off:off + 512],
                            start=(ic == 0), stop=(ic == 7))
                    return mm
                return f
            td = pr.run("t", mkv())
            pr.wait("v", "t", td)
            pr.run("v", (lambda tt=tt, vh=vh, ps=ps: lambda v: v.tensor_copy(
                out=vs[:, tt * D + vh * 512: tt * D + (vh + 1) * 512],
                in_=ps[:, :]))())
            evs.append(c0.get("v"))
            gi += 1
    V_ALL = c0.get("v")

    def p_stores(sync):
        sync.wait_ge(sems["v"], V_ALL)
        for cc in range(8):
            sync.dma_start(out=S.qd1[cc * 128:(cc + 1) * 128, t0:t0 + TC],
                           in_=qs[:, cc * TC:(cc + 1) * TC]).then_inc(sems["st"], 16)
            sync.dma_start(out=S.kd1[cc * 128:(cc + 1) * 128, t0:t0 + TC],
                           in_=ksb[:, cc * TC:(cc + 1) * TC]).then_inc(sems["st"], 16)
            c0.inc("st", 32)
        for tt in range(16):
            sync.dma_start(out=S.vd1[t0 + tt * 128:t0 + (tt + 1) * 128, :],
                           in_=vs[:, tt * D:(tt + 1) * D]).then_inc(sems["st"], 16)
            c0.inc("st", 16)
    block.sync(p_stores)
    pr.barrier()
    pr.wait("v", "st", c0.get("st"))
    pr.wait("t", "st", c0.get("st"))
    _p.close()



def emit_l1_ffn_half(P, half):
    nc, pr, c0, sems, block = P["nc"], P["pr"], P["c0"], P["sems"], P["block"]
    K, I, S = P["K"], P["I"], P["S"]
    psA, psB, psE_, psH_, psC_, psSum_ = P["PS"][0], P["PS"][1], P["PS"][2], P["PS"][3], P["PS"][4], P["PS"][5]
    psE = P["PS"][2]
    psH = P["PS"][3]
    psC = P["PS"][4]
    psSum = P["PS"][4]
    psSq = P["PS"][5]
    H, M, NE, EH = 2048, 64, 8, 1024
    t0 = half * TC
    tg = f"lf{half}"
    _p = ExitStack()
    bencc = _p.enter_context(nc.sbuf_tensor(tg + "be", [128, 16], F32))
    bs2cc = _p.enter_context(nc.sbuf_tensor(tg + "b2", [64, 1], F32))
    bc2sc = _p.enter_context(nc.sbuf_tensor(tg + "bc", [128, 16], F32))
    lgb = _p.enter_context(nc.sbuf_tensor(tg + "lg", [128, 16], F32))
    bdrow = _p.enter_context(nc.sbuf_tensor(tg + "bd", [1, D], BF16))
    wrb = _p.enter_context(nc.sbuf_tensor(tg + "wr", [64, NE], BF16))
    brr = _p.enter_context(nc.sbuf_tensor(tg + "br", [1, NE], BF16))
    rtc = _p.enter_context(nc.sbuf_tensor(tg + "rt", [128, 8], F32))
    grt = _p.enter_context(nc.sbuf_tensor(tg + "gr8", [128, 16], F32))
    gbt = _p.enter_context(nc.sbuf_tensor(tg + "gb", [128, 8], BF16))
    gtmp = _p.enter_context(nc.sbuf_tensor(tg + "gt", [8, 128], BF16))
    ws2cb = _p.enter_context(nc.sbuf_tensor(tg + "w2c", [128, 16 * 64], BF16))
    wc2sb = _p.enter_context(nc.sbuf_tensor(tg + "wcs", [64, H], BF16))
    wdecb = _p.enter_context(nc.sbuf_tensor(tg + "wd", [128, 16 * 1024], BF16))
    eo = _p.enter_context(nc.sbuf_tensor(tg + "eo", [64, TC], BF16))
    psG = _p.enter_context(nc.psum_tensor(tg + "pG", [128, 1024], BF16))
    psY = _p.enter_context(nc.psum_tensor(tg + "pY", [128, 512], F32))

    _cm_contb = nc.sbuf_tensor(tg + "ct", [64, TC], BF16); contb = _cm_contb.__enter__()
    _cm_gr = nc.sbuf_tensor(tg + "gr", [1, NE * TC], BF16); gr = _cm_gr.__enter__()
    _cm_xbb = nc.sbuf_tensor(tg + "xbb", [128, 8 * TC], BF16); xbb = _cm_xbb.__enter__()
    _cm_wen = nc.sbuf_tensor(tg + "we", [128, 8 * H], BF16); wencb = _cm_wen.__enter__()
    _cm_spk = nc.sbuf_tensor(tg + "sp", [128, 2 * 512], BF16); spkb = _cm_spk.__enter__()
    _cm_w1b = nc.sbuf_tensor(tg + "w1b", [64, NE * EH], BF16); w1b = _cm_w1b.__enter__()
    _cm_b1r = nc.sbuf_tensor(tg + "b1r", [1, NE * EH], BF16); b1r = _cm_b1r.__enter__()
    _cm_w2b = nc.sbuf_tensor(tg + "w2b", [128, NE * 8 * M], BF16); w2b = _cm_w2b.__enter__()
    _cm_b2r = nc.sbuf_tensor(tg + "b2r", [1, NE * M], BF16); b2r = _cm_b2r.__enter__()
    _cm_cg = nc.sbuf_tensor(tg + "cg", [64, 512], BF16); cg = _cm_cg.__enter__()
    _cm_hsb = nc.sbuf_tensor(tg + "hs", [128, 2 * 512], BF16); hsb = _cm_hsb.__enter__()

    V0, T0_ = c0.get("v"), c0.get("t")
    ST0_, SG0_ = c0.get("st"), c0.get("sg")

    def p_loads(sync):
        sync.wait_ge(sems["v"], V0)
        sync.wait_ge(sems["t"], T0_)
        sync.wait_ge(sems["st"], ST0_)
        sync.wait_ge(sems["sg"], SG0_)
        for hc in range(16):
            sync.dma_start(out=bencc[:, hc:hc + 1],
                           in_=I.benc[hc * 128:(hc + 1) * 128].rearrange("(p o) -> p o", o=1)
                           ).then_inc(sems["dh"], 16)
            sync.dma_start(out=bc2sc[:, hc:hc + 1],
                           in_=I.bc2s[hc * 128:(hc + 1) * 128].rearrange("(p o) -> p o", o=1)
                           ).then_inc(sems["dh"], 16)
            c0.inc("dh", 32)
        sync.dma_start(out=bs2cc[:, :],
                       in_=I.bs2c.rearrange("(p o) -> p o", o=1)).then_inc(sems["dh"], 16)
        c0.inc("dh", 16)
        for cc in range(8):
            sync.dma_start(out=lgb[:, cc:cc + 1],
                           in_=I.lng[cc * 128:(cc + 1) * 128].rearrange("(p o) -> p o", o=1)
                           ).then_inc(sems["dh"], 16)
            sync.dma_start(out=lgb[:, 8 + cc:9 + cc],
                           in_=I.lnb[cc * 128:(cc + 1) * 128].rearrange("(p o) -> p o", o=1)
                           ).then_inc(sems["dh"], 16)
            c0.inc("dh", 32)
    block.sync(p_loads)

    def p_gploads(gpsimd):
        gpsimd.wait_ge(sems["v"], V0)
        gpsimd.wait_ge(sems["t"], T0_)
        gpsimd.wait_ge(sems["st"], ST0_)
        gpsimd.wait_ge(sems["sg"], SG0_)
        for cc in range(8):
            gpsimd.dma_start(out=xbb[:, cc * TC:(cc + 1) * TC],
                             in_=S.x3park[cc * 128:(cc + 1) * 128, t0:t0 + TC]
                             ).then_inc(sems["ds"], 16)
            gpsimd.dma_start(out=wencb[:, cc * H:(cc + 1) * H],
                             in_=I.wenc[cc * 128:(cc + 1) * 128, :]).then_inc(sems["ds"], 16)
            c0.inc("ds", 32)
        for hc in range(16):
            gpsimd.dma_start(out=ws2cb[:, hc * 64:(hc + 1) * 64],
                             in_=I.ws2c[hc * 128:(hc + 1) * 128, :]).then_inc(sems["ds"], 16)
            gpsimd.dma_start(out=wdecb[:, hc * 1024:(hc + 1) * 1024],
                             in_=I.wdec[hc * 128:(hc + 1) * 128, :]).then_inc(sems["ds"], 16)
            c0.inc("ds", 32)
        gpsimd.dma_start(out=wc2sb[:, :], in_=I.wc2s[:, :]).then_inc(sems["ds"], 16)
        gpsimd.dma_start(out=wrb[:, :], in_=I.wr[:, :]).then_inc(sems["ds"], 16)
        gpsimd.dma_start(out=brr[:, :], in_=I.br.rearrange("(o e) -> o e", o=1)
                         ).then_inc(sems["ds"], 16)
        gpsimd.dma_start(out=bdrow[:, :], in_=I.bdec.rearrange("(o d) -> o d", o=1)
                         ).then_inc(sems["ds"], 16)
        c0.inc("ds", 64)
        for e in range(NE):
            gpsimd.dma_start(out=w1b[:, e * EH:(e + 1) * EH],
                             in_=I.w1e[e, :, :]).then_inc(sems["ds"], 16)
            gpsimd.dma_start(out=b1r[:, e * EH:(e + 1) * EH],
                             in_=I.b1e[e, :].rearrange("(o h) -> o h", o=1)
                             ).then_inc(sems["ds"], 16)
            gpsimd.dma_start(out=b2r[:, e * M:(e + 1) * M],
                             in_=I.b2e[e, :].rearrange("(o m) -> o m", o=1)
                             ).then_inc(sems["ds"], 16)
            c0.inc("ds", 48)
            for hc8 in range(8):
                gpsimd.dma_start(out=w2b[:, (e * 8 + hc8) * M:(e * 8 + hc8 + 1) * M],
                                 in_=I.w2e[e, hc8 * 128:(hc8 + 1) * 128, :]
                                 ).then_inc(sems["ds"], 16)
                c0.inc("ds", 16)
    block.gpsimd(p_gploads)
    DH0 = c0.get("dh")
    DS0 = c0.get("ds")
    for eng in ("t", "s", "v"):
        pr.wait(eng, "dh", DH0)
        pr.wait(eng, "ds", DS0)
    pr.run("v", lambda v: v.memset(K.epsb[:, :], EPS))

    psCt = [psE, psH, psSum, psSq]
    psSpk = [psA, psB, psY]
    sig_done = {}
    cont_mm_done = {}
    spk_gi = 0
    pend_cont = [None]

    def flush_cont():
        if pend_cont[0] is None:
            return
        g, hc_, tci_, par_ = pend_cont[0]
        pend_cont[0] = None
        pr.wait("t", "s", sig_done[g])
        def mkcont(hc=hc_, tci=tci_, par=par_):
            def f(tensor):
                return tensor.matmul(
                    psCt[tci][0:64, :], ws2cb[:, hc * 64:(hc + 1) * 64],
                    spkb[:, par * 512:(par + 1) * 512],
                    start=(hc == 0), stop=(hc == 15))
            return f
        cont_mm_done[(hc_, tci_)] = pr.run("t", mkcont())

    for hc in range(16):
        for tci in range(4):
            bank = psSpk[spk_gi % 3]
            if spk_gi >= 3:
                pr.wait("t", "s", sig_done[spk_gi - 3])
            def mkspk(hc=hc, tci=tci, bank=bank):
                def f(tensor):
                    mm = None
                    for ic in range(8):
                        mm = tensor.matmul(
                            bank[:, :],
                            wencb[:, ic * H + hc * 128: ic * H + (hc + 1) * 128],
                            xbb[:, ic * TC + tci * 512: ic * TC + (tci + 1) * 512],
                            start=(ic == 0), stop=(ic == 7))
                    return mm
                return f
            td = pr.run("t", mkspk())
            pr.wait("s", "t", td)
            sd = pr.run("s", (lambda hc=hc, bank=bank, par=spk_gi % 2: lambda s:
                              s.activation(spkb[:, par * 512:(par + 1) * 512],
                                           bank[:, :], AF.Sigmoid,
                                           bias=bencc[:, hc:hc + 1], scale=1.0))())
            sig_done[spk_gi] = sd
            flush_cont()
            pend_cont[0] = (spk_gi, hc, tci, spk_gi % 2)
            spk_gi += 1
    flush_cont()
    for tci in range(4):
        pr.wait("s", "t", cont_mm_done[(15, tci)])
        pr.run("s", (lambda tci=tci: lambda s: s.activation(
            contb[:, tci * 512:(tci + 1) * 512], psCt[tci][0:64, :], AF.Identity,
            bias=bs2cc[:, :], scale=1.0))())
    S_CONT = c0.get("s")

    pr.wait("t", "s", S_CONT)
    H0B = c0.get("h0")
    gstore_marks = []
    for tt in range(16):
        def mklog(tt=tt):
            def f(tensor):
                tensor.matmul(psC[0:128, 0:NE], contb[:, tt * 128:(tt + 1) * 128],
                              wrb[:, :], start=True, stop=False)
                return tensor.matmul(psC[0:128, 0:NE], K.or512[:, 0:128], brr[:, :],
                                     start=False, stop=True)
            return f
        td = pr.run("t", mklog())
        pr.wait("v", "t", td)
        pr.run("v", lambda v: v.tensor_reduce(out=rtc[:, 0:1], in_=psC[0:128, 0:NE],
                                              axis=mybir.AxisListType.X, op=AluOpType.max))
        pr.run("v", lambda v: v.tensor_scalar(out=rtc[:, 1:2], in0=rtc[:, 0:1],
                                              scalar1=-1.0, scalar2=None,
                                              op0=AluOpType.mult))
        V_NM = c0.get("v")
        pr.wait("s", "v", V_NM)
        sd = pr.run("s", lambda s: s.activation(grt[:, 0:8], psC[0:128, 0:NE], AF.Exp,
                                                bias=rtc[:, 1:2], scale=1.0))
        pr.wait("v", "s", sd)
        pr.run("v", lambda v: v.tensor_reduce(out=rtc[:, 2:3], in_=grt[:, 0:8],
                                              axis=mybir.AxisListType.X, op=AluOpType.add))
        pr.run("v", lambda v: v.reciprocal(out=rtc[:, 3:4], in_=rtc[:, 2:3]))
        pr.run("v", lambda v: v.tensor_scalar(out=grt[:, 0:8], in0=grt[:, 0:8],
                                              scalar1=rtc[:, 3:4], scalar2=None,
                                              op0=AluOpType.mult))
        pr.run("v", lambda v: v.tensor_reduce(out=rtc[:, 4:5], in_=grt[:, 0:8],
                                              axis=mybir.AxisListType.X, op=AluOpType.max))
        pr.run("v", lambda v: v.tensor_scalar(out=grt[:, 8:16], in0=grt[:, 0:8],
                                              scalar1=rtc[:, 4:5], scalar2=None,
                                              op0=AluOpType.is_ge))
        pr.run("v", lambda v: v.scalar_tensor_tensor(out=grt[:, 8:16], in0=grt[:, 8:16],
                                                     scalar=-1e9, in1=grt[:, 0:8],
                                                     op0=AluOpType.mult, op1=AluOpType.add))
        pr.run("v", lambda v: v.tensor_reduce(out=rtc[:, 5:6], in_=grt[:, 8:16],
                                              axis=mybir.AxisListType.X, op=AluOpType.max))
        pr.run("v", lambda v: v.tensor_tensor(out=rtc[:, 6:7], in0=rtc[:, 4:5],
                                              in1=rtc[:, 5:6], op=AluOpType.add))
        pr.run("v", lambda v: v.reciprocal(out=rtc[:, 7:8], in_=rtc[:, 6:7]))
        pr.run("v", lambda v: v.tensor_scalar(out=grt[:, 8:16], in0=grt[:, 0:8],
                                              scalar1=rtc[:, 5:6], scalar2=None,
                                              op0=AluOpType.is_ge))
        pr.run("v", lambda v: v.tensor_scalar(out=grt[:, 0:8], in0=grt[:, 0:8],
                                              scalar1=rtc[:, 7:8], scalar2=None,
                                              op0=AluOpType.mult))
        vg = pr.run("v", lambda v: v.tensor_tensor(out=gbt[:, 0:8], in0=grt[:, 0:8],
                                                   in1=grt[:, 8:16], op=AluOpType.mult))
        pr.wait("t", "v", vg)
        tg_ = pr.run("t", lambda t: t.transpose(psG[0:8, 0:128], gbt[:, 0:8],
                                                K.i128[:, :]))
        pr.wait("v", "t", tg_)
        if tt >= 1:
            pr.wait("v", "h0", H0B + 16 * tt)
        pr.run("v", lambda v: v.tensor_copy(out=gtmp[:, :], in_=psG[0:8, 0:128]))
        gstore_marks.append(c0.get("v"))

    def sync_gstore(sync):
        for tt in range(16):
            sync.wait_ge(sems["v"], gstore_marks[tt])
            sync.dma_start(out=S.gdram[:, tt * 128:(tt + 1) * 128],
                           in_=gtmp[:, :]).then_inc(sems["h0"], 16)
            c0.inc("h0", 16)
        sync.wait_ge(sems["h0"], H0B + 256)
        sync.dma_start(out=gr[:, :],
                       in_=S.gdram.rearrange("e t -> (e t)").rearrange("(o n) -> o n", o=1)
                       ).then_inc(sems["h0"], 16)
        c0.inc("h0", 16)
    block.sync(sync_gstore)
    H0_GR = c0.get("h0")

    pr.wait("t", "h0", H0_GR)
    cg_done = {}
    eo_done = {}
    relu_done = {}
    psHrot = [psH, psY]
    ehg = 0
    pend_w2 = [None]

    def flush_w2():
        if pend_w2[0] is None:
            return
        g, e_, eh_, par_, start_ = pend_w2[0]
        pend_w2[0] = None
        pr.wait("t", "s", relu_done[g])
        def mkw2(e=e_, eh=eh_, par=par_, start=start_):
            def f(tensor):
                return tensor.matmul(
                    psE[0:64, :], w2b[:, (e * 8 + eh) * M:(e * 8 + eh + 1) * M],
                    hsb[:, par * 512:(par + 1) * 512],
                    start=start, stop=False)
            return f
        pr.run("t", mkw2())

    def emit_b2(e_, tci_, stop_):
        def f(tensor):
            return tensor.matmul(psE[0:64, :], b2r[:, e_ * M:(e_ + 1) * M],
                                 gr[:, e_ * TC + tci_ * 512: e_ * TC + (tci_ + 1) * 512],
                                 start=False, stop=stop_)
        pr.run("t", f)

    for tci in range(4):
        for e in range(NE):
            if (tci, e) != (0, 0):
                prev = cg_done.get((tci, e - 1)) if e > 0 else cg_done.get((tci - 1, NE - 1))
                pr.wait("t", "v", prev or 0)
            def mkgb(e=e, tci=tci):
                def f(tensor):
                    return tensor.matmul(psC[0:64, :], K.oc64[:, :],
                                         gr[:, e * TC + tci * 512: e * TC + (tci + 1) * 512],
                                         start=True, stop=True)
                return f
            tb = pr.run("t", mkgb())
            pr.wait("v", "t", tb)
            vd = pr.run("v", (lambda tci=tci: lambda v: v.tensor_tensor(
                out=cg[:, :], in0=psC[0:64, :], in1=contb[:, tci * 512:(tci + 1) * 512],
                op=AluOpType.mult))())
            cg_done[(tci, e)] = vd
            pr.wait("t", "v", vd)
            for eh in range(8):
                par = ehg % 2
                bank = psHrot[par]
                if ehg >= 2:
                    pr.wait("t", "s", relu_done[ehg - 2])
                def mkh(e=e, eh=eh, tci=tci, bank=bank):
                    def f(tensor):
                        tensor.matmul(bank[:, :],
                                      w1b[:, e * EH + eh * 128: e * EH + (eh + 1) * 128],
                                      cg[:, :], start=True, stop=False)
                        return tensor.matmul(
                            bank[:, :], b1r[:, e * EH + eh * 128: e * EH + (eh + 1) * 128],
                            gr[:, e * TC + tci * 512: e * TC + (tci + 1) * 512],
                            start=False, stop=True)
                    return f
                th = pr.run("t", mkh())
                pr.wait("s", "t", th)
                sd = pr.run("s", (lambda par=par, bank=bank: lambda s: s.activation(
                    hsb[:, par * 512:(par + 1) * 512], bank[:, :], AF.Relu,
                    scale=1.0))())
                relu_done[ehg] = sd
                flush_w2()
                pend_w2[0] = (ehg, e, eh, par, e == 0 and eh == 0)
                ehg += 1
            flush_w2()
            emit_b2(e, tci, e == NE - 1)
        TEOD = c0.get("t")
        pr.wait("v", "t", TEOD)
        ed = pr.run("v", (lambda tci=tci: lambda v: v.tensor_copy(
            out=eo[:, tci * 512:(tci + 1) * 512], in_=psE[0:64, :]))())
        eo_done[tci] = ed
    V_EO = c0.get("v")

    _cm_hsb.__exit__(None, None, None)
    _cm_cg.__exit__(None, None, None)
    _cm_b2r.__exit__(None, None, None)
    _cm_w2b.__exit__(None, None, None)
    _cm_b1r.__exit__(None, None, None)
    _cm_w1b.__exit__(None, None, None)
    _cm_spk.__exit__(None, None, None)
    _cm_wen.__exit__(None, None, None)
    _cm_xbb.__exit__(None, None, None)
    _cm_gr.__exit__(None, None, None)
    _cm_contb.__exit__(None, None, None)
    _cm_dec = nc.sbuf_tensor(tg + "dec", [128, 8 * TC], BF16); dec = _cm_dec.__enter__()
    _cm_xb = nc.sbuf_tensor(tg + "xb", [128, 8 * TC], F32); xb = _cm_xb.__enter__()
    _cm_smtc = nc.sbuf_tensor(tg + "sm", [128, 16 * 512], BF16); smtc = _cm_smtc.__enter__()

    def p_xloads(sync):
        sync.wait_ge(sems["v"], V_EO)
        for cc in range(8):
            sync.dma_start(out=xb[:, cc * TC:(cc + 1) * TC],
                           in_=S.x3park[cc * 128:(cc + 1) * 128, t0:t0 + TC]
                           ).then_inc(sems["dh"], 16)
            c0.inc("dh", 16)
    block.sync(p_xloads)
    DH1 = c0.get("dh")

    evac_marks = []
    dec_gi = 0
    psSm = [psC, psH, psY]
    for tci in range(4):
        sm_marks = {}
        for hc in range(16):
            bank = psSm[hc % 3]
            if hc >= 3:
                pr.wait("t", "s", sm_marks[hc - 3])
            def mksm(hc=hc, tci=tci, bank=bank):
                def f(tensor):
                    return tensor.matmul(bank[0:128, :], wc2sb[:, hc * 128:(hc + 1) * 128],
                                         eo[:, tci * 512:(tci + 1) * 512],
                                         start=True, stop=True)
                return f
            tsm = pr.run("t", mksm())
            pr.wait("s", "t", tsm)
            if tci > 0:
                pr.wait("s", "t", c0.get("%s_dec_%d" % (tg, tci - 1), 0))
            sd = pr.run("s", (lambda hc=hc, bank=bank: lambda s: s.activation(
                smtc[:, hc * 512:(hc + 1) * 512], bank[0:128, :], AF.Sigmoid,
                bias=bc2sc[:, hc:hc + 1], scale=1.0))())
            sm_marks[hc] = sd
        pr.wait("t", "s", sm_marks[15])
        for oc in range(8):
            bank = psA if dec_gi % 2 == 0 else psB
            if dec_gi >= 2:
                pr.wait("t", "v", evac_marks[dec_gi - 2])
            def mkdec(oc=oc, bank=bank):
                def f(tensor):
                    for hc in range(16):
                        tensor.matmul(bank[:, :],
                                      wdecb[:, hc * 1024 + oc * 128: hc * 1024 + (oc + 1) * 128],
                                      smtc[:, hc * 512:(hc + 1) * 512],
                                      start=(hc == 0), stop=False)
                    return tensor.matmul(bank[:, :], bdrow[:, oc * 128:(oc + 1) * 128],
                                         K.or512[:, :], start=False, stop=True)
                return f
            td = pr.run("t", mkdec())
            pr.wait("v", "t", td)
            vd = pr.run("v", (lambda oc=oc, tci=tci, bank=bank: lambda v: v.tensor_copy(
                out=dec[:, oc * TC + tci * 512: oc * TC + (tci + 1) * 512],
                in_=bank[:, :]))())
            evac_marks.append(vd)
            dec_gi += 1
        c0.v["%s_dec_%d" % (tg, tci)] = c0.get("t")

    _cm_smtc.__exit__(None, None, None)
    _cm_scr = nc.sbuf_tensor(tg + "sc", [128, 4 * 512], BF16); scr = _cm_scr.__enter__()
    _cm_rw = nc.sbuf_tensor(tg + "rw", [1, 4 * TC], F32); rows_b = _cm_rw.__enter__()
    rows = [rows_b[:, k * TC:(k + 1) * TC] for k in range(4)]
    _cm_bm = nc.sbuf_tensor(tg + "bm", [128, TC], F32); bcm = _cm_bm.__enter__()
    _cm_bs = nc.sbuf_tensor(tg + "bs", [128, TC], F32); bcs = _cm_bs.__enter__()
    _cm_xo = nc.sbuf_tensor(tg + "xo", [128, TC], F32); xo2 = _cm_xo.__enter__()

    decbuf = [dec[:, oc * TC:(oc + 1) * TC] for oc in range(8)]
    pr.wait("v", "dh", DH1)
    emit_ln_tail(pr, decbuf, scr, rows, bcm, bcs,
                 lambda ch: xo2[:, :],
                 psSum, psSq, psC, K.ocb, K.oc1f, K.epsb,
                 gcol=lgb[:, 0:8], bcol=lgb[:, 8:16],
                 pre_wait=("t", c0.get("t")),
                 post_add_ap=lambda ch: xb[:, ch * TC:(ch + 1) * TC])
    rbuf2 = [xb[:, oc * TC:(oc + 1) * TC] for oc in range(8)]
    ST0 = c0.get("st")

    def chunk_prewait(ch):
        if ch >= 1:
            return ("st", ST0 + 16 * ch)
        return None
    ready = emit_ln_tail(pr, rbuf2, scr, rows, bcm, bcs,
                         lambda ch: xo2[:, :],
                         psSum, psSq, psC, K.ocb, K.oc1f, K.epsb,
                         chunk_prewait=chunk_prewait)

    def p_outstores(sync):
        for ch in range(8):
            sync.wait_ge(sems["v"], ready[ch])
            sync.dma_start(out=I.xo[ch * 128:(ch + 1) * 128, t0:t0 + TC],
                           in_=xo2[:, :]).then_inc(sems["st"], 16)
            c0.inc("st", 16)
    block.sync(p_outstores)
    pr.barrier()
    pr.wait("v", "st", c0.get("st"))
    pr.wait("t", "st", c0.get("st"))
    _cm_xo.__exit__(None, None, None)
    _cm_bs.__exit__(None, None, None)
    _cm_bm.__exit__(None, None, None)
    _cm_rw.__exit__(None, None, None)
    _cm_scr.__exit__(None, None, None)
    _cm_xb.__exit__(None, None, None)
    _cm_dec.__exit__(None, None, None)
    _p.close()



_PROG = None


def kernel(**inputs):
    global _PROG
    z = inputs
    if _PROG is None:
        _PROG = build_fused()
    nc = _PROG
    consts = host_consts()
    x = np.ascontiguousarray(np.asarray(z["L0_attn_x"], np.float32))
    w1T = np.ascontiguousarray(np.asarray(z["L0_ffn_w1"], np.float32).T)
    w2T = np.ascontiguousarray(np.asarray(z["L0_ffn_w2"], np.float32).T)
    cwf = np.ascontiguousarray(np.asarray(z["L0_ffn_conv_w"], np.float32)[:, 0, :])
    in_maps = []
    for core in range(NCORE):
        b = core // 2
        m = dict(consts)
        m.update({
            "xT": np.ascontiguousarray(x[b].T),
            "wq": z["L0_attn_Wq"], "bq": z["L0_attn_bq"],
            "wk": z["L0_attn_Wk"], "bk": z["L0_attn_bk"],
            "wv": z["L0_attn_Wv"], "bv": z["L0_attn_bv"],
            "wo": z["L0_attn_Wo"], "bo": z["L0_attn_bo"],
            "w1T": w1T, "cw": cwf, "cb": z["L0_ffn_conv_b"], "w2T": w2T,
            "wqkv": z["L1_attn_w_qkv"], "wproj": z["L1_attn_w_proj"],
            "wenc": z["L1_ffn_W_enc"], "benc": z["L1_ffn_b_enc"],
            "ws2c": z["L1_ffn_W_s2c"], "bs2c": z["L1_ffn_b_s2c"],
            "wr": z["L1_ffn_Wr"], "br": z["L1_ffn_br"],
            "w1e": z["L1_ffn_W1"], "b1e": z["L1_ffn_b1"],
            "w2e": z["L1_ffn_W2"], "b2e": z["L1_ffn_b2"],
            "wc2s": z["L1_ffn_W_c2s"], "bc2s": z["L1_ffn_b_c2s"],
            "wdec": z["L1_ffn_W_dec"], "bdec": z["L1_ffn_b_dec"],
            "lng": z["L1_ffn_ln_g"], "lnb": z["L1_ffn_ln_b"],
        })
        in_maps.append(m)
    res = run_bass_kernel_spmd(nc, in_maps, core_ids=list(range(NCORE)))
    out = np.zeros((B, T, D), np.float32)
    for b in range(B):
        out[b] = res.results[2 * b]["xo"].T
    return out


# version 12 (final)
# speedup vs baseline: 1.0449x; 1.0219x over previous
from contextlib import ExitStack

import numpy as np
import ml_dtypes
import concourse.bass as bass
import concourse.mybir as mybir
from concourse.bass_utils import run_bass_kernel_spmd
from concourse.alu_op_type import AluOpType

F32 = mybir.dt.float32
BF16 = mybir.dt.bfloat16
AF = mybir.ActivationFunctionType
NP_BF16 = ml_dtypes.bfloat16

B, T, D = 4, 4096, 1024
NCORE = 8
NH, HD = 16, 64
TC = 2048
NEG = -30000.0
MAGIC = 12582912.0
EPS = 1e-5


def host_rope_R():
    theta = 1.0 / 10000.0 ** (np.arange(0, 32, 2, dtype=np.float64) / 32.0)
    out = np.zeros((8, 128, 128), np.float32)
    for h in range(NH):
        c, off = divmod(h * 64, 128)
        cs = np.cos(h * theta).astype(np.float32)
        sn = np.sin(h * theta).astype(np.float32)
        blk = np.zeros((64, 64), np.float32)
        for half in (0, 32):
            for j in range(16):
                e, o = half + 2 * j, half + 2 * j + 1
                blk[e, half + j] = cs[j]
                blk[o, half + j] = -sn[j]
                blk[e, half + 16 + j] = sn[j]
                blk[o, half + 16 + j] = cs[j]
        out[c, off:off + 64, off:off + 64] = blk
    return out.astype(NP_BF16)


def host_consts():
    tri = np.zeros((128, 512), np.float32)
    tri[:, :128] = np.where(np.arange(128)[:, None] > np.arange(128)[None, :],
                            NEG, 0.0)
    hm = np.zeros((5, 128, 8), np.float32)
    for t in range(4):
        for sub in range(2):
            hm[t, sub * 64:(sub + 1) * 64, 2 * t + sub] = 1.0
    hm[4, :, :] = 0.5
    return {
        "ropeR": host_rope_R(),
        "tri": tri.astype(NP_BF16),
        "i128": np.eye(128, dtype=NP_BF16),
        "hm": hm.astype(NP_BF16),
        "i8": np.eye(8, dtype=np.float32),
        "ones8": np.ones((8, 128), np.float32),
        "onesf": np.ones((1, 64), dtype=NP_BF16),
        "oc1f": np.ones((1, 128), np.float32),
        "c127": np.full((1, 128), 127.0, np.float32),
        "ocf": np.ones((128, 1), np.float32),
        "ocb": np.ones((128, 1), dtype=NP_BF16),
        "or512": np.ones((1, 512), dtype=NP_BF16),
        "oc64": np.ones((1, 64), dtype=NP_BF16),
    }



class Cnt:
    def __init__(self):
        self.v = {}

    def inc(self, name, amt=1):
        self.v[name] = self.v.get(name, 0) + amt
        return self.v[name]

    def get(self, name, default=0):
        return self.v.get(name, default)


def _emit(engine, prog, sems):
    for item in prog:
        if item[0] == "wait":
            engine.wait_ge(sems[item[1]], item[2])
        else:
            inst = item[1](engine)
            if item[2] is not None:
                inst.then_inc(sems[item[2][0]], item[2][1])


class Prog:

    def __init__(self, nc):
        self.nc = nc
        self.c = Cnt()
        self.progs = {"t": [], "s": [], "v": []}

    def wait(self, eng, sem, val):
        if val > 0:
            self.progs[eng].append(("wait", sem, val))

    def run(self, eng, fn):
        if eng in ("s", "v") and self.c.get(eng) > 0:
            self.progs[eng].append(("wait", eng, self.c.get(eng)))
        self.c.inc(eng, 1)
        self.progs[eng].append(("op", fn, (eng, 1)))
        return self.c.get(eng)

    def barrier(self):
        t, s, v = self.c.get("t"), self.c.get("s"), self.c.get("v")
        self.wait("t", "s", s); self.wait("t", "v", v)
        self.wait("s", "t", t); self.wait("s", "v", v)
        self.wait("v", "t", t); self.wait("v", "s", s)
        return t, s, v

    def emit(self, block, sems):
        block.tensor(lambda e: _emit(e, self.progs["t"], sems))
        block.scalar(lambda e: _emit(e, self.progs["s"], sems))
        block.vector(lambda e: _emit(e, self.progs["v"], sems))


class Ctx:
    pass



def emit_ln_tail(pr, rbuf, scr, rows, bcm, bcs, xout_ap, psSum, psSq, psBc,
                 onescolb, ones1f, epsb, gcol=None, bcol=None, nchunk=8, ntc=4,
                 pre_wait=None, chunk_prewait=None, post_add_ap=None):
    c = pr.c
    CW = 512
    if pre_wait is not None:
        pr.wait("v", pre_wait[0], pre_wait[1])
    pr.wait("s", "v", c.get("v"))
    if pre_wait is not None:
        pr.wait("s", pre_wait[0], pre_wait[1])
    mm_done = {}
    for tc_i in range(ntc):
        for ch in range(nchunk):
            slot = ch % 2
            if (tc_i, ch) >= (0, 2) and mm_done:
                key = (tc_i, ch - 2) if ch >= 2 else (tc_i - 1, nchunk - 2 + ch)
                if key in mm_done:
                    pr.wait("v", "t", mm_done[key])
                    pr.wait("s", "t", mm_done[key])
            cv = pr.run("v", (lambda ch=ch, tc_i=tc_i, slot=slot: lambda v: v.tensor_copy(
                out=scr[:, (2 * slot) * CW:(2 * slot + 1) * CW],
                in_=rbuf[ch][:, tc_i * CW:(tc_i + 1) * CW]))())
            sd = pr.run("s", (lambda ch=ch, tc_i=tc_i, slot=slot: lambda s: s.activation(
                scr[:, (2 * slot + 1) * CW:(2 * slot + 2) * CW],
                rbuf[ch][:, tc_i * CW:(tc_i + 1) * CW], AF.Square, scale=1.0))())
            pr.wait("t", "v", cv)
            pr.wait("t", "s", sd)
            def mk_mm(ch=ch, slot=slot, first=(ch == 0), last=(ch == nchunk - 1)):
                def f(tensor):
                    tensor.matmul(psSum[0:1, :], onescolb[:, :],
                                  scr[:, (2 * slot) * CW:(2 * slot + 1) * CW],
                                  start=first, stop=last)
                    return tensor.matmul(psSq[0:1, :], onescolb[:, :],
                                         scr[:, (2 * slot + 1) * CW:(2 * slot + 2) * CW],
                                         start=first, stop=last)
                return f
            mm_done[(tc_i, ch)] = pr.run("t", mk_mm())
        pr.wait("v", "t", mm_done[(tc_i, nchunk - 1)])
        pr.run("v", (lambda tc_i=tc_i: lambda v: v.tensor_copy(
            out=rows[0][:, tc_i * CW:(tc_i + 1) * CW], in_=psSum[0:1, :]))())
        V_EV = pr.run("v", (lambda tc_i=tc_i: lambda v: v.tensor_copy(
            out=rows[1][:, tc_i * CW:(tc_i + 1) * CW], in_=psSq[0:1, :]))())
        if tc_i < ntc - 1:
            pr.wait("t", "v", V_EV)
    n = float(nchunk * 128)
    pr.run("v", lambda v: v.tensor_scalar(out=rows[0][:, :], in0=rows[0][:, :],
                                          scalar1=1.0 / n, scalar2=None,
                                          op0=AluOpType.mult))
    pr.run("v", lambda v: v.tensor_scalar(out=rows[1][:, :], in0=rows[1][:, :],
                                          scalar1=1.0 / n, scalar2=None,
                                          op0=AluOpType.mult))
    pr.run("v", lambda v: v.tensor_tensor(out=rows[2][:, :], in0=rows[0][:, :],
                                          in1=rows[0][:, :], op=AluOpType.mult))
    pr.run("v", lambda v: v.tensor_tensor(out=rows[1][:, :], in0=rows[1][:, :],
                                          in1=rows[2][:, :], op=AluOpType.subtract))
    pr.run("v", lambda v: v.memset(epsb[:, :], EPS))
    V_VAR = c.get("v")
    pr.wait("s", "v", V_VAR)
    S_SQ = pr.run("s", lambda s: s.activation(rows[2][:, :], rows[1][:, :],
                                              AF.Sqrt, bias=epsb[:, :], scale=1.0))
    pr.wait("t", "s", S_SQ)
    for tc_i in range(ntc):
        td = pr.run("t", (lambda tc_i=tc_i: lambda t: t.matmul(
            psBc[0:128, :], ones1f[:, :],
            rows[2][:, tc_i * CW:(tc_i + 1) * CW],
            start=True, stop=True))())
        pr.wait("v", "t", td)
        pr.run("v", (lambda tc_i=tc_i: lambda v: v.reciprocal(
            out=bcs[:, tc_i * CW:(tc_i + 1) * CW], in_=psBc[0:128, :]))())
        pr.wait("t", "v", c.get("v"))
        td = pr.run("t", (lambda tc_i=tc_i: lambda t: t.matmul(
            psBc[0:128, :], ones1f[:, :],
            rows[0][:, tc_i * CW:(tc_i + 1) * CW],
            start=True, stop=True))())
        pr.wait("v", "t", td)
        pr.run("v", (lambda tc_i=tc_i: lambda v: v.tensor_copy(
            out=bcm[:, tc_i * CW:(tc_i + 1) * CW], in_=psBc[0:128, :]))())
        pr.run("v", (lambda tc_i=tc_i: lambda v: v.tensor_tensor(
            out=bcm[:, tc_i * CW:(tc_i + 1) * CW],
            in0=bcm[:, tc_i * CW:(tc_i + 1) * CW],
            in1=bcs[:, tc_i * CW:(tc_i + 1) * CW], op=AluOpType.mult))())
        if tc_i < ntc - 1:
            pr.wait("t", "v", c.get("v"))
    ready = []
    for ch in range(nchunk):
        if chunk_prewait is not None:
            pw = chunk_prewait(ch)
            if pw is not None:
                pr.wait("v", pw[0], pw[1])
        xo = xout_ap(ch)
        pr.run("v", (lambda ch=ch, xo=xo: lambda v: v.tensor_tensor(
            out=xo, in0=rbuf[ch][:, :], in1=bcs[:, :], op=AluOpType.mult))())
        pr.run("v", (lambda ch=ch, xo=xo: lambda v: v.tensor_tensor(
            out=xo, in0=xo, in1=bcm[:, :], op=AluOpType.subtract))())
        if gcol is not None:
            pr.run("v", (lambda ch=ch, xo=xo: lambda v: v.tensor_scalar(
                out=xo, in0=xo, scalar1=gcol[:, ch:ch + 1], scalar2=bcol[:, ch:ch + 1],
                op0=AluOpType.mult, op1=AluOpType.add))())
        if post_add_ap is not None:
            pa = post_add_ap(ch)
            pr.run("v", (lambda xo=xo, pa=pa: lambda v: v.tensor_tensor(
                out=pa, in0=xo, in1=pa, op=AluOpType.add))())
        ready.append(c.get("v"))
    return ready



def emit_absmax_row(pr, xch, nch, sc32, scb, psT, psTR, mrow, i128, sqtmp=None):
    c = pr.c
    if nch == 1:
        pr.run("v", lambda v: v.tensor_copy(out=scb[:, 0:TC], in_=xch(0)))
    else:
        pr.run("v", lambda v: v.tensor_tensor(out=sc32[:, :], in0=xch(0), in1=xch(0),
                                              op=AluOpType.mult))
        for ch in range(1, nch):
            pr.run("v", (lambda ch=ch: lambda v: v.tensor_tensor(
                out=sqtmp[:, :], in0=xch(ch), in1=xch(ch), op=AluOpType.mult))())
            pr.run("v", lambda v: v.tensor_tensor(out=sc32[:, :], in0=sc32[:, :],
                                                  in1=sqtmp[:, :], op=AluOpType.max))
        pr.run("v", lambda v: v.tensor_copy(out=scb[:, 0:TC], in_=sc32[:, :]))
    V_AM = c.get("v")
    pr.wait("t", "v", V_AM)
    for tt in range(16):
        td = pr.run("t", (lambda tt=tt: lambda t: t.transpose(
            psT[0:128, 0:128], scb[:, tt * 128:(tt + 1) * 128], i128[:, :]))())
        pr.wait("v", "t", td)
        pr.run("v", (lambda tt=tt: lambda v: v.tensor_reduce(
            out=sc32[:, tt:tt + 1], in_=psT[0:128, 0:128],
            axis=mybir.AxisListType.X, op=AluOpType.max))())
        vd = pr.run("v", (lambda tt=tt: lambda v: v.tensor_copy(
            out=scb[:, TC + tt:TC + tt + 1], in_=sc32[:, tt:tt + 1]))())
        pr.wait("t", "v", vd)
        td2 = pr.run("t", (lambda tt=tt: lambda t: t.transpose(
            psTR[0:1, 0:128], scb[:, TC + tt:TC + tt + 1], i128[:, :]))())
        pr.wait("v", "t", td2)
        pr.run("v", (lambda tt=tt: lambda v: v.tensor_copy(
            out=mrow[:, tt * 128:(tt + 1) * 128], in_=psTR[0:1, 0:128]))())
    pr.run("v", lambda v: v.tensor_scalar(out=mrow[:, :], in0=mrow[:, :],
                                          scalar1=1e-30, scalar2=None,
                                          op0=AluOpType.max))
    V_PRE = c.get("v")
    pr.wait("s", "v", V_PRE)
    S_RT = pr.run("s", lambda s: s.activation(mrow[:, :], mrow[:, :], AF.Sqrt,
                                              scale=1.0))
    pr.wait("v", "s", S_RT)
    return c.get("v")


def emit_row_bcast(pr, row_sl, dst, lhs, psX, ntc=4):
    for tc_i in range(ntc):
        td = pr.run("t", (lambda tc_i=tc_i: lambda t: t.matmul(
            psX[0:128, :], lhs[:, :], row_sl(tc_i), start=True, stop=True))())
        pr.wait("v", "t", td)
        pr.run("v", (lambda tc_i=tc_i: lambda v: v.tensor_copy(
            out=dst[:, tc_i * 512:(tc_i + 1) * 512], in_=psX[0:128, :]))())
        pr.wait("t", "v", pr.c.get("v"))


def emit_wquant_pass1(pr, c0, wf, wsum, wcol, msc, wsc128, ocf, oc1f, psX,
                      group_sz, ng, n_el, wbase):
    pr.run("v", lambda v: v.memset(wsum[:, :], 0.0))
    absum_done = []
    for gq in range(ng):
        half = gq % 2
        pr.wait("v", "w%d" % half, wbase[half] + (gq // 2 + 1) * group_sz * 16)
        for cc in range(group_sz):
            pr.run("v", (lambda half=half, cc=cc: lambda v: v.tensor_reduce(
                out=wcol[:, :], in_=wf[:, (half * group_sz + cc) * 1024:
                                       (half * group_sz + cc + 1) * 1024],
                axis=mybir.AxisListType.X, op=AluOpType.add,
                apply_absolute_value=True))())
            pr.run("v", lambda v: v.tensor_tensor(out=wsum[:, :], in0=wsum[:, :],
                                                  in1=wcol[:, :], op=AluOpType.add))
        absum_done.append(pr.c.get("v"))
    pr.wait("t", "v", pr.c.get("v"))
    td = pr.run("t", lambda t: t.matmul(psX[0:1, 0:1], wsum[:, :], ocf[:, :],
                                        start=True, stop=True))
    pr.wait("v", "t", td)
    pr.run("v", lambda v: v.tensor_copy(out=msc[:, 0:1], in_=psX[0:1, 0:1]))
    pr.run("v", (lambda n_el=n_el: lambda v: v.tensor_scalar(
        out=msc[:, 1:2], in0=msc[:, 0:1], scalar1=1.0 / n_el, scalar2=1e-5,
        op0=AluOpType.mult, op1=AluOpType.max))())
    pr.run("v", lambda v: v.reciprocal(out=msc[:, 2:3], in_=msc[:, 1:2]))
    V_SC = pr.c.get("v")
    pr.wait("t", "v", V_SC)
    td = pr.run("t", lambda t: t.matmul(psX[0:128, 0:1], oc1f[:, :], msc[:, 2:3],
                                        start=True, stop=True))
    pr.wait("v", "t", td)
    pr.run("v", lambda v: v.tensor_copy(out=wsc128[:, :], in_=psX[0:128, 0:1]))
    return absum_done


def emit_ternarize_group(pr, wf, wq_ap, wsc128, half, group_sz, wtmp):
    for cc in range(group_sz):
        src_ap = wf[:, (half * group_sz + cc) * 1024:(half * group_sz + cc + 1) * 1024]
        dst = wq_ap(cc)
        pr.run("v", (lambda src_ap=src_ap: lambda v: v.tensor_scalar(
            out=wtmp[:, :], in0=src_ap, scalar1=wsc128[:, :], scalar2=1.0,
            op0=AluOpType.mult, op1=AluOpType.min))())
        pr.run("v", lambda v: v.tensor_scalar(
            out=wtmp[:, :], in0=wtmp[:, :], scalar1=-1.0, scalar2=MAGIC,
            op0=AluOpType.max, op1=AluOpType.add))
        pr.run("v", (lambda dst=dst: lambda v: v.tensor_scalar(
            out=dst, in0=wtmp[:, :], scalar1=MAGIC, scalar2=None,
            op0=AluOpType.subtract))())
    return pr.c.get("v")



def build_fused():
    nc = bass.Bass("TRN2", target_bir_lowering=False)

    def din(name, shape, dt=F32):
        return nc.dram_tensor(name, shape, dt, kind="ExternalInput").ap()

    I = Ctx()
    I.xT = din("xT", [D, T])
    I.wq = din("wq", [D, D]); I.bq = din("bq", [D])
    I.wk = din("wk", [D, HD]); I.bk = din("bk", [HD])
    I.wv = din("wv", [D, HD]); I.bv = din("bv", [HD])
    I.wo = din("wo", [D, D]); I.bo = din("bo", [D])
    I.w1T = din("w1T", [D, 4096])
    I.cw = din("cw", [4096, 3]); I.cb = din("cb", [4096])
    I.w2T = din("w2T", [4096, D])
    I.wqkv = din("wqkv", [D, 3 * D])
    I.wproj = din("wproj", [D, D])
    I.wenc = din("wenc", [D, 2048]); I.benc = din("benc", [2048])
    I.ws2c = din("ws2c", [2048, 64]); I.bs2c = din("bs2c", [64])
    I.wr = din("wr", [64, 8]); I.br = din("br", [8])
    I.w1e = din("w1e", [8, 64, 1024]); I.b1e = din("b1e", [8, 1024])
    I.w2e = din("w2e", [8, 1024, 64]); I.b2e = din("b2e", [8, 64])
    I.wc2s = din("wc2s", [64, 2048]); I.bc2s = din("bc2s", [2048])
    I.wdec = din("wdec", [2048, D]); I.bdec = din("bdec", [D])
    I.lng = din("lng", [D]); I.lnb = din("lnb", [D])
    I.ropeR = din("ropeR", [8, 128, 128], BF16)
    I.tri_t = din("tri", [128, 512], BF16)
    I.i128_t = din("i128", [128, 128], BF16)
    I.hm_t = din("hm", [5, 128, 8], BF16)
    I.i8_t = din("i8", [8, 8], F32)
    I.ones8_t = din("ones8", [8, 128], F32)
    I.onesf_t = din("onesf", [1, 64], BF16)
    I.oc1f_t = din("oc1f", [1, 128], F32)
    I.c127_t = din("c127", [1, 128], F32)
    I.ocf_t = din("ocf", [128, 1], F32)
    I.ocb_t = din("ocb", [128, 1], BF16)
    I.or512_t = din("or512", [1, 512], BF16)
    I.oc64_t = din("oc64", [1, 64], BF16)
    I.xo = nc.dram_tensor("xo", [D, T], F32, kind="ExternalOutput").ap()

    S = Ctx()
    S.qd0 = nc.dram_tensor("qd0", [D, T], BF16).ap()
    S.kd0 = nc.dram_tensor("kd0", [HD, T], BF16).ap()
    S.vd0 = nc.dram_tensor("vd0", [T, HD], BF16).ap()
    S.attd = nc.dram_tensor("attd", [D, T], BF16).ap()
    S.x1park = nc.dram_tensor("x1park", [D, T], F32).ap()
    S.hD = nc.dram_tensor("hD", [4096, T], BF16).ap()
    S.csx = nc.dram_tensor("csx", [4096, T], BF16).ap()
    S.qx = nc.dram_tensor("qx", [4096, T], BF16).ap()
    S.x2park = nc.dram_tensor("x2park", [D, T], F32).ap()
    S.qd1 = nc.dram_tensor("qd1", [D, T], BF16).ap()
    S.kd1 = nc.dram_tensor("kd1", [D, T], BF16).ap()
    S.vd1 = nc.dram_tensor("vd1", [T, D], BF16).ap()
    S.x3park = nc.dram_tensor("x3park", [D, T], F32).ap()
    S.gdram = nc.dram_tensor("gdram", [8, TC], BF16).ap()

    pr = Prog(nc)
    c0 = pr.c

    with ExitStack() as _st:
        E = _st.enter_context
        K = Ctx()
        K.i128 = E(nc.sbuf_tensor("i128_s", [128, 128], BF16))
        K.tri = E(nc.sbuf_tensor("tri_s", [128, 512], BF16))
        K.hmb = E(nc.sbuf_tensor("hm_s", [128, 5 * 8], BF16))
        K.i8 = E(nc.sbuf_tensor("i8_s", [8, 8], F32))
        K.ones8 = E(nc.sbuf_tensor("ones8_s", [8, 128], F32))
        K.onesf = E(nc.sbuf_tensor("onesf_s", [1, 64], BF16))
        K.oc1f = E(nc.sbuf_tensor("oc1f_s", [1, 128], F32))
        K.c127 = E(nc.sbuf_tensor("c127_s", [1, 128], F32))
        K.ocf = E(nc.sbuf_tensor("ocf_s", [128, 1], F32))
        K.ocb = E(nc.sbuf_tensor("ocb_s", [128, 1], BF16))
        K.or512 = E(nc.sbuf_tensor("or512_s", [1, 512], BF16))
        K.oc64 = E(nc.sbuf_tensor("oc64_s", [1, 64], BF16))
        K.epsb = E(nc.sbuf_tensor("epsb", [1, 1], F32))
        psA = E(nc.psum_tensor("psA", [128, 512], F32))
        psB = E(nc.psum_tensor("psB", [128, 512], F32))
        psC = E(nc.psum_tensor("psC", [128, 512], F32))
        psD = E(nc.psum_tensor("psD", [128, 512], F32))
        psE = E(nc.psum_tensor("psE", [128, 512], F32))
        psF = E(nc.psum_tensor("psF", [128, 512], F32))
        PS = [psA, psB, psC, psD, psE, psF]
        sem_names = ["dh", "ds", "t", "s", "v", "st", "sg",
                     "w0", "w1", "h0", "h1", "c0", "c1", "sp"]
        sems = {n: E(nc.semaphore("s_" + n)) for n in sem_names}
        block = E(nc.Block())

        P = dict(nc=nc, pr=pr, c0=c0, sems=sems, block=block, K=K, I=I, S=S,
                 PS=PS)

        def sync_consts(sync):
            for dst, srct in ((K.i128, I.i128_t), (K.tri, I.tri_t),
                              (K.i8, I.i8_t), (K.ones8, I.ones8_t),
                              (K.onesf, I.onesf_t), (K.oc1f, I.oc1f_t),
                              (K.c127, I.c127_t), (K.ocf, I.ocf_t),
                              (K.ocb, I.ocb_t), (K.or512, I.or512_t),
                              (K.oc64, I.oc64_t)):
                sync.dma_start(out=dst[:, :], in_=srct[:, :]).then_inc(sems["dh"], 16)
                c0.inc("dh", 16)
            for t5 in range(5):
                sync.dma_start(out=K.hmb[:, t5 * 8:(t5 + 1) * 8],
                               in_=I.hm_t[t5, :, :]).then_inc(sems["dh"], 16)
                c0.inc("dh", 16)
        block.sync(sync_consts)
        pr.wait("t", "dh", c0.get("dh"))
        pr.wait("v", "dh", c0.get("dh"))
        pr.wait("s", "dh", c0.get("dh"))

        for half in range(2):
            emit_p1_half(P, half)
        emit_attn(P, mqa=True, q_src=S.qd0, k_src=S.kd0, v_src=S.vd0,
                  att_dst=S.attd)
        for half in range(2):
            emit_proj_ln_half(P, half, att_src=S.attd, w_src=I.wo,
                              bias_src=I.bo, x_src=I.xT, xpark=S.x1park,
                              tag="p3")
        for half in range(2):
            emit_ffn_a_half(P, half)
        emit_w2quant(P)
        for half in range(2):
            emit_conv_half(P, half)
            emit_bitlin2_ln_half(P, half)
        P["msc2cm"].__exit__(None, None, None)
        P["w2qcm"].__exit__(None, None, None)
        for half in range(2):
            emit_l1_qkv_half(P, half)
        emit_attn(P, mqa=False, q_src=S.qd1, k_src=S.kd1, v_src=S.vd1,
                  att_dst=S.attd)
        for half in range(2):
            emit_proj_ln_half(P, half, att_src=S.attd, w_src=I.wproj,
                              bias_src=None, x_src=S.x2park, xpark=S.x3park,
                              tag="p9")
        for half in range(2):
            emit_l1_ffn_half(P, half)

        pr.emit(block, sems)
        FINAL_ST = c0.get("st")
        FINAL_SG = c0.get("sg")

        def sync_final(sync):
            sync.wait_ge(sems["st"], FINAL_ST)
            sync.wait_ge(sems["sg"], FINAL_SG)
        block.sync(sync_final)
    return nc



def emit_p1_half(P, half):
    nc, pr, c0, sems, block = P["nc"], P["pr"], P["c0"], P["sems"], P["block"]
    K, I, S = P["K"], P["I"], P["S"]
    psA, psB, psC, psD, psE, psF = P["PS"]
    t0 = half * TC
    _p = ExitStack()
    xb = _p.enter_context(nc.sbuf_tensor(f"p1x{half}", [128, 8 * TC], BF16))
    wqb = _p.enter_context(nc.sbuf_tensor(f"p1q{half}", [128, 8 * D], BF16))
    wkf = _p.enter_context(nc.sbuf_tensor(f"p1kf{half}", [128, 8 * HD], F32))
    wkp = _p.enter_context(nc.sbuf_tensor(f"p1kp{half}", [128, 8 * HD], BF16))
    wvb = _p.enter_context(nc.sbuf_tensor(f"p1wv{half}", [128, 8 * HD], BF16))
    Rb = _p.enter_context(nc.sbuf_tensor(f"p1R{half}", [128, 8 * 128], BF16))
    bqc = _p.enter_context(nc.sbuf_tensor(f"p1bq{half}", [128, 8], F32))
    bkc = _p.enter_context(nc.sbuf_tensor(f"p1bk{half}", [64, 1], F32))
    bvr = _p.enter_context(nc.sbuf_tensor(f"p1bv{half}", [1, HD], BF16))
    qpt = _p.enter_context(nc.sbuf_tensor(f"p1qp{half}", [128, 2 * 512], BF16))
    qrt = _p.enter_context(nc.sbuf_tensor(f"p1qr{half}", [128, 2 * 512], BF16))
    kst = _p.enter_context(nc.sbuf_tensor(f"p1ks{half}", [64, TC], BF16))
    vst = _p.enter_context(nc.sbuf_tensor(f"p1vs{half}", [128, 2 * HD], BF16))

    V0, T0_, S0 = c0.get("v"), c0.get("t"), c0.get("s")
    STP, SGP = c0.get("st"), c0.get("sg")

    def p_loads_sync(sync):
        sync.wait_ge(sems["v"], V0)
        sync.wait_ge(sems["t"], T0_)
        sync.wait_ge(sems["st"], STP)
        sync.wait_ge(sems["sg"], SGP)
        for cc in range(8):
            sync.dma_start(out=wkf[:, cc * HD:(cc + 1) * HD],
                           in_=I.wk[cc * 128:(cc + 1) * 128, :]).then_inc(sems["dh"], 16)
            c0.inc("dh", 16)
        for cc in range(8):
            sync.dma_start(out=bqc[:, cc:cc + 1],
                           in_=I.bq[cc * 128:(cc + 1) * 128].rearrange("(p o) -> p o", o=1)
                           ).then_inc(sems["dh"], 16)
            c0.inc("dh", 16)
        for i, base in enumerate((0, 1, 32, 33)):
            sync.dma_start(out=bkc[16 * i:16 * (i + 1), :],
                           in_=I.bk[base:min(base + 32, HD):2].rearrange("(p o) -> p o", o=1),
                           allow_slow_non_contiguous=True).then_inc(sems["dh"], 16)
            c0.inc("dh", 16)
        for cc in range(8):
            sync.dma_start(out=Rb[:, cc * 128:(cc + 1) * 128],
                           in_=I.ropeR[cc, :, :]).then_inc(sems["dh"], 16)
            c0.inc("dh", 16)
    block.sync(p_loads_sync)
    P_DH = c0.get("dh")

    def p_loads_gp(gpsimd):
        gpsimd.wait_ge(sems["v"], V0)
        gpsimd.wait_ge(sems["t"], T0_)
        gpsimd.wait_ge(sems["st"], STP)
        gpsimd.wait_ge(sems["sg"], SGP)
        for cc in range(8):
            gpsimd.dma_start(out=xb[:, cc * TC:(cc + 1) * TC],
                             in_=I.xT[cc * 128:(cc + 1) * 128, t0:t0 + TC]
                             ).then_inc(sems["ds"], 16)
            gpsimd.dma_start(out=wqb[:, cc * D:(cc + 1) * D],
                             in_=I.wq[cc * 128:(cc + 1) * 128, :]).then_inc(sems["ds"], 16)
            gpsimd.dma_start(out=wvb[:, cc * HD:(cc + 1) * HD],
                             in_=I.wv[cc * 128:(cc + 1) * 128, :]).then_inc(sems["ds"], 16)
            c0.inc("ds", 48)
        gpsimd.dma_start(out=bvr[:, :], in_=I.bv.rearrange("(o h) -> o h", o=1)
                         ).then_inc(sems["ds"], 16)
        c0.inc("ds", 16)
    block.gpsimd(p_loads_gp)
    P_DS = c0.get("ds")

    pr.wait("v", "dh", P_DH)
    for cc in range(8):
        src = wkf[:, cc * HD:(cc + 1) * HD]
        dst = wkp[:, cc * HD:(cc + 1) * HD]
        for i, base in enumerate((0, 1, 32, 33)):
            pr.run("v", (lambda src=src, dst=dst, i=i, base=base: lambda v:
                         v.tensor_copy(out=dst[:, 16 * i:16 * (i + 1)],
                                       in_=src[:, base:min(base + 32, HD):2]))())
    V_PERM = c0.get("v")

    pr.wait("t", "ds", P_DS)
    pr.wait("t", "dh", P_DH)
    pr.wait("t", "v", V_PERM)
    SB = c0.get("s")
    STB = c0.get("st")
    q_store = {}
    for oc in range(8):
        for tci in range(4):
            gi = oc * 4 + tci
            slot = gi % 2
            ps = psA if slot == 0 else psB
            if gi >= 2:
                pr.wait("t", "s", SB + gi - 1)
            def mk_qmm(oc=oc, tci=tci, ps=ps):
                def f(tensor):
                    mm = None
                    for ic in range(8):
                        mm = tensor.matmul(
                            ps[:, :],
                            wqb[:, ic * D + oc * 128: ic * D + (oc + 1) * 128],
                            xb[:, ic * TC + tci * 512: ic * TC + (tci + 1) * 512],
                            start=(ic == 0), stop=(ic == 7))
                    return mm
                return f
            tq = pr.run("t", mk_qmm())
            pr.wait("s", "t", tq)
            if gi >= 2:
                pr.wait("s", "t", c0.get("p1_rope_%d_%d" % (half, slot), 0))
            sd = pr.run("s", (lambda oc=oc, slot=slot, ps=ps: lambda s: s.activation(
                qpt[:, slot * 512:(slot + 1) * 512], ps[:, :], AF.Identity,
                bias=bqc[:, oc:oc + 1], scale=1.0))())
            pr.wait("t", "s", sd)
            if gi >= 1:
                pr.wait("t", "v", c0.get("p1_qr_%d_%d" % (half, gi - 1), 0))
            def mk_rope(oc=oc, slot=slot):
                def f(tensor):
                    return tensor.matmul(psC[:, :], Rb[:, oc * 128:(oc + 1) * 128],
                                         qpt[:, slot * 512:(slot + 1) * 512],
                                         start=True, stop=True)
                return f
            tr = pr.run("t", mk_rope())
            c0.v["p1_rope_%d_%d" % (half, slot)] = tr
            pr.wait("v", "t", tr)
            if gi >= 2:
                pr.wait("v", "st", STB + 16 * (gi - 1))
            vd = pr.run("v", (lambda slot=slot: lambda v: v.tensor_copy(
                out=qrt[:, slot * 512:(slot + 1) * 512], in_=psC[:, :]))())
            c0.v["p1_qr_%d_%d" % (half, gi)] = vd
            q_store[gi] = (vd, oc, tci, slot)

    def p_qstores(sync):
        for gi in range(32):
            vd, oc, tci, slot = q_store[gi]
            sync.wait_ge(sems["v"], vd)
            sync.dma_start(
                out=S.qd0[oc * 128:(oc + 1) * 128, t0 + tci * 512:t0 + (tci + 1) * 512],
                in_=qrt[:, slot * 512:(slot + 1) * 512]).then_inc(sems["st"], 16)
            c0.inc("st", 16)
    block.sync(p_qstores)

    for tci in range(4):
        if tci >= 1:
            pr.wait("t", "s", c0.get("p1_k_%d" % half, 0))
        def mk_kmm(tci=tci):
            def f(tensor):
                mm = None
                for ic in range(8):
                    mm = tensor.matmul(
                        psD[0:64, :], wkp[:, ic * HD:(ic + 1) * HD],
                        xb[:, ic * TC + tci * 512: ic * TC + (tci + 1) * 512],
                        start=(ic == 0), stop=(ic == 7))
                return mm
            return f
        tk = pr.run("t", mk_kmm())
        pr.wait("s", "t", tk)
        sd = pr.run("s", (lambda tci=tci: lambda s: s.activation(
            kst[:, tci * 512:(tci + 1) * 512], psD[0:64, :],
            AF.Identity, bias=bkc[:, :], scale=1.0))())
        c0.v["p1_k_%d" % half] = sd
    S_KDONE = c0.get("s")

    v_store = {}
    STB2 = c0.get("st")
    for tt in range(16):
        slot = tt % 2
        ps = psE if slot == 0 else psF
        if tt >= 2:
            pr.wait("t", "v", c0.get("p1_v_%d_%d" % (half, tt - 2), 0))
        def mk_vmm(tt=tt, ps=ps):
            def f(tensor):
                for ic in range(8):
                    tensor.matmul(ps[0:128, 0:HD],
                                  xb[:, ic * TC + tt * 128: ic * TC + (tt + 1) * 128],
                                  wvb[:, ic * HD:(ic + 1) * HD],
                                  start=(ic == 0), stop=False)
                return tensor.matmul(ps[0:128, 0:HD], K.or512[:, 0:128],
                                     bvr[:, :], start=False, stop=True)
            return f
        tv = pr.run("t", mk_vmm())
        pr.wait("v", "t", tv)
        if tt >= 2:
            pr.wait("v", "st", STB2 + 16 * tt)
        vd = pr.run("v", (lambda tt=tt, slot=slot, ps=ps: lambda v: v.tensor_copy(
            out=vst[:, slot * HD:(slot + 1) * HD], in_=ps[0:128, 0:HD]))())
        c0.v["p1_v_%d_%d" % (half, tt)] = vd
        v_store[tt] = (vd, tt, slot)

    def p_kvstores(sync):
        sync.wait_ge(sems["s"], S_KDONE)
        sync.dma_start(out=S.kd0[:, t0:t0 + TC], in_=kst[:, :]).then_inc(sems["st"], 16)
        c0.inc("st", 16)
        for tt in range(16):
            vd, _, slot = v_store[tt]
            sync.wait_ge(sems["v"], vd)
            sync.dma_start(out=S.vd0[t0 + tt * 128:t0 + (tt + 1) * 128, :],
                           in_=vst[:, slot * HD:(slot + 1) * HD]).then_inc(sems["st"], 16)
            c0.inc("st", 16)
    block.sync(p_kvstores)
    pr.barrier()
    pr.wait("v", "st", c0.get("st"))
    pr.wait("t", "st", c0.get("st"))
    pr.wait("s", "st", c0.get("st"))
    _p.close()



def emit_attn(P, mqa, q_src, k_src, v_src, att_dst):
    nc, pr, c0, sems, block = P["nc"], P["pr"], P["c0"], P["sems"], P["block"]
    K = P["K"]
    psA, psB, psC, psD, psE, psF = P["PS"]
    tri, i128, hmb, i8, ones8, onesf = (K.tri, K.i128, K.hmb, K.i8, K.ones8,
                                        K.onesf)
    psPV = [psC, psD, psE, psF]
    NQB = T // 512
    nvt = 32 if mqa else 256
    for p in range(2):
        sfx = f"{'ab'[p]}{int(mqa)}"
        _pa = ExitStack()
        qh = _pa.enter_context(nc.sbuf_tensor("qh" + sfx, [128, 4 * T], BF16))
        kf = _pa.enter_context(nc.sbuf_tensor("kf" + sfx,
                                              [128, (T if mqa else 4 * T)], BF16))
        va = _pa.enter_context(nc.sbuf_tensor("va" + sfx, [128, nvt * 65], BF16))
        n2 = _pa.enter_context(nc.sbuf_tensor("n2" + sfx, [8, 2 * T], BF16))
        mx = _pa.enter_context(nc.sbuf_tensor("mx" + sfx, [8, 4], F32))
        i8n = _pa.enter_context(nc.sbuf_tensor("i8n" + sfx, [8, 8], F32))
        negm = _pa.enter_context(nc.sbuf_tensor("ng" + sfx, [128, 8], F32))
        pbuf = _pa.enter_context(nc.sbuf_tensor("pb" + sfx, [128, 7 * 512], BF16))
        drow = _pa.enter_context(nc.sbuf_tensor("dr" + sfx, [1, 4 * 512], F32))
        drowb = _pa.enter_context(nc.sbuf_tensor("db" + sfx, [1, 4 * 512], BF16))
        drow2 = _pa.enter_context(nc.sbuf_tensor("d2" + sfx, [1, 4 * 512], F32))
        bco = _pa.enter_context(nc.sbuf_tensor("bc" + sfx, [128, 512], F32))
        psM = _pa.enter_context(nc.psum_tensor("psM" + sfx, [128, 512], F32))
        psX = _pa.enter_context(nc.psum_tensor("psX" + sfx, [128, 512], F32))
        psS = [psA, psB, psM]

        V_FREE = c0.get("v")
        T_FREE = c0.get("t")
        ST_RDY = c0.get("st")
        SG_RDY = c0.get("sg")

        def pass_loads(sync, p=p):
            sync.wait_ge(sems["st"], ST_RDY)
            sync.wait_ge(sems["sg"], SG_RDY)
            sync.wait_ge(sems["v"], V_FREE)
            sync.wait_ge(sems["t"], T_FREE)
            for t5 in range(4):
                sync.dma_start(out=qh[:, t5 * T:(t5 + 1) * T],
                               in_=q_src[p * 512 + t5 * 128: p * 512 + (t5 + 1) * 128, :]
                               ).then_inc(sems["dh"], 16)
                c0.inc("dh", 16)
            if mqa:
                sync.dma_start(out=kf[0:64, :], in_=k_src[:, :]).then_inc(sems["dh"], 16)
                sync.dma_start(out=kf[64:128, :], in_=k_src[:, :]).then_inc(sems["dh"], 16)
                c0.inc("dh", 32)
                for kt in range(32):
                    sync.dma_start(out=va[:, kt * 65:kt * 65 + 64],
                                   in_=v_src[kt * 128:(kt + 1) * 128, :]
                                   ).then_inc(sems["dh"], 16)
                    c0.inc("dh", 16)
            else:
                for t5 in range(4):
                    sync.dma_start(out=kf[:, t5 * T:(t5 + 1) * T],
                                   in_=k_src[p * 512 + t5 * 128: p * 512 + (t5 + 1) * 128, :]
                                   ).then_inc(sems["dh"], 16)
                    c0.inc("dh", 16)
        block.sync(pass_loads)
        LOADS_DH = c0.get("dh")
        DS_VA = c0.get("ds")
        if not mqa:
            def pass_vloads(gpsimd, p=p):
                gpsimd.wait_ge(sems["st"], ST_RDY)
                gpsimd.wait_ge(sems["sg"], SG_RDY)
                gpsimd.wait_ge(sems["v"], V_FREE)
                gpsimd.wait_ge(sems["t"], T_FREE)
                for hl in range(8):
                    hgl = p * 8 + hl
                    for kt in range(32):
                        gpsimd.dma_start(
                            out=va[:, (hl * 32 + kt) * 65:(hl * 32 + kt) * 65 + 64],
                            in_=v_src[kt * 128:(kt + 1) * 128, hgl * 64:(hgl + 1) * 64]
                        ).then_inc(sems["ds"], 16)
                        c0.inc("ds", 16)
            block.gpsimd(pass_vloads)
            DS_VA = c0.get("ds")

        pr.wait("v", "dh", LOADS_DH)
        for i in range(nvt):
            pr.run("v", (lambda i=i: lambda v: v.memset(
                va[:, i * 65 + 64:(i + 1) * 65], 1.0))())

        _cm_qsq = nc.sbuf_tensor("qsq" + sfx, [128, 4 * T], BF16)
        qsq = _cm_qsq.__enter__()
        for t5 in range(4):
            pr.run("v", (lambda t5=t5: lambda v: v.tensor_tensor(
                out=qsq[:, t5 * T:(t5 + 1) * T], in0=qh[:, t5 * T:(t5 + 1) * T],
                in1=qh[:, t5 * T:(t5 + 1) * T], op=AluOpType.mult))())
        V_QSQ = c0.get("v")
        pr.wait("t", "v", V_QSQ)
        pr.wait("t", "dh", LOADS_DH)
        for cc in range(8):
            if cc >= 1:
                pr.wait("t", "v", V_QSQ + cc)
            def mk_qn(cc=cc):
                def f(tensor):
                    mm = None
                    for t5 in range(4):
                        mm = tensor.matmul(psM[0:8, :], hmb[:, t5 * 8:(t5 + 1) * 8],
                                           qsq[:, t5 * T + cc * 512: t5 * T + (cc + 1) * 512],
                                           start=(t5 == 0), stop=(t5 == 3))
                    return mm
                return f
            tq = pr.run("t", mk_qn())
            pr.wait("v", "t", tq)
            pr.run("v", (lambda cc=cc: lambda v: v.tensor_copy(
                out=n2[0:8, cc * 512:(cc + 1) * 512], in_=psM[0:8, :]))())
        pr.wait("v", "t", c0.get("t"))
        for t5 in range(1 if mqa else 4):
            pr.run("v", (lambda t5=t5: lambda v: v.tensor_tensor(
                out=qsq[:, t5 * T:(t5 + 1) * T], in0=kf[:, t5 * T:(t5 + 1) * T],
                in1=kf[:, t5 * T:(t5 + 1) * T], op=AluOpType.mult))())
        V_KSQ = c0.get("v")
        pr.wait("t", "v", V_KSQ)
        for cc in range(8):
            if cc >= 1:
                pr.wait("t", "v", V_KSQ + cc)
            def mk_kn(cc=cc):
                def f(tensor):
                    if mqa:
                        return tensor.matmul(psM[0:8, :], hmb[:, 32:40],
                                             qsq[:, cc * 512:(cc + 1) * 512],
                                             start=True, stop=True)
                    mm = None
                    for t5 in range(4):
                        mm = tensor.matmul(psM[0:8, :], hmb[:, t5 * 8:(t5 + 1) * 8],
                                           qsq[:, t5 * T + cc * 512: t5 * T + (cc + 1) * 512],
                                           start=(t5 == 0), stop=(t5 == 3))
                    return mm
                return f
            tk = pr.run("t", mk_kn())
            pr.wait("v", "t", tk)
            pr.run("v", (lambda cc=cc: lambda v: v.tensor_copy(
                out=n2[0:8, T + cc * 512: T + (cc + 1) * 512], in_=psM[0:8, :]))())
        pr.run("v", lambda v: v.tensor_reduce(out=mx[0:8, 0:1], in_=n2[0:8, 0:T],
                                              axis=mybir.AxisListType.X, op=AluOpType.max))
        pr.run("v", lambda v: v.tensor_reduce(out=mx[0:8, 1:2], in_=n2[0:8, T:2 * T],
                                              axis=mybir.AxisListType.X, op=AluOpType.max))
        pr.run("v", lambda v: v.tensor_tensor(out=mx[0:8, 2:3], in0=mx[0:8, 0:1],
                                              in1=mx[0:8, 1:2], op=AluOpType.mult))
        V_M2 = c0.get("v")
        pr.wait("s", "v", V_M2)
        S_SQRT = pr.run("s", lambda s: s.activation(mx[0:8, 3:4], mx[0:8, 2:3],
                                                    AF.Sqrt, scale=1.0 / 64.0))
        pr.wait("v", "s", S_SQRT)
        pr.run("v", lambda v: v.tensor_scalar(out=i8n[:, :], in0=i8[:, :],
                                              scalar1=mx[0:8, 3:4], scalar2=-1.0,
                                              op0=AluOpType.mult, op1=AluOpType.mult))
        V_I8N = c0.get("v")
        pr.wait("t", "v", V_I8N)
        T_NEGM = pr.run("t", lambda t: t.matmul(psX[:, 0:8], ones8[:, :], i8n[:, :],
                                                start=True, stop=True))
        pr.wait("v", "t", T_NEGM)
        pr.run("v", lambda v: v.tensor_copy(out=negm[:, :], in_=psX[:, 0:8]))
        for slot, delta in ((4, 128), (5, 256), (6, 384)):
            pr.run("v", (lambda slot=slot, delta=delta: lambda v: v.memset(
                pbuf[:, slot * 512:slot * 512 + delta], 0.0))())
        PRE_V = c0.get("v")
        pr.wait("t", "v", PRE_V)
        pr.wait("s", "v", PRE_V)
        if not mqa:
            pr.wait("t", "ds", DS_VA)
        _cm_qsq.__exit__(None, None, None)
        _cm_att = nc.sbuf_tensor("att" + sfx, [128, 4 * T], BF16)
        att = _cm_att.__enter__()

        def k_lhsT(h, kt):
            p0 = (h % 2) * 64
            off = 0 if mqa else (h // 2) * T
            return kf[p0:p0 + 64, off + kt * 128:off + (kt + 1) * 128]

        def va_lhsT(h, kt):
            i = kt if mqa else (h * 32 + kt)
            return va[:, i * 65:(i + 1) * 65]

        exp_done_at = {}
        pbuf_last_pv = {i: 0 for i in range(7)}
        pv_stop_at = {}
        norm_done_at = {}
        last_psX_free = 0
        sc_idx = 0
        pend_pv = [None]

        def flush_pv():
            if pend_pv[0] is None:
                return
            i, hi, h, kt, slot, start, stop, qb_, hg_ = pend_pv[0]
            pend_pv[0] = None
            pr.wait("t", "s", exp_done_at[i])
            if kt == 0:
                pq = (qb_, hg_ - 1) if hg_ == 1 else (qb_ - 1, 1)
                if pq in norm_done_at:
                    pr.wait("t", "v", norm_done_at[pq])
            def mk_pv(hi=hi, h=h, kt=kt, slot=slot, start=start, stop=stop):
                def f(tensor):
                    return tensor.matmul(psPV[hi][0:65, :], va_lhsT(h, kt),
                                         pbuf[:, slot * 512:(slot + 1) * 512],
                                         start=start, stop=stop)
                return f
            pv_t = pr.run("t", mk_pv())
            pbuf_last_pv[slot] = pv_t
            if stop:
                pv_stop_at[(qb_, hg_, hi)] = pv_t

        for qb in range(NQB):
            nkt = 4 * (qb + 1)
            for hg in range(2):
                for kt in range(nkt):
                    dd = kt * 128 - qb * 512
                    for hi in range(4):
                        h = hg * 4 + hi
                        i = sc_idx
                        bank = psS[i % 3]
                        lo = max(0, dd)
                        if i >= 3:
                            pr.wait("t", "s", exp_done_at[i - 3])
                        def mk_sc(bank=bank, h=h, kt=kt, qb=qb, dd=dd, lo=lo):
                            def f(tensor):
                                if dd >= 0:
                                    tensor.matmul(bank[0:128, lo:512],
                                                  i128[:, :], tri[:, 0:512 - lo],
                                                  start=True, stop=False,
                                                  skip_group_check=True)
                                return tensor.matmul(
                                    bank[0:128, lo:512], k_lhsT(h, kt),
                                    qh[(h % 2) * 64:(h % 2) * 64 + 64,
                                       (h // 2) * T + qb * 512 + lo:(h // 2) * T + (qb + 1) * 512],
                                    start=(dd < 0), stop=True,
                                    skip_group_check=(dd >= 0))
                            return f
                        sc_t = pr.run("t", mk_sc())
                        slot = (i % 4) if dd <= 0 else 3 + dd // 128
                        pr.wait("s", "t", sc_t)
                        pr.wait("s", "t", pbuf_last_pv[slot])
                        def mk_exp(bank=bank, slot=slot, lo=lo, h=h):
                            def f(scalar):
                                return scalar.activation(
                                    pbuf[:, slot * 512 + lo:(slot + 1) * 512],
                                    bank[0:128, lo:512], AF.Exp,
                                    bias=negm[:, h:h + 1], scale=0.125)
                            return f
                        exp_done_at[i] = pr.run("s", mk_exp())
                        flush_pv()
                        pend_pv[0] = (i, hi, h, kt, slot,
                                      kt == 0, kt == nkt - 1, qb, hg)
                        sc_idx += 1
                flush_pv()
                for hi in range(4):
                    pr.wait("v", "t", pv_stop_at[(qb, hg, hi)])
                    pr.run("v", (lambda hi=hi: lambda v: v.tensor_copy(
                        out=drowb[:, hi * 512:(hi + 1) * 512],
                        in_=psPV[hi][64:65, :]))())
                V_RECIP = c0.get("v")
                for pp in range(2):
                    pr.wait("t", "v", V_RECIP)
                    if last_psX_free:
                        pr.wait("t", "v", last_psX_free)
                    def mk_bc(pp=pp):
                        def f(tensor):
                            tensor.matmul(psX[0:64, :], onesf[:, :],
                                          drowb[:, (2 * pp) * 512:(2 * pp + 1) * 512],
                                          start=True, stop=True)
                            return tensor.matmul(psX[64:128, :], onesf[:, :],
                                                 drowb[:, (2 * pp + 1) * 512:(2 * pp + 2) * 512],
                                                 start=True, stop=True)
                        return f
                    bc_t = pr.run("t", mk_bc())
                    pr.wait("v", "t", bc_t)
                    pr.run("v", lambda v: v.reciprocal(out=bco[:, :], in_=psX[:, :]))
                    last_psX_free = c0.get("v")
                    for sub in range(2):
                        hi = pp * 2 + sub
                        h = hg * 4 + hi
                        def mk_nm(hi=hi, h=h, sub=sub, qb=qb):
                            def f(vector):
                                return vector.tensor_tensor(
                                    out=att[(h % 2) * 64:(h % 2) * 64 + 64,
                                            (h // 2) * T + qb * 512:(h // 2) * T + (qb + 1) * 512],
                                    in0=psPV[hi][0:64, :],
                                    in1=bco[sub * 64:(sub + 1) * 64, :],
                                    op=AluOpType.mult)
                            return f
                        pr.run("v", mk_nm())
                norm_done_at[(qb, hg)] = c0.get("v")
        V_ATT_DONE = c0.get("v")

        def pass_stores(sync, p=p, V_ATT_DONE=V_ATT_DONE):
            sync.wait_ge(sems["v"], V_ATT_DONE)
            for t5 in range(4):
                sync.dma_start(
                    out=att_dst[p * 512 + t5 * 128: p * 512 + (t5 + 1) * 128, :],
                    in_=att[:, t5 * T:(t5 + 1) * T]).then_inc(sems["st"], 16)
                c0.inc("st", 16)
        block.sync(pass_stores)
        _cm_att.__exit__(None, None, None)
        _pa.close()
        pr.barrier()
        pr.wait("v", "st", c0.get("st"))
        pr.wait("t", "st", c0.get("st"))
        pr.wait("s", "st", c0.get("st"))



def emit_proj_ln_half(P, half, att_src, w_src, bias_src, x_src, xpark, tag):
    nc, pr, c0, sems, block = P["nc"], P["pr"], P["c0"], P["sems"], P["block"]
    K = P["K"]
    psA, psB, psC, psD, psE, psF = P["PS"]
    t0 = half * TC
    tg = f"{tag}{half}"
    _p = ExitStack()
    borow = _p.enter_context(nc.sbuf_tensor(tg + "bo", [1, D], BF16))
    rbuf = [_p.enter_context(nc.sbuf_tensor(f"{tg}rf{i}", [128, TC], F32))
            for i in range(8)]
    _cm_ab = nc.sbuf_tensor(tg + "ab", [128, 8 * TC], BF16); ab = _cm_ab.__enter__()
    _cm_xb = nc.sbuf_tensor(tg + "xb", [128, 8 * TC], F32); xb = _cm_xb.__enter__()
    _cm_wb = nc.sbuf_tensor(tg + "wb", [128, 8 * D], BF16); wb = _cm_wb.__enter__()

    V0, T0_ = c0.get("v"), c0.get("t")
    ST0, SG0 = c0.get("st"), c0.get("sg")
    DH0 = c0.get("dh")
    DS0 = c0.get("ds")

    def p_loads(sync):
        sync.wait_ge(sems["v"], V0)
        sync.wait_ge(sems["t"], T0_)
        sync.wait_ge(sems["st"], ST0)
        sync.wait_ge(sems["sg"], SG0)
        for cc in range(8):
            sync.dma_start(out=xb[:, cc * TC:(cc + 1) * TC],
                           in_=x_src[cc * 128:(cc + 1) * 128, t0:t0 + TC]
                           ).then_inc(sems["dh"], 16)
            sync.dma_start(out=ab[:, cc * TC:(cc + 1) * TC],
                           in_=att_src[cc * 128:(cc + 1) * 128, t0:t0 + TC]
                           ).then_inc(sems["dh"], 16)
            c0.inc("dh", 32)
    block.sync(p_loads)

    def p_loads_gp(gpsimd):
        gpsimd.wait_ge(sems["v"], V0)
        gpsimd.wait_ge(sems["t"], T0_)
        gpsimd.wait_ge(sems["st"], ST0)
        gpsimd.wait_ge(sems["sg"], SG0)
        for cc in range(8):
            gpsimd.dma_start(out=wb[:, cc * D:(cc + 1) * D],
                             in_=w_src[cc * 128:(cc + 1) * 128, :]).then_inc(sems["ds"], 16)
            c0.inc("ds", 16)
        if bias_src is not None:
            gpsimd.dma_start(out=borow[:, :],
                             in_=bias_src.rearrange("(o h) -> o h", o=1)
                             ).then_inc(sems["ds"], 16)
            c0.inc("ds", 16)
    block.gpsimd(p_loads_gp)
    DH1 = c0.get("dh")
    DS1 = c0.get("ds")

    pr.wait("t", "dh", DH1)
    pr.wait("t", "ds", DS1)
    VB = c0.get("v")
    psRot = [psA, psB, psE, psF]
    gi = 0
    for oc in range(8):
        for tc_i in range(4):
            ps = psRot[gi % 4]
            if gi >= 4:
                pr.wait("t", "v", VB + gi - 3)
            def mk_mm(oc=oc, tc_i=tc_i, ps=ps, with_bias=bias_src is not None):
                def f(tensor):
                    mm = None
                    for ic in range(8):
                        mm = tensor.matmul(
                            ps[:, :], wb[:, ic * D + oc * 128: ic * D + (oc + 1) * 128],
                            ab[:, ic * TC + tc_i * 512: ic * TC + (tc_i + 1) * 512],
                            start=(ic == 0), stop=(ic == 7) and not with_bias)
                    if with_bias:
                        mm = tensor.matmul(ps[:, :],
                                           borow[:, oc * 128:(oc + 1) * 128],
                                           K.or512[:, :], start=False, stop=True)
                    return mm
                return f
            td = pr.run("t", mk_mm())
            pr.wait("v", "t", td)
            pr.run("v", (lambda oc=oc, tc_i=tc_i, ps=ps: lambda v: v.tensor_tensor(
                out=rbuf[oc][:, tc_i * 512:(tc_i + 1) * 512], in0=ps[:, :],
                in1=xb[:, oc * TC + tc_i * 512: oc * TC + (tc_i + 1) * 512],
                op=AluOpType.add))())
            gi += 1
    T_MM_DONE = c0.get("t")
    _cm_wb.__exit__(None, None, None)
    _cm_xb.__exit__(None, None, None)
    _cm_ab.__exit__(None, None, None)
    _cm_xo = nc.sbuf_tensor(tg + "xo", [128, 8 * TC], F32); xout = _cm_xo.__enter__()
    _cm_scr = nc.sbuf_tensor(tg + "sc", [128, 4 * 512], BF16); scr = _cm_scr.__enter__()
    _cm_rw = nc.sbuf_tensor(tg + "rw", [1, 4 * TC], F32); rows_b = _cm_rw.__enter__()
    rows = [rows_b[:, k * TC:(k + 1) * TC] for k in range(4)]
    _cm_bm = nc.sbuf_tensor(tg + "bm", [128, TC], F32); bcm = _cm_bm.__enter__()
    _cm_bs = nc.sbuf_tensor(tg + "bs", [128, TC], F32); bcs = _cm_bs.__enter__()
    ready = emit_ln_tail(pr, rbuf, scr, rows, bcm, bcs,
                         lambda ch: xout[:, ch * TC:(ch + 1) * TC],
                         psC, psD, psE, K.ocb, K.oc1f, K.epsb,
                         pre_wait=("t", T_MM_DONE))

    def p_xstores(gpsimd):
        for ch in range(8):
            gpsimd.wait_ge(sems["v"], ready[ch])
            gpsimd.dma_start(out=xpark[ch * 128:(ch + 1) * 128, t0:t0 + TC],
                             in_=xout[:, ch * TC:(ch + 1) * TC]).then_inc(sems["sg"], 16)
            c0.inc("sg", 16)
    block.gpsimd(p_xstores)
    pr.barrier()
    pr.wait("v", "sg", c0.get("sg"))
    pr.wait("t", "sg", c0.get("sg"))
    _cm_bs.__exit__(None, None, None)
    _cm_bm.__exit__(None, None, None)
    _cm_rw.__exit__(None, None, None)
    _cm_scr.__exit__(None, None, None)
    _cm_xo.__exit__(None, None, None)
    _p.close()



def emit_ffn_a_half(P, half):
    nc, pr, c0, sems, block = P["nc"], P["pr"], P["c0"], P["sems"], P["block"]
    K, I, S = P["K"], P["I"], P["S"]
    psA, psB, psC, psD, psE, psF = P["PS"]
    t0 = half * TC
    tg = f"fa{half}"
    NG = 4
    NGS = 8
    _p = ExitStack()
    xq = _p.enter_context(nc.sbuf_tensor(tg + "xq", [128, 8 * TC], BF16))
    mrow = _p.enter_context(nc.sbuf_tensor(tg + "mr", [1, TC], F32))
    rrow = _p.enter_context(nc.sbuf_tensor(tg + "rr", [1, TC], F32))
    wsum = _p.enter_context(nc.sbuf_tensor(tg + "ws", [128, 1], F32))
    wcol = _p.enter_context(nc.sbuf_tensor(tg + "wc", [128, 1], F32))
    msc = _p.enter_context(nc.sbuf_tensor(tg + "ms", [1, 4], F32))
    wsc128 = _p.enter_context(nc.sbuf_tensor(tg + "w1", [128, 1], F32))
    bcX = _p.enter_context(nc.sbuf_tensor(tg + "bX", [128, TC], F32))
    bcR = _p.enter_context(nc.sbuf_tensor(tg + "bR", [128, TC], F32))
    psT = _p.enter_context(nc.psum_tensor(tg + "pT", [128, 1024], BF16))
    psTR = _p.enter_context(nc.psum_tensor(tg + "pR", [128, 1024], BF16))

    _cm_xb = nc.sbuf_tensor(tg + "xb", [128, 8 * TC], F32); xb = _cm_xb.__enter__()
    _cm_s32 = nc.sbuf_tensor(tg + "s3", [128, TC], F32); s32 = _cm_s32.__enter__()
    _cm_sq = nc.sbuf_tensor(tg + "sq", [128, TC], F32); sqt = _cm_sq.__enter__()
    _cm_sb = nc.sbuf_tensor(tg + "sb", [128, TC + 16], BF16); sb = _cm_sb.__enter__()

    V0, T0_ = c0.get("v"), c0.get("t")
    ST0_, SG0_ = c0.get("st"), c0.get("sg")

    def p_loads(sync):
        sync.wait_ge(sems["v"], V0)
        sync.wait_ge(sems["t"], T0_)
        sync.wait_ge(sems["st"], ST0_)
        sync.wait_ge(sems["sg"], SG0_)
        for cc in range(8):
            sync.dma_start(out=xb[:, cc * TC:(cc + 1) * TC],
                           in_=S.x1park[cc * 128:(cc + 1) * 128, t0:t0 + TC]
                           ).then_inc(sems["dh"], 16)
            c0.inc("dh", 16)
    block.sync(p_loads)
    DH0 = c0.get("dh")

    pr.wait("v", "dh", DH0)
    pr.wait("t", "dh", DH0)
    pr.wait("s", "dh", DH0)
    emit_absmax_row(pr, lambda ch: xb[:, ch * TC:(ch + 1) * TC], 8,
                    s32, sb, psT, psTR, mrow, K.i128, sqtmp=sqt)
    pr.run("v", lambda v: v.tensor_scalar(out=mrow[:, :], in0=mrow[:, :],
                                          scalar1=1e-5, scalar2=None,
                                          op0=AluOpType.max))
    pr.run("v", lambda v: v.reciprocal(out=rrow[:, :], in_=mrow[:, :]))
    pr.wait("t", "v", c0.get("v"))
    emit_row_bcast(pr, lambda tc_i: rrow[:, tc_i * 512:(tc_i + 1) * 512],
                   bcX, K.c127, psC)
    for ch in range(8):
        pr.run("v", (lambda ch=ch: lambda v: v.tensor_tensor(
            out=s32[:, :], in0=xb[:, ch * TC:(ch + 1) * TC],
            in1=bcX[:, :], op=AluOpType.mult))())
        pr.run("v", (lambda ch=ch: lambda v: v.tensor_scalar(
            out=xq[:, ch * TC:(ch + 1) * TC], in0=s32[:, :],
            scalar1=MAGIC, scalar2=MAGIC, op0=AluOpType.add,
            op1=AluOpType.subtract))())
    V_XQ = c0.get("v")

    _cm_sb.__exit__(None, None, None)
    _cm_sq.__exit__(None, None, None)
    _cm_s32.__exit__(None, None, None)
    _cm_xb.__exit__(None, None, None)
    _cm_wf = nc.sbuf_tensor(tg + "wf", [128, 2 * NGS * 1024], F32); wf = _cm_wf.__enter__()
    _cm_wq = nc.sbuf_tensor(tg + "wq", [128, 2 * NGS * 1024], BF16); wq1 = _cm_wq.__enter__()
    _cm_wt = nc.sbuf_tensor(tg + "wt", [128, 1024], F32); wtmp = _cm_wt.__enter__()
    _cm_hb = nc.sbuf_tensor(tg + "hb", [128, 8 * TC], BF16); hb = _cm_hb.__enter__()

    marks = {}
    W0 = {0: c0.get("w0"), 1: c0.get("w1")}

    def gp_loads(gpsimd):
        gpsimd.wait_ge(sems["v"], V_XQ)
        gpsimd.wait_ge(sems["st"], ST0_)
        gpsimd.wait_ge(sems["sg"], SG0_)
        for p in range(2):
            for gq in range(NG):
                hf = gq % 2
                if p == 0 and gq >= 2:
                    gpsimd.wait_ge(sems["v"], marks["absum"][gq - 2])
                elif p == 1 and gq < 2:
                    gpsimd.wait_ge(sems["v"], marks["absum"][NG - 2 + gq])
                elif p == 1:
                    gpsimd.wait_ge(sems["v"], marks["tern"][gq - 2])
                sem_h = sems["w0"] if hf == 0 else sems["w1"]
                for cc in range(NGS):
                    gpsimd.dma_start(
                        out=wf[:, (hf * NGS + cc) * 1024:(hf * NGS + cc + 1) * 1024],
                        in_=I.w1T[cc * 128:(cc + 1) * 128, gq * 1024:(gq + 1) * 1024]
                    ).then_inc(sem_h, 16)
                    c0.inc("w%d" % hf, 16)

    marks["absum"] = emit_wquant_pass1(pr, c0, wf, wsum, wcol, msc, wsc128,
                                       K.ocf, K.oc1f, psC, NGS, NG,
                                       4096.0 * 1024.0, W0)
    pr.run("v", lambda v: v.tensor_scalar(out=rrow[:, :], in0=mrow[:, :],
                                          scalar1=msc[:, 1:2], scalar2=1.0 / 127.0,
                                          op0=AluOpType.mult, op1=AluOpType.mult))
    pr.wait("t", "v", c0.get("v"))
    emit_row_bcast(pr, lambda tc_i: rrow[:, tc_i * 512:(tc_i + 1) * 512],
                   bcR, K.oc1f, psC)

    marks["tern"] = []
    store_ready = []
    mm_last = {}
    ST0 = c0.get("st")
    for gq in range(NG):
        hf = gq % 2
        wqp = gq % 2
        pr.wait("v", "w%d" % hf, W0[hf] + (NG // 2 + gq // 2 + 1) * NGS * 16)
        if gq > 1:
            pr.wait("v", "t", mm_last[gq - 2])
        tern_v = emit_ternarize_group(
            pr, wf,
            lambda cc, wqp=wqp: wq1[:, (wqp * NGS + cc) * 1024:(wqp * NGS + cc + 1) * 1024],
            wsc128, hf, NGS, wtmp)
        marks["tern"].append(tern_v)
        if gq > 0:
            pr.wait("t", "st", ST0 + 128 * gq)
        pr.wait("t", "v", tern_v)
        evac_at = []
        psRot = [psA, psB, psC, psD]
        gi = 0
        for hc in range(8):
            for tc_i in range(4):
                ps = psRot[gi % 4]
                if gi >= 4:
                    pr.wait("t", "v", evac_at[gi - 4])
                def mk(hc=hc, tc_i=tc_i, ps=ps, wqp=wqp):
                    def f(tensor):
                        mm = None
                        for ic in range(8):
                            mm = tensor.matmul(
                                ps[:, :],
                                wq1[:, (wqp * NGS + ic) * 1024 + hc * 128:
                                    (wqp * NGS + ic) * 1024 + (hc + 1) * 128],
                                xq[:, ic * TC + tc_i * 512: ic * TC + (tc_i + 1) * 512],
                                start=(ic == 0), stop=(ic == 7))
                        return mm
                    return f
                td = pr.run("t", mk())
                pr.wait("v", "t", td)
                pr.run("v", (lambda hc=hc, tc_i=tc_i, ps=ps: lambda v: v.tensor_tensor(
                    out=hb[:, hc * TC + tc_i * 512: hc * TC + (tc_i + 1) * 512],
                    in0=ps[:, :], in1=bcR[:, tc_i * 512:(tc_i + 1) * 512],
                    op=AluOpType.mult))())
                evac_at.append(c0.get("v"))
                gi += 1
        mm_last[gq] = c0.get("t")
        store_ready.append(c0.get("v"))

    def p_hstores(sync):
        for gq in range(NG):
            sync.wait_ge(sems["v"], store_ready[gq])
            for hc in range(8):
                sync.dma_start(
                    out=S.hD[(gq * 8 + hc) * 128:(gq * 8 + hc + 1) * 128, t0:t0 + TC],
                    in_=hb[:, hc * TC:(hc + 1) * TC]).then_inc(sems["st"], 16)
                c0.inc("st", 16)
    block.sync(p_hstores)
    block.gpsimd(gp_loads)
    pr.barrier()
    pr.wait("v", "st", c0.get("st"))
    pr.wait("t", "st", c0.get("st"))
    _cm_hb.__exit__(None, None, None)
    _cm_wt.__exit__(None, None, None)
    _cm_wq.__exit__(None, None, None)
    _cm_wf.__exit__(None, None, None)
    _p.close()



def emit_w2quant(P):
    nc, pr, c0, sems, block = P["nc"], P["pr"], P["c0"], P["sems"], P["block"]
    K, I = P["K"], P["I"]
    psA, psB, psC, psD, psE, psF = P["PS"]
    NG, NGS = 4, 8
    _w2q = nc.sbuf_tensor("w2q", [128, 32 * 1024], BF16)
    w2q = _w2q.__enter__()
    P["w2qcm"] = _w2q
    P["w2q"] = w2q
    msc2 = nc.sbuf_tensor("msc2", [1, 4], F32)
    P["msc2cm"] = msc2
    P["msc2"] = msc2.__enter__()
    _p = ExitStack()
    wf = _p.enter_context(nc.sbuf_tensor("w2f", [128, 2 * NGS * 1024], F32))
    wtmp = _p.enter_context(nc.sbuf_tensor("w2t", [128, 1024], F32))
    wsum = _p.enter_context(nc.sbuf_tensor("w2s", [128, 1], F32))
    wcol = _p.enter_context(nc.sbuf_tensor("w2c", [128, 1], F32))
    wsc128 = _p.enter_context(nc.sbuf_tensor("w2k", [128, 1], F32))
    marks = {}
    W0 = {0: c0.get("w0"), 1: c0.get("w1")}
    V0, T0_ = c0.get("v"), c0.get("t")
    ST0_, SG0_ = c0.get("st"), c0.get("sg")

    def gp_loads(gpsimd):
        gpsimd.wait_ge(sems["v"], V0)
        gpsimd.wait_ge(sems["t"], T0_)
        gpsimd.wait_ge(sems["st"], ST0_)
        gpsimd.wait_ge(sems["sg"], SG0_)
        for p in range(2):
            for gq in range(NG):
                hf = gq % 2
                if p == 0 and gq >= 2:
                    gpsimd.wait_ge(sems["v"], marks["absum"][gq - 2])
                elif p == 1 and gq < 2:
                    gpsimd.wait_ge(sems["v"], marks["absum"][NG - 2 + gq])
                elif p == 1:
                    gpsimd.wait_ge(sems["v"], marks["tern"][gq - 2])
                sem_h = sems["w0"] if hf == 0 else sems["w1"]
                for cc in range(NGS):
                    gpsimd.dma_start(
                        out=wf[:, (hf * NGS + cc) * 1024:(hf * NGS + cc + 1) * 1024],
                        in_=I.w2T[(gq * NGS + cc) * 128:(gq * NGS + cc + 1) * 128, :]
                    ).then_inc(sem_h, 16)
                    c0.inc("w%d" % hf, 16)

    marks["absum"] = emit_wquant_pass1(pr, c0, wf, wsum, wcol, P["msc2"],
                                       wsc128, K.ocf, K.oc1f, psC, NGS, NG,
                                       4096.0 * 1024.0, W0)
    marks["tern"] = []
    for gq in range(NG):
        hf = gq % 2
        pr.wait("v", "w%d" % hf, W0[hf] + (NG // 2 + gq // 2 + 1) * NGS * 16)
        tern_v = emit_ternarize_group(
            pr, wf,
            lambda cc, gq=gq: w2q[:, (gq * NGS + cc) * 1024:(gq * NGS + cc + 1) * 1024],
            wsc128, hf, NGS, wtmp)
        marks["tern"].append(tern_v)
    block.gpsimd(gp_loads)
    pr.barrier()
    _p.close()



def emit_conv_half(P, half):
    nc, pr, c0, sems, block = P["nc"], P["pr"], P["c0"], P["sems"], P["block"]
    K, I, S = P["K"], P["I"], P["S"]
    psA, psB, psC, psD, psE, psF = P["PS"]
    t0 = half * TC
    tg = f"cv{half}"
    _pm = ExitStack()
    mrow = _pm.enter_context(nc.sbuf_tensor(tg + "mr", [1, TC], F32))
    bcX = _pm.enter_context(nc.sbuf_tensor(tg + "bX", [128, TC], F32))
    P[f"mrow_cv{half}"] = mrow
    P[f"bcX_cv{half}"] = bcX
    P[f"mrow_cv{half}_cm"] = _pm
    _p = ExitStack()
    cwcol = _p.enter_context(nc.sbuf_tensor(tg + "cw", [128, 96], F32))
    cbcol = _p.enter_context(nc.sbuf_tensor(tg + "cb", [128, 32], F32))
    heb = _p.enter_context(nc.sbuf_tensor(tg + "he", [128, 2 * (TC + 2)], BF16))
    tb0 = _p.enter_context(nc.sbuf_tensor(tg + "t0", [128, TC], F32))
    tb1 = _p.enter_context(nc.sbuf_tensor(tg + "t1", [128, TC], F32))
    csb = _p.enter_context(nc.sbuf_tensor(tg + "cs", [128, 2 * TC], BF16))
    am = _p.enter_context(nc.sbuf_tensor(tg + "am", [128, TC], F32))
    csq = _p.enter_context(nc.sbuf_tensor(tg + "cq", [128, TC], BF16))

    V0, T0_, S0 = c0.get("v"), c0.get("t"), c0.get("s")
    ST0_, SG0_ = c0.get("st"), c0.get("sg")

    def p_loads0(sync):
        sync.wait_ge(sems["v"], V0)
        sync.wait_ge(sems["t"], T0_)
        sync.wait_ge(sems["st"], ST0_)
        sync.wait_ge(sems["sg"], SG0_)
        for hc in range(32):
            sync.dma_start(out=cwcol[:, 3 * hc:3 * (hc + 1)],
                           in_=I.cw[hc * 128:(hc + 1) * 128, :]).then_inc(sems["dh"], 16)
            sync.dma_start(out=cbcol[:, hc:hc + 1],
                           in_=I.cb[hc * 128:(hc + 1) * 128].rearrange("(p o) -> p o", o=1)
                           ).then_inc(sems["dh"], 16)
            c0.inc("dh", 32)
    block.sync(p_loads0)
    DH0 = c0.get("dh")

    conv_done = {}
    silu_done = {}
    quant_done = {}
    pr.run("v", lambda v: v.memset(am[:, :], 0.0))
    if half == 0:
        pr.run("v", lambda v: v.memset(heb[:, 0:1], 0.0))
        pr.run("v", lambda v: v.memset(heb[:, (TC + 2):(TC + 3)], 0.0))
    else:
        pr.run("v", lambda v: v.memset(heb[:, TC + 1:TC + 2], 0.0))
        pr.run("v", lambda v: v.memset(heb[:, 2 * TC + 3:2 * TC + 4], 0.0))
    V_EDGE = c0.get("v")
    pr.wait("v", "dh", DH0)
    pr.wait("s", "dh", DH0)
    HB0 = {0: c0.get("h0"), 1: c0.get("h1")}
    CB0 = {0: c0.get("c0"), 1: c0.get("c1")}

    def p_hloads(sync):
        sync.wait_ge(sems["v"], V_EDGE)
        for hc in range(32):
            par = hc % 2
            if hc >= 2:
                sync.wait_ge(sems["v"], conv_done[hc - 2])
            base = par * (TC + 2)
            sync.dma_start(out=heb[:, base + 1:base + 1 + TC],
                           in_=S.hD[hc * 128:(hc + 1) * 128, t0:t0 + TC]
                           ).then_inc(sems["h0"] if par == 0 else sems["h1"], 16)
            c0.inc("h%d" % par, 16)
            if half == 0:
                sync.dma_start(out=heb[:, base + TC + 1:base + TC + 2],
                               in_=S.hD[hc * 128:(hc + 1) * 128, TC:TC + 1],
                               allow_slow_non_contiguous=True
                               ).then_inc(sems["h0"] if par == 0 else sems["h1"], 16)
            else:
                sync.dma_start(out=heb[:, base:base + 1],
                               in_=S.hD[hc * 128:(hc + 1) * 128, TC - 1:TC],
                               allow_slow_non_contiguous=True
                               ).then_inc(sems["h0"] if par == 0 else sems["h1"], 16)
            c0.inc("h%d" % par, 16)
            sync.wait_ge(sems["s"], silu_done[hc])
            sync.dma_start(out=S.csx[hc * 128:(hc + 1) * 128, t0:t0 + TC],
                           in_=csb[:, par * TC:(par + 1) * TC]
                           ).then_inc(sems["c0"] if par == 0 else sems["c1"], 16)
            c0.inc("c%d" % par, 16)

    SP0 = c0.get("sp")
    for hc in range(32):
        par = hc % 2
        he = heb[:, par * (TC + 2):(par + 1) * (TC + 2)]
        pr.wait("v", "h%d" % par, HB0[par] + (hc // 2 + 1) * 32)
        pr.run("v", (lambda he=he, hc=hc: lambda v: v.tensor_scalar(
            out=tb0[:, :], in0=he[:, 0:TC], scalar1=cwcol[:, 3 * hc:3 * hc + 1],
            scalar2=cbcol[:, hc:hc + 1], op0=AluOpType.mult, op1=AluOpType.add))())
        pr.run("v", (lambda he=he, hc=hc: lambda v: v.scalar_tensor_tensor(
            out=tb1[:, :], in0=he[:, 1:TC + 1], scalar=cwcol[:, 3 * hc + 1:3 * hc + 2],
            in1=tb0[:, :], op0=AluOpType.mult, op1=AluOpType.add))())
        vd = pr.run("v", (lambda he=he, hc=hc: lambda v: v.scalar_tensor_tensor(
            out=tb0[:, :], in0=he[:, 2:TC + 2], scalar=cwcol[:, 3 * hc + 2:3 * hc + 3],
            in1=tb1[:, :], op0=AluOpType.mult, op1=AluOpType.add))())
        conv_done[hc] = vd
        pr.wait("s", "v", vd)
        if hc >= 2:
            pr.wait("s", "c%d" % par, CB0[par] + (hc // 2) * 16)
        sd = pr.run("s", (lambda par=par: lambda s: s.activation(
            csb[:, par * TC:(par + 1) * TC], tb0[:, :], AF.Silu, scale=1.0))())
        silu_done[hc] = sd
        sq = pr.run("s", (lambda par=par: lambda s: s.activation(
            csq[:, :], csb[:, par * TC:(par + 1) * TC], AF.Square, scale=1.0))())
        pr.wait("v", "s", sq)
        pr.run("v", lambda v: v.tensor_tensor(out=am[:, :], in0=am[:, :],
                                              in1=csq[:, :], op=AluOpType.max))
    block.sync(p_hloads)

    _q = ExitStack()
    scb = _q.enter_context(nc.sbuf_tensor(tg + "sb", [128, TC + 16], BF16))
    psT = _q.enter_context(nc.psum_tensor(tg + "pT", [128, 1024], BF16))
    psTR = _q.enter_context(nc.psum_tensor(tg + "pR", [128, 1024], BF16))
    emit_absmax_row(pr, lambda ch: am[:, :], 1, tb0, scb, psT, psTR, mrow, K.i128)
    pr.run("v", lambda v: v.tensor_scalar(out=mrow[:, :], in0=mrow[:, :],
                                          scalar1=1e-5, scalar2=None,
                                          op0=AluOpType.max))
    pr.run("v", lambda v: v.reciprocal(out=bcX[0:1, 0:TC] if False else tb0[0:1, 0:TC],
                                       in_=mrow[:, :]))
    pr.wait("t", "v", c0.get("v"))
    emit_row_bcast(pr, lambda tc_i: tb0[0:1, tc_i * 512:(tc_i + 1) * 512],
                   bcX, K.c127, psC)

    pr.barrier()
    pr.wait("v", "c0", c0.get("c0"))
    pr.wait("v", "c1", c0.get("c1"))
    pr.wait("t", "c0", c0.get("c0"))
    pr.wait("t", "c1", c0.get("c1"))
    _q.close()
    _p.close()



def emit_bitlin2_ln_half(P, half):
    nc, pr, c0, sems, block = P["nc"], P["pr"], P["c0"], P["sems"], P["block"]
    K, I, S = P["K"], P["I"], P["S"]
    psA, psB, psC, psD, psE, psF = P["PS"]
    w2q = P["w2q"]
    msc2 = P["msc2"]
    mrow = P[f"mrow_cv{half}"]
    t0 = half * TC
    tg = f"b2{half}"
    _cm_xb = nc.sbuf_tensor(tg + "xb", [128, 8 * TC], F32); xb = _cm_xb.__enter__()
    _p = ExitStack()
    rrow = _p.enter_context(nc.sbuf_tensor(tg + "rr", [1, TC], F32))
    bcR = _p.enter_context(nc.sbuf_tensor(tg + "bR", [128, TC], F32))
    xqt = _p.enter_context(nc.sbuf_tensor(tg + "xq", [128, 32 * 512], BF16))
    sevac = _p.enter_context(nc.sbuf_tensor(tg + "se", [128, 512], F32))
    tq = _p.enter_context(nc.sbuf_tensor(tg + "tq", [128, 512], F32))
    bcX = P[f"bcX_cv{half}"]

    pr.run("v", lambda v: v.tensor_scalar(out=rrow[:, :], in0=mrow[:, :],
                                          scalar1=msc2[:, 1:2], scalar2=1.0 / 127.0,
                                          op0=AluOpType.mult, op1=AluOpType.mult))
    pr.wait("t", "v", c0.get("v"))
    emit_row_bcast(pr, lambda tc_i: rrow[:, tc_i * 512:(tc_i + 1) * 512],
                   bcR, K.oc1f, psC)

    V0, T0_ = c0.get("v"), c0.get("t")
    ST0_, SG0_ = c0.get("st"), c0.get("sg")
    C0B = {0: c0.get("c0"), 1: c0.get("c1")}
    DH0 = c0.get("dh")

    def p_xloads(sync):
        sync.wait_ge(sems["v"], V0)
        sync.wait_ge(sems["t"], T0_)
        sync.wait_ge(sems["st"], ST0_)
        sync.wait_ge(sems["sg"], SG0_)
        sync.wait_ge(sems["c0"], C0B[0])
        sync.wait_ge(sems["c1"], C0B[1])
        for cc in range(8):
            sync.dma_start(out=xb[:, cc * TC:(cc + 1) * TC],
                           in_=S.x1park[cc * 128:(cc + 1) * 128, t0:t0 + TC]
                           ).then_inc(sems["dh"], 16)
            c0.inc("dh", 16)
    block.sync(p_xloads)
    DH1 = c0.get("dh")

    mm_done_tc = {}
    HB = {0: c0.get("h0"), 1: c0.get("h1")}

    def p_xqloads(sync):
        for tc_i in range(4):
            par = tc_i % 2
            if tc_i >= 1:
                sync.wait_ge(sems["t"], mm_done_tc[tc_i - 1])
            for hc in range(32):
                sync.dma_start(
                    out=xqt[:, hc * 512:(hc + 1) * 512],
                    in_=S.csx[hc * 128:(hc + 1) * 128, t0 + tc_i * 512:t0 + (tc_i + 1) * 512]
                ).then_inc(sems["h0"] if par == 0 else sems["h1"], 16)
                c0.inc("h%d" % par, 16)

    rbuf = [xb[:, oc * TC:(oc + 1) * TC] for oc in range(8)]
    evac_marks = []
    quant_tc = {}
    psRot = [psA, psB, psC, psD]

    def emit_quant_tc(tc_i):
        par = tc_i % 2
        pr.wait("v", "h%d" % par, HB[par] + (tc_i // 2 + 1) * 32 * 16)
        for hc in range(32):
            sl = xqt[:, hc * 512:(hc + 1) * 512]
            pr.run("v", (lambda sl=sl, tc_i=tc_i: lambda v: v.tensor_tensor(
                out=tq[:, :], in0=sl,
                in1=bcX[:, tc_i * 512:(tc_i + 1) * 512], op=AluOpType.mult))())
            pr.run("v", (lambda sl=sl: lambda v: v.tensor_scalar(
                out=sl, in0=tq[:, :], scalar1=MAGIC, scalar2=MAGIC,
                op0=AluOpType.add, op1=AluOpType.subtract))())
        quant_tc[tc_i] = c0.get("v")

    def emit_mm_tc(tc_i):
        par = tc_i % 2
        pr.wait("t", "v", quant_tc[tc_i])
        pr.wait("t", "dh", DH1)
        for oc in range(8):
            gi = tc_i * 8 + oc
            ps = psRot[gi % 4]
            if gi >= 4:
                pr.wait("t", "v", evac_marks[gi - 4])
            def mkmm(oc=oc, ps=ps, par=par):
                def f(tensor):
                    mm = None
                    for hc in range(32):
                        mm = tensor.matmul(
                            ps[:, :],
                            w2q[:, hc * 1024 + oc * 128: hc * 1024 + (oc + 1) * 128],
                            xqt[:, hc * 512:(hc + 1) * 512],
                            start=(hc == 0), stop=(hc == 31))
                    return mm
                return f
            td = pr.run("t", mkmm())
            pr.wait("v", "t", td)
            pr.run("v", (lambda ps=ps, tc_i=tc_i: lambda v: v.tensor_tensor(
                out=sevac[:, :], in0=ps[:, :], in1=bcR[:, tc_i * 512:(tc_i + 1) * 512],
                op=AluOpType.mult))())
            vd = pr.run("v", (lambda oc=oc, tc_i=tc_i: lambda v: v.tensor_tensor(
                out=xb[:, oc * TC + tc_i * 512: oc * TC + (tc_i + 1) * 512],
                in0=sevac[:, :],
                in1=xb[:, oc * TC + tc_i * 512: oc * TC + (tc_i + 1) * 512],
                op=AluOpType.add))())
            evac_marks.append(vd)
        mm_done_tc[tc_i] = c0.get("t")

    for tc_i in range(4):
        emit_quant_tc(tc_i)
        emit_mm_tc(tc_i)
    block.sync(p_xqloads)
    pr.barrier()
    _p.close()

    _cm_scr = nc.sbuf_tensor(tg + "sc", [128, 4 * 512], BF16); scr = _cm_scr.__enter__()
    _cm_rw = nc.sbuf_tensor(tg + "rw", [1, 3 * TC], F32); rows_b = _cm_rw.__enter__()
    rows = [rows_b[:, 0:TC], rows_b[:, TC:2 * TC], rows_b[:, 2 * TC:3 * TC],
            rows_b[:, 2 * TC:3 * TC]]
    _cm_bm = nc.sbuf_tensor(tg + "bm", [128, TC], F32); bcm = _cm_bm.__enter__()
    _cm_bs = nc.sbuf_tensor(tg + "bs", [128, TC], F32); bcs = _cm_bs.__enter__()
    _cm_xo = nc.sbuf_tensor(tg + "xo", [128, TC], F32); xo2 = _cm_xo.__enter__()
    SG0 = c0.get("sg")

    def chunk_prewait(ch):
        if ch >= 1:
            return ("sg", SG0 + 16 * ch)
        return None
    ready = emit_ln_tail(pr, rbuf, scr, rows, bcm, bcs,
                         lambda ch: xo2[:, :],
                         psC, psD, psE, K.ocb, K.oc1f, K.epsb,
                         pre_wait=("t", c0.get("t")),
                         chunk_prewait=chunk_prewait)

    def p_xstores(gpsimd):
        for ch in range(8):
            gpsimd.wait_ge(sems["v"], ready[ch])
            gpsimd.dma_start(out=S.x2park[ch * 128:(ch + 1) * 128, t0:t0 + TC],
                             in_=xo2[:, :]).then_inc(sems["sg"], 16)
            c0.inc("sg", 16)
    block.gpsimd(p_xstores)
    pr.barrier()
    pr.wait("v", "sg", c0.get("sg"))
    pr.wait("t", "sg", c0.get("sg"))
    _cm_xo.__exit__(None, None, None)
    _cm_bs.__exit__(None, None, None)
    _cm_bm.__exit__(None, None, None)
    _cm_rw.__exit__(None, None, None)
    _cm_scr.__exit__(None, None, None)
    _cm_xb.__exit__(None, None, None)
    P[f"mrow_cv{half}_cm"].close()



def emit_l1_qkv_half(P, half):
    nc, pr, c0, sems, block = P["nc"], P["pr"], P["c0"], P["sems"], P["block"]
    K, I, S = P["K"], P["I"], P["S"]
    psA, psB, psC, psD, psE, psF = P["PS"]
    t0 = half * TC
    tg = f"l1q{half}"
    _p = ExitStack()
    xb = _p.enter_context(nc.sbuf_tensor(tg + "xb", [128, 8 * TC], BF16))
    wb = _p.enter_context(nc.sbuf_tensor(tg + "wb", [128, 8 * 3 * D], BF16))
    qs = _p.enter_context(nc.sbuf_tensor(tg + "qs", [128, 8 * TC], BF16))
    ksb = _p.enter_context(nc.sbuf_tensor(tg + "ks", [128, 8 * TC], BF16))
    vs = _p.enter_context(nc.sbuf_tensor(tg + "vs", [128, 16 * D], BF16))
    V0, T0_ = c0.get("v"), c0.get("t")
    ST0_, SG0_ = c0.get("st"), c0.get("sg")

    def p_loads(gpsimd):
        gpsimd.wait_ge(sems["v"], V0)
        gpsimd.wait_ge(sems["t"], T0_)
        gpsimd.wait_ge(sems["st"], ST0_)
        gpsimd.wait_ge(sems["sg"], SG0_)
        for cc in range(8):
            gpsimd.dma_start(out=xb[:, cc * TC:(cc + 1) * TC],
                             in_=S.x2park[cc * 128:(cc + 1) * 128, t0:t0 + TC]
                             ).then_inc(sems["ds"], 16)
            c0.inc("ds", 16)
        for cc in range(8):
            gpsimd.dma_start(out=wb[:, cc * 3 * D:(cc + 1) * 3 * D],
                             in_=I.wqkv[cc * 128:(cc + 1) * 128, :]).then_inc(sems["ds"], 16)
            c0.inc("ds", 16)
    block.gpsimd(p_loads)
    DS0 = c0.get("ds")

    pr.wait("t", "ds", DS0)
    evs = []
    psRot = [psA, psB, psC, psD]
    gi = 0
    for which, outb in ((0, qs), (1, ksb)):
        for oc in range(8):
            for tci in range(4):
                ps = psRot[gi % 4]
                if gi >= 4:
                    pr.wait("t", "v", evs[gi - 4])
                def mk(which=which, oc=oc, tci=tci, ps=ps):
                    def f(tensor):
                        mm = None
                        for ic in range(8):
                            off = ic * 3 * D + which * D + oc * 128
                            mm = tensor.matmul(
                                ps[:, :], wb[:, off:off + 128],
                                xb[:, ic * TC + tci * 512: ic * TC + (tci + 1) * 512],
                                start=(ic == 0), stop=(ic == 7))
                        return mm
                    return f
                td = pr.run("t", mk())
                pr.wait("v", "t", td)
                pr.run("v", (lambda outb=outb, oc=oc, tci=tci, ps=ps: lambda v:
                             v.tensor_copy(
                                 out=outb[:, oc * TC + tci * 512: oc * TC + (tci + 1) * 512],
                                 in_=ps[:, :]))())
                evs.append(c0.get("v"))
                gi += 1
    for tt in range(16):
        for vh in range(2):
            ps = psRot[gi % 4]
            if gi >= 4:
                pr.wait("t", "v", evs[gi - 4])
            def mkv(tt=tt, vh=vh, ps=ps):
                def f(tensor):
                    mm = None
                    for ic in range(8):
                        off = ic * 3 * D + 2 * D + vh * 512
                        mm = tensor.matmul(
                            ps[:, :],
                            xb[:, ic * TC + tt * 128: ic * TC + (tt + 1) * 128],
                            wb[:, off:off + 512],
                            start=(ic == 0), stop=(ic == 7))
                    return mm
                return f
            td = pr.run("t", mkv())
            pr.wait("v", "t", td)
            pr.run("v", (lambda tt=tt, vh=vh, ps=ps: lambda v: v.tensor_copy(
                out=vs[:, tt * D + vh * 512: tt * D + (vh + 1) * 512],
                in_=ps[:, :]))())
            evs.append(c0.get("v"))
            gi += 1
    V_ALL = c0.get("v")

    def p_stores(sync):
        sync.wait_ge(sems["v"], V_ALL)
        for cc in range(8):
            sync.dma_start(out=S.qd1[cc * 128:(cc + 1) * 128, t0:t0 + TC],
                           in_=qs[:, cc * TC:(cc + 1) * TC]).then_inc(sems["st"], 16)
            sync.dma_start(out=S.kd1[cc * 128:(cc + 1) * 128, t0:t0 + TC],
                           in_=ksb[:, cc * TC:(cc + 1) * TC]).then_inc(sems["st"], 16)
            c0.inc("st", 32)
        for tt in range(16):
            sync.dma_start(out=S.vd1[t0 + tt * 128:t0 + (tt + 1) * 128, :],
                           in_=vs[:, tt * D:(tt + 1) * D]).then_inc(sems["st"], 16)
            c0.inc("st", 16)
    block.sync(p_stores)
    pr.barrier()
    pr.wait("v", "st", c0.get("st"))
    pr.wait("t", "st", c0.get("st"))
    _p.close()



def emit_l1_ffn_half(P, half):
    nc, pr, c0, sems, block = P["nc"], P["pr"], P["c0"], P["sems"], P["block"]
    K, I, S = P["K"], P["I"], P["S"]
    psA, psB, psE_, psH_, psC_, psSum_ = P["PS"][0], P["PS"][1], P["PS"][2], P["PS"][3], P["PS"][4], P["PS"][5]
    psE = P["PS"][2]
    psH = P["PS"][3]
    psC = P["PS"][4]
    psSum = P["PS"][4]
    psSq = P["PS"][5]
    H, M, NE, EH = 2048, 64, 8, 1024
    t0 = half * TC
    tg = f"lf{half}"
    _p = ExitStack()
    bencc = _p.enter_context(nc.sbuf_tensor(tg + "be", [128, 16], F32))
    bs2cc = _p.enter_context(nc.sbuf_tensor(tg + "b2", [64, 1], F32))
    bc2sc = _p.enter_context(nc.sbuf_tensor(tg + "bc", [128, 16], F32))
    lgb = _p.enter_context(nc.sbuf_tensor(tg + "lg", [128, 16], F32))
    bdrow = _p.enter_context(nc.sbuf_tensor(tg + "bd", [1, D], BF16))
    wrb = _p.enter_context(nc.sbuf_tensor(tg + "wr", [64, NE], BF16))
    brr = _p.enter_context(nc.sbuf_tensor(tg + "br", [1, NE], BF16))
    rtc = _p.enter_context(nc.sbuf_tensor(tg + "rt", [128, 8], F32))
    grt = _p.enter_context(nc.sbuf_tensor(tg + "gr8", [128, 16], F32))
    gbt = _p.enter_context(nc.sbuf_tensor(tg + "gb", [128, 8], BF16))
    gtmp = _p.enter_context(nc.sbuf_tensor(tg + "gt", [8, 128], BF16))
    ws2cb = _p.enter_context(nc.sbuf_tensor(tg + "w2c", [128, 16 * 64], BF16))
    wc2sb = _p.enter_context(nc.sbuf_tensor(tg + "wcs", [64, H], BF16))
    wdecb = _p.enter_context(nc.sbuf_tensor(tg + "wd", [128, 16 * 1024], BF16))
    eo = _p.enter_context(nc.sbuf_tensor(tg + "eo", [64, TC], BF16))
    psG = _p.enter_context(nc.psum_tensor(tg + "pG", [128, 1024], BF16))
    psY = _p.enter_context(nc.psum_tensor(tg + "pY", [128, 512], F32))

    _cm_contb = nc.sbuf_tensor(tg + "ct", [64, TC], BF16); contb = _cm_contb.__enter__()
    _cm_gr = nc.sbuf_tensor(tg + "gr", [1, NE * TC], BF16); gr = _cm_gr.__enter__()
    _cm_xbb = nc.sbuf_tensor(tg + "xbb", [128, 8 * TC], BF16); xbb = _cm_xbb.__enter__()
    _cm_wen = nc.sbuf_tensor(tg + "we", [128, 8 * H], BF16); wencb = _cm_wen.__enter__()
    _cm_spk = nc.sbuf_tensor(tg + "sp", [128, 2 * 512], BF16); spkb = _cm_spk.__enter__()
    _cm_w1b = nc.sbuf_tensor(tg + "w1b", [64, NE * EH], BF16); w1b = _cm_w1b.__enter__()
    _cm_b1r = nc.sbuf_tensor(tg + "b1r", [1, NE * EH], BF16); b1r = _cm_b1r.__enter__()
    _cm_w2b = nc.sbuf_tensor(tg + "w2b", [128, NE * 8 * M], BF16); w2b = _cm_w2b.__enter__()
    _cm_b2r = nc.sbuf_tensor(tg + "b2r", [1, NE * M], BF16); b2r = _cm_b2r.__enter__()
    _cm_cg = nc.sbuf_tensor(tg + "cg", [64, 512], BF16); cg = _cm_cg.__enter__()
    _cm_hsb = nc.sbuf_tensor(tg + "hs", [128, 2 * 512], BF16); hsb = _cm_hsb.__enter__()

    V0, T0_ = c0.get("v"), c0.get("t")
    ST0_, SG0_ = c0.get("st"), c0.get("sg")

    def p_loads(sync):
        sync.wait_ge(sems["v"], V0)
        sync.wait_ge(sems["t"], T0_)
        sync.wait_ge(sems["st"], ST0_)
        sync.wait_ge(sems["sg"], SG0_)
        for hc in range(16):
            sync.dma_start(out=bencc[:, hc:hc + 1],
                           in_=I.benc[hc * 128:(hc + 1) * 128].rearrange("(p o) -> p o", o=1)
                           ).then_inc(sems["dh"], 16)
            sync.dma_start(out=bc2sc[:, hc:hc + 1],
                           in_=I.bc2s[hc * 128:(hc + 1) * 128].rearrange("(p o) -> p o", o=1)
                           ).then_inc(sems["dh"], 16)
            c0.inc("dh", 32)
        sync.dma_start(out=bs2cc[:, :],
                       in_=I.bs2c.rearrange("(p o) -> p o", o=1)).then_inc(sems["dh"], 16)
        c0.inc("dh", 16)
        for cc in range(8):
            sync.dma_start(out=lgb[:, cc:cc + 1],
                           in_=I.lng[cc * 128:(cc + 1) * 128].rearrange("(p o) -> p o", o=1)
                           ).then_inc(sems["dh"], 16)
            sync.dma_start(out=lgb[:, 8 + cc:9 + cc],
                           in_=I.lnb[cc * 128:(cc + 1) * 128].rearrange("(p o) -> p o", o=1)
                           ).then_inc(sems["dh"], 16)
            c0.inc("dh", 32)
    block.sync(p_loads)

    def p_gploads(gpsimd):
        gpsimd.wait_ge(sems["v"], V0)
        gpsimd.wait_ge(sems["t"], T0_)
        gpsimd.wait_ge(sems["st"], ST0_)
        gpsimd.wait_ge(sems["sg"], SG0_)
        for cc in range(8):
            gpsimd.dma_start(out=xbb[:, cc * TC:(cc + 1) * TC],
                             in_=S.x3park[cc * 128:(cc + 1) * 128, t0:t0 + TC]
                             ).then_inc(sems["ds"], 16)
            gpsimd.dma_start(out=wencb[:, cc * H:(cc + 1) * H],
                             in_=I.wenc[cc * 128:(cc + 1) * 128, :]).then_inc(sems["ds"], 16)
            c0.inc("ds", 32)
        for hc in range(16):
            gpsimd.dma_start(out=ws2cb[:, hc * 64:(hc + 1) * 64],
                             in_=I.ws2c[hc * 128:(hc + 1) * 128, :]).then_inc(sems["ds"], 16)
            gpsimd.dma_start(out=wdecb[:, hc * 1024:(hc + 1) * 1024],
                             in_=I.wdec[hc * 128:(hc + 1) * 128, :]).then_inc(sems["ds"], 16)
            c0.inc("ds", 32)
        gpsimd.dma_start(out=wc2sb[:, :], in_=I.wc2s[:, :]).then_inc(sems["ds"], 16)
        gpsimd.dma_start(out=wrb[:, :], in_=I.wr[:, :]).then_inc(sems["ds"], 16)
        gpsimd.dma_start(out=brr[:, :], in_=I.br.rearrange("(o e) -> o e", o=1)
                         ).then_inc(sems["ds"], 16)
        gpsimd.dma_start(out=bdrow[:, :], in_=I.bdec.rearrange("(o d) -> o d", o=1)
                         ).then_inc(sems["ds"], 16)
        c0.inc("ds", 64)
        for e in range(NE):
            gpsimd.dma_start(out=w1b[:, e * EH:(e + 1) * EH],
                             in_=I.w1e[e, :, :]).then_inc(sems["ds"], 16)
            gpsimd.dma_start(out=b1r[:, e * EH:(e + 1) * EH],
                             in_=I.b1e[e, :].rearrange("(o h) -> o h", o=1)
                             ).then_inc(sems["ds"], 16)
            gpsimd.dma_start(out=b2r[:, e * M:(e + 1) * M],
                             in_=I.b2e[e, :].rearrange("(o m) -> o m", o=1)
                             ).then_inc(sems["ds"], 16)
            c0.inc("ds", 48)
            for hc8 in range(8):
                gpsimd.dma_start(out=w2b[:, (e * 8 + hc8) * M:(e * 8 + hc8 + 1) * M],
                                 in_=I.w2e[e, hc8 * 128:(hc8 + 1) * 128, :]
                                 ).then_inc(sems["ds"], 16)
                c0.inc("ds", 16)
    block.gpsimd(p_gploads)
    DH0 = c0.get("dh")
    DS0 = c0.get("ds")
    for eng in ("t", "s", "v"):
        pr.wait(eng, "dh", DH0)
        pr.wait(eng, "ds", DS0)
    pr.run("v", lambda v: v.memset(K.epsb[:, :], EPS))

    psCt = [psE, psH, psSum, psSq]
    psSpk = [psA, psB, psY]
    sig_done = {}
    cont_mm_done = {}
    spk_gi = 0
    pend_cont = [None]

    def flush_cont():
        if pend_cont[0] is None:
            return
        g, hc_, tci_, par_ = pend_cont[0]
        pend_cont[0] = None
        pr.wait("t", "s", sig_done[g])
        def mkcont(hc=hc_, tci=tci_, par=par_):
            def f(tensor):
                return tensor.matmul(
                    psCt[tci][0:64, :], ws2cb[:, hc * 64:(hc + 1) * 64],
                    spkb[:, par * 512:(par + 1) * 512],
                    start=(hc == 0), stop=(hc == 15))
            return f
        cont_mm_done[(hc_, tci_)] = pr.run("t", mkcont())

    for hc in range(16):
        for tci in range(4):
            bank = psSpk[spk_gi % 3]
            if spk_gi >= 3:
                pr.wait("t", "s", sig_done[spk_gi - 3])
            def mkspk(hc=hc, tci=tci, bank=bank):
                def f(tensor):
                    mm = None
                    for ic in range(8):
                        mm = tensor.matmul(
                            bank[:, :],
                            wencb[:, ic * H + hc * 128: ic * H + (hc + 1) * 128],
                            xbb[:, ic * TC + tci * 512: ic * TC + (tci + 1) * 512],
                            start=(ic == 0), stop=(ic == 7))
                    return mm
                return f
            td = pr.run("t", mkspk())
            pr.wait("s", "t", td)
            sd = pr.run("s", (lambda hc=hc, bank=bank, par=spk_gi % 2: lambda s:
                              s.activation(spkb[:, par * 512:(par + 1) * 512],
                                           bank[:, :], AF.Sigmoid,
                                           bias=bencc[:, hc:hc + 1], scale=1.0))())
            sig_done[spk_gi] = sd
            flush_cont()
            pend_cont[0] = (spk_gi, hc, tci, spk_gi % 2)
            spk_gi += 1
    flush_cont()
    for tci in range(4):
        pr.wait("s", "t", cont_mm_done[(15, tci)])
        pr.run("s", (lambda tci=tci: lambda s: s.activation(
            contb[:, tci * 512:(tci + 1) * 512], psCt[tci][0:64, :], AF.Identity,
            bias=bs2cc[:, :], scale=1.0))())
    S_CONT = c0.get("s")

    pr.wait("t", "s", S_CONT)
    H0B = c0.get("h0")
    gstore_marks = []
    for tt in range(16):
        def mklog(tt=tt):
            def f(tensor):
                tensor.matmul(psC[0:128, 0:NE], contb[:, tt * 128:(tt + 1) * 128],
                              wrb[:, :], start=True, stop=False)
                return tensor.matmul(psC[0:128, 0:NE], K.or512[:, 0:128], brr[:, :],
                                     start=False, stop=True)
            return f
        td = pr.run("t", mklog())
        pr.wait("v", "t", td)
        pr.run("v", lambda v: v.tensor_reduce(out=rtc[:, 0:1], in_=psC[0:128, 0:NE],
                                              axis=mybir.AxisListType.X, op=AluOpType.max))
        pr.run("v", lambda v: v.tensor_scalar(out=rtc[:, 1:2], in0=rtc[:, 0:1],
                                              scalar1=-1.0, scalar2=None,
                                              op0=AluOpType.mult))
        V_NM = c0.get("v")
        pr.wait("s", "v", V_NM)
        sd = pr.run("s", lambda s: s.activation(grt[:, 0:8], psC[0:128, 0:NE], AF.Exp,
                                                bias=rtc[:, 1:2], scale=1.0))
        pr.wait("v", "s", sd)
        pr.run("v", lambda v: v.tensor_reduce(out=rtc[:, 2:3], in_=grt[:, 0:8],
                                              axis=mybir.AxisListType.X, op=AluOpType.add))
        pr.run("v", lambda v: v.reciprocal(out=rtc[:, 3:4], in_=rtc[:, 2:3]))
        pr.run("v", lambda v: v.tensor_scalar(out=grt[:, 0:8], in0=grt[:, 0:8],
                                              scalar1=rtc[:, 3:4], scalar2=None,
                                              op0=AluOpType.mult))
        pr.run("v", lambda v: v.tensor_reduce(out=rtc[:, 4:5], in_=grt[:, 0:8],
                                              axis=mybir.AxisListType.X, op=AluOpType.max))
        pr.run("v", lambda v: v.tensor_scalar(out=grt[:, 8:16], in0=grt[:, 0:8],
                                              scalar1=rtc[:, 4:5], scalar2=None,
                                              op0=AluOpType.is_ge))
        pr.run("v", lambda v: v.scalar_tensor_tensor(out=grt[:, 8:16], in0=grt[:, 8:16],
                                                     scalar=-1e9, in1=grt[:, 0:8],
                                                     op0=AluOpType.mult, op1=AluOpType.add))
        pr.run("v", lambda v: v.tensor_reduce(out=rtc[:, 5:6], in_=grt[:, 8:16],
                                              axis=mybir.AxisListType.X, op=AluOpType.max))
        pr.run("v", lambda v: v.tensor_tensor(out=rtc[:, 6:7], in0=rtc[:, 4:5],
                                              in1=rtc[:, 5:6], op=AluOpType.add))
        pr.run("v", lambda v: v.reciprocal(out=rtc[:, 7:8], in_=rtc[:, 6:7]))
        pr.run("v", lambda v: v.tensor_scalar(out=grt[:, 8:16], in0=grt[:, 0:8],
                                              scalar1=rtc[:, 5:6], scalar2=None,
                                              op0=AluOpType.is_ge))
        pr.run("v", lambda v: v.tensor_scalar(out=grt[:, 0:8], in0=grt[:, 0:8],
                                              scalar1=rtc[:, 7:8], scalar2=None,
                                              op0=AluOpType.mult))
        vg = pr.run("v", lambda v: v.tensor_tensor(out=gbt[:, 0:8], in0=grt[:, 0:8],
                                                   in1=grt[:, 8:16], op=AluOpType.mult))
        pr.wait("t", "v", vg)
        tg_ = pr.run("t", lambda t: t.transpose(psG[0:8, 0:128], gbt[:, 0:8],
                                                K.i128[:, :]))
        pr.wait("v", "t", tg_)
        if tt >= 1:
            pr.wait("v", "h0", H0B + 16 * tt)
        pr.run("v", lambda v: v.tensor_copy(out=gtmp[:, :], in_=psG[0:8, 0:128]))
        gstore_marks.append(c0.get("v"))

    def sync_gstore(sync):
        for tt in range(16):
            sync.wait_ge(sems["v"], gstore_marks[tt])
            sync.dma_start(out=S.gdram[:, tt * 128:(tt + 1) * 128],
                           in_=gtmp[:, :]).then_inc(sems["h0"], 16)
            c0.inc("h0", 16)
        sync.wait_ge(sems["h0"], H0B + 256)
        sync.dma_start(out=gr[:, :],
                       in_=S.gdram.rearrange("e t -> (e t)").rearrange("(o n) -> o n", o=1)
                       ).then_inc(sems["h0"], 16)
        c0.inc("h0", 16)
    block.sync(sync_gstore)
    H0_GR = c0.get("h0")

    pr.wait("t", "h0", H0_GR)
    cg_done = {}
    eo_done = {}
    relu_done = {}
    psHrot = [psH, psY]
    ehg = 0
    pend_w2 = [None]

    def flush_w2():
        if pend_w2[0] is None:
            return
        g, e_, eh_, par_, start_ = pend_w2[0]
        pend_w2[0] = None
        pr.wait("t", "s", relu_done[g])
        def mkw2(e=e_, eh=eh_, par=par_, start=start_):
            def f(tensor):
                return tensor.matmul(
                    psE[0:64, :], w2b[:, (e * 8 + eh) * M:(e * 8 + eh + 1) * M],
                    hsb[:, par * 512:(par + 1) * 512],
                    start=start, stop=False)
            return f
        pr.run("t", mkw2())

    def emit_b2(e_, tci_, stop_):
        def f(tensor):
            return tensor.matmul(psE[0:64, :], b2r[:, e_ * M:(e_ + 1) * M],
                                 gr[:, e_ * TC + tci_ * 512: e_ * TC + (tci_ + 1) * 512],
                                 start=False, stop=stop_)
        pr.run("t", f)

    for tci in range(4):
        for e in range(NE):
            if (tci, e) != (0, 0):
                prev = cg_done.get((tci, e - 1)) if e > 0 else cg_done.get((tci - 1, NE - 1))
                pr.wait("t", "v", prev or 0)
            def mkgb(e=e, tci=tci):
                def f(tensor):
                    return tensor.matmul(psC[0:64, :], K.oc64[:, :],
                                         gr[:, e * TC + tci * 512: e * TC + (tci + 1) * 512],
                                         start=True, stop=True)
                return f
            tb = pr.run("t", mkgb())
            pr.wait("v", "t", tb)
            vd = pr.run("v", (lambda tci=tci: lambda v: v.tensor_tensor(
                out=cg[:, :], in0=psC[0:64, :], in1=contb[:, tci * 512:(tci + 1) * 512],
                op=AluOpType.mult))())
            cg_done[(tci, e)] = vd
            pr.wait("t", "v", vd)
            for eh in range(8):
                par = ehg % 2
                bank = psHrot[par]
                if ehg >= 2:
                    pr.wait("t", "s", relu_done[ehg - 2])
                def mkh(e=e, eh=eh, tci=tci, bank=bank):
                    def f(tensor):
                        tensor.matmul(bank[:, :],
                                      w1b[:, e * EH + eh * 128: e * EH + (eh + 1) * 128],
                                      cg[:, :], start=True, stop=False)
                        return tensor.matmul(
                            bank[:, :], b1r[:, e * EH + eh * 128: e * EH + (eh + 1) * 128],
                            gr[:, e * TC + tci * 512: e * TC + (tci + 1) * 512],
                            start=False, stop=True)
                    return f
                th = pr.run("t", mkh())
                pr.wait("s", "t", th)
                sd = pr.run("s", (lambda par=par, bank=bank: lambda s: s.activation(
                    hsb[:, par * 512:(par + 1) * 512], bank[:, :], AF.Relu,
                    scale=1.0))())
                relu_done[ehg] = sd
                flush_w2()
                pend_w2[0] = (ehg, e, eh, par, e == 0 and eh == 0)
                ehg += 1
            flush_w2()
            emit_b2(e, tci, e == NE - 1)
        TEOD = c0.get("t")
        pr.wait("v", "t", TEOD)
        ed = pr.run("v", (lambda tci=tci: lambda v: v.tensor_copy(
            out=eo[:, tci * 512:(tci + 1) * 512], in_=psE[0:64, :]))())
        eo_done[tci] = ed
    V_EO = c0.get("v")

    _cm_hsb.__exit__(None, None, None)
    _cm_cg.__exit__(None, None, None)
    _cm_b2r.__exit__(None, None, None)
    _cm_w2b.__exit__(None, None, None)
    _cm_b1r.__exit__(None, None, None)
    _cm_w1b.__exit__(None, None, None)
    _cm_spk.__exit__(None, None, None)
    _cm_wen.__exit__(None, None, None)
    _cm_xbb.__exit__(None, None, None)
    _cm_gr.__exit__(None, None, None)
    _cm_contb.__exit__(None, None, None)
    _cm_dec = nc.sbuf_tensor(tg + "dec", [128, 8 * TC], BF16); dec = _cm_dec.__enter__()
    _cm_xb = nc.sbuf_tensor(tg + "xb", [128, 8 * TC], F32); xb = _cm_xb.__enter__()
    _cm_smtc = nc.sbuf_tensor(tg + "sm", [128, 16 * 512], BF16); smtc = _cm_smtc.__enter__()

    def p_xloads(sync):
        sync.wait_ge(sems["v"], V_EO)
        for cc in range(8):
            sync.dma_start(out=xb[:, cc * TC:(cc + 1) * TC],
                           in_=S.x3park[cc * 128:(cc + 1) * 128, t0:t0 + TC]
                           ).then_inc(sems["dh"], 16)
            c0.inc("dh", 16)
    block.sync(p_xloads)
    DH1 = c0.get("dh")

    evac_marks = []
    dec_gi = 0
    psSm = [psC, psH, psY]
    for tci in range(4):
        sm_marks = {}
        for hc in range(16):
            bank = psSm[hc % 3]
            if hc >= 3:
                pr.wait("t", "s", sm_marks[hc - 3])
            def mksm(hc=hc, tci=tci, bank=bank):
                def f(tensor):
                    return tensor.matmul(bank[0:128, :], wc2sb[:, hc * 128:(hc + 1) * 128],
                                         eo[:, tci * 512:(tci + 1) * 512],
                                         start=True, stop=True)
                return f
            tsm = pr.run("t", mksm())
            pr.wait("s", "t", tsm)
            if tci > 0:
                pr.wait("s", "t", c0.get("%s_dec_%d" % (tg, tci - 1), 0))
            sd = pr.run("s", (lambda hc=hc, bank=bank: lambda s: s.activation(
                smtc[:, hc * 512:(hc + 1) * 512], bank[0:128, :], AF.Sigmoid,
                bias=bc2sc[:, hc:hc + 1], scale=1.0))())
            sm_marks[hc] = sd
        pr.wait("t", "s", sm_marks[15])
        for oc in range(8):
            bank = psA if dec_gi % 2 == 0 else psB
            if dec_gi >= 2:
                pr.wait("t", "v", evac_marks[dec_gi - 2])
            def mkdec(oc=oc, bank=bank):
                def f(tensor):
                    for hc in range(16):
                        tensor.matmul(bank[:, :],
                                      wdecb[:, hc * 1024 + oc * 128: hc * 1024 + (oc + 1) * 128],
                                      smtc[:, hc * 512:(hc + 1) * 512],
                                      start=(hc == 0), stop=False)
                    return tensor.matmul(bank[:, :], bdrow[:, oc * 128:(oc + 1) * 128],
                                         K.or512[:, :], start=False, stop=True)
                return f
            td = pr.run("t", mkdec())
            pr.wait("v", "t", td)
            vd = pr.run("v", (lambda oc=oc, tci=tci, bank=bank: lambda v: v.tensor_copy(
                out=dec[:, oc * TC + tci * 512: oc * TC + (tci + 1) * 512],
                in_=bank[:, :]))())
            evac_marks.append(vd)
            dec_gi += 1
        c0.v["%s_dec_%d" % (tg, tci)] = c0.get("t")

    _cm_smtc.__exit__(None, None, None)
    _cm_scr = nc.sbuf_tensor(tg + "sc", [128, 4 * 512], BF16); scr = _cm_scr.__enter__()
    _cm_rw = nc.sbuf_tensor(tg + "rw", [1, 4 * TC], F32); rows_b = _cm_rw.__enter__()
    rows = [rows_b[:, k * TC:(k + 1) * TC] for k in range(4)]
    _cm_bm = nc.sbuf_tensor(tg + "bm", [128, TC], F32); bcm = _cm_bm.__enter__()
    _cm_bs = nc.sbuf_tensor(tg + "bs", [128, TC], F32); bcs = _cm_bs.__enter__()
    _cm_xo = nc.sbuf_tensor(tg + "xo", [128, TC], F32); xo2 = _cm_xo.__enter__()

    decbuf = [dec[:, oc * TC:(oc + 1) * TC] for oc in range(8)]
    pr.wait("v", "dh", DH1)
    emit_ln_tail(pr, decbuf, scr, rows, bcm, bcs,
                 lambda ch: xo2[:, :],
                 psSum, psSq, psC, K.ocb, K.oc1f, K.epsb,
                 gcol=lgb[:, 0:8], bcol=lgb[:, 8:16],
                 pre_wait=("t", c0.get("t")),
                 post_add_ap=lambda ch: xb[:, ch * TC:(ch + 1) * TC])
    rbuf2 = [xb[:, oc * TC:(oc + 1) * TC] for oc in range(8)]
    ST0 = c0.get("st")

    def chunk_prewait(ch):
        if ch >= 1:
            return ("st", ST0 + 16 * ch)
        return None
    ready = emit_ln_tail(pr, rbuf2, scr, rows, bcm, bcs,
                         lambda ch: xo2[:, :],
                         psSum, psSq, psC, K.ocb, K.oc1f, K.epsb,
                         chunk_prewait=chunk_prewait)

    def p_outstores(sync):
        for ch in range(8):
            sync.wait_ge(sems["v"], ready[ch])
            sync.dma_start(out=I.xo[ch * 128:(ch + 1) * 128, t0:t0 + TC],
                           in_=xo2[:, :]).then_inc(sems["st"], 16)
            c0.inc("st", 16)
    block.sync(p_outstores)
    pr.barrier()
    pr.wait("v", "st", c0.get("st"))
    pr.wait("t", "st", c0.get("st"))
    _cm_xo.__exit__(None, None, None)
    _cm_bs.__exit__(None, None, None)
    _cm_bm.__exit__(None, None, None)
    _cm_rw.__exit__(None, None, None)
    _cm_scr.__exit__(None, None, None)
    _cm_xb.__exit__(None, None, None)
    _cm_dec.__exit__(None, None, None)
    _p.close()



_PROG = None


def kernel(**inputs):
    global _PROG
    z = inputs
    if _PROG is None:
        _PROG = build_fused()
    nc = _PROG
    consts = host_consts()
    x = np.ascontiguousarray(np.asarray(z["L0_attn_x"], np.float32))
    w1T = np.ascontiguousarray(np.asarray(z["L0_ffn_w1"], np.float32).T)
    w2T = np.ascontiguousarray(np.asarray(z["L0_ffn_w2"], np.float32).T)
    cwf = np.ascontiguousarray(np.asarray(z["L0_ffn_conv_w"], np.float32)[:, 0, :])
    in_maps = []
    for core in range(NCORE):
        b = core // 2
        m = dict(consts)
        m.update({
            "xT": np.ascontiguousarray(x[b].T),
            "wq": z["L0_attn_Wq"], "bq": z["L0_attn_bq"],
            "wk": z["L0_attn_Wk"], "bk": z["L0_attn_bk"],
            "wv": z["L0_attn_Wv"], "bv": z["L0_attn_bv"],
            "wo": z["L0_attn_Wo"], "bo": z["L0_attn_bo"],
            "w1T": w1T, "cw": cwf, "cb": z["L0_ffn_conv_b"], "w2T": w2T,
            "wqkv": z["L1_attn_w_qkv"], "wproj": z["L1_attn_w_proj"],
            "wenc": z["L1_ffn_W_enc"], "benc": z["L1_ffn_b_enc"],
            "ws2c": z["L1_ffn_W_s2c"], "bs2c": z["L1_ffn_b_s2c"],
            "wr": z["L1_ffn_Wr"], "br": z["L1_ffn_br"],
            "w1e": z["L1_ffn_W1"], "b1e": z["L1_ffn_b1"],
            "w2e": z["L1_ffn_W2"], "b2e": z["L1_ffn_b2"],
            "wc2s": z["L1_ffn_W_c2s"], "bc2s": z["L1_ffn_b_c2s"],
            "wdec": z["L1_ffn_W_dec"], "bdec": z["L1_ffn_b_dec"],
            "lng": z["L1_ffn_ln_g"], "lnb": z["L1_ffn_ln_b"],
        })
        in_maps.append(m)
    res = run_bass_kernel_spmd(nc, in_maps, core_ids=list(range(NCORE)))
    out = np.zeros((B, T, D), np.float32)
    for b in range(B):
        out[b] = res.results[2 * b]["xo"].T
    return out
